# version 05
# baseline (speedup 1.0000x reference)
import numpy as np
import contextlib
import secrets
from collections import defaultdict
import ml_dtypes
import concourse.bass as bass
import concourse.mybir as mybir
from concourse.bass_utils import run_bass_kernel_spmd

F32 = mybir.dt.float32
BF16 = mybir.dt.bfloat16
FP8 = mybir.dt.float8e4
I32 = mybir.dt.int32
AF = mybir.ActivationFunctionType
ALU = mybir.AluOpType
AX = mybir.AxisListType.X

NCORE = 8
Bz, T, D = 4, 4096, 2048
FC = 16
TOK = 2048
NH, MYH = 16, 8
DN, DR, DQK, DV = 128, 64, 192, 128
QL, KVL = 1536, 512
HID = 8192
STRIPE = [(0, 3), (1, 2)]
SHIFT = 12.0
SC0 = 1.0 / float(np.sqrt(192.0))
SC1 = 1.0 / float(np.sqrt(128.0))
GC1 = 0.7978845608028654
GC2 = 0.044715
EPSR = 1e-6
EPSL = 1e-5

ENGS = ("tensor", "vector", "scalar", "gpsimd", "sync")
QSEM = {"gpsimd": "DG", "sync": "DS", "scalar": "DA"}


class Prog:

    def __init__(self):
        self.ops = {e: [] for e in ENGS}
        self.tot = defaultdict(int)
        self.semnames = set(["MM", "DV", "AC", "PO", "PL"])

    def emit(self, eng, fn, waits=(), inc=None):
        for s, _ in waits:
            self.semnames.add(s)
        self.ops[eng].append((fn, tuple(waits), inc))
        if inc is not None:
            self.semnames.add(inc[0])
            self.tot[inc[0]] += inc[1]
        return (inc[0], self.tot[inc[0]]) if inc is not None else None

    def val(self, sem):
        return self.tot[sem]

    def barrier_waits(self):
        return tuple((s, self.tot[s]) for s in sorted(self.tot)
                     if self.tot[s] > 0 and s != "PL")

    def barrier(self, name=""):
        w = self.barrier_waits()
        for e in ENGS:
            self.emit(e, lambda en: en.nop(), w)


def _rope_tables():
    inv = 1.0 / (10000.0 ** (np.arange(0, DR, 2, dtype=np.float64) / DR))
    t = np.arange(T, dtype=np.float64)
    fr = np.outer(inv, t)
    return np.cos(fr), np.sin(fr)


def _vec_tile(v, cols):
    return np.ascontiguousarray(v.reshape(cols, 128).T).astype(np.float32)


def host_inputs(inputs):
    x = np.asarray(inputs["L0_attn_hidden_states"], np.float32)
    cos, sin = _rope_tables()
    nonce = secrets.randbits(20) + 1
    f32 = lambda a: np.ascontiguousarray(a, dtype=np.float32)
    com = {
        "wqa": f32(inputs["L0_attn_Wq_a"]),
        "wkva": f32(inputs["L0_attn_Wkv_a"]),
        "wkr": f32(inputs["L0_attn_Wk_rope"]),
        "wo0": f32(inputs["L0_attn_Wo"]),
        "wg0": f32(inputs["L0_ffn_w_gating"][0]),
        "wg1": f32(inputs["L0_ffn_w_gating"][1]),
        "wlin": f32(inputs["L0_ffn_w_linear"]),
        "wo1": f32(inputs["L1_attn_Wo"]),
        "w1t": f32(np.asarray(inputs["L1_ffn_weight"]).T),
        "cosq": f32(cos * SC0), "sinq": f32(sin * SC0),
        "cosk": f32(cos), "sink": f32(sin),
        "qnw": _vec_tile(f32(inputs["L0_attn_q_norm_w"]), 12),
        "kvnw": _vec_tile(f32(inputs["L0_attn_kv_norm_w"]), 4),
        "bo1": _vec_tile(f32(inputs["L1_attn_bo"]), 16),
        "fb1": _vec_tile(f32(inputs["L1_ffn_bias"]), 16),
    }
    wqb = f32(inputs["L0_attn_Wq_b"])
    wkvb = f32(inputs["L0_attn_Wkv_b"])
    wq1 = f32(inputs["L1_attn_Wq"])
    wk1 = f32(inputs["L1_attn_Wk"])
    wv1 = f32(inputs["L1_attn_Wv"])
    bq1 = f32(inputs["L1_attn_bq"])
    bk1 = f32(inputs["L1_attn_bk"])
    bv1 = f32(inputs["L1_attn_bv"])
    maps = []
    for c in range(NCORE):
        b, j = divmod(c, 2)
        sA, sB = STRIPE[j]
        posA, posB = sA * 1024, sB * 1024
        xc = np.concatenate([x[b, posA:posA + 1024], x[b, posB:posB + 1024]], 0)
        hs = slice(j * MYH, j * MYH + MYH)
        m = dict(com)
        m["x_fm"] = np.ascontiguousarray(xc.T)
        m["wqb"] = wqb
        m["wkvb"] = np.ascontiguousarray(
            wkvb.reshape(KVL, NH, DN + DV)[:, hs].reshape(KVL, MYH * (DN + DV)))
        m["tabq"] = np.ascontiguousarray(np.concatenate(
            [com["cosq"][:, posA:posA + 1024], com["cosq"][:, posB:posB + 1024],
             com["sinq"][:, posA:posA + 1024], com["sinq"][:, posB:posB + 1024]],
            axis=1).reshape(32, 2, 2048).transpose(1, 0, 2).reshape(64, 2048))
        m["tabk"] = np.ascontiguousarray(np.concatenate(
            [com["cosk"][:, posA:posA + 1024], com["cosk"][:, posB:posB + 1024],
             com["sink"][:, posA:posA + 1024], com["sink"][:, posB:posB + 1024]],
            axis=1).reshape(32, 2, 2048).transpose(1, 0, 2).reshape(64, 2048))
        m["wq1"] = np.ascontiguousarray(
            wq1.reshape(D, NH, DV)[:, hs].reshape(D, MYH * DV))
        m["wk1"] = np.ascontiguousarray(wk1[:, j * 256:(j + 1) * 256])
        m["wv1"] = np.ascontiguousarray(wv1[:, j * 256:(j + 1) * 256])
        m["bq1"] = _vec_tile(bq1.reshape(NH, DV)[hs].reshape(-1), 8)
        m["bk1"] = _vec_tile(bk1[j * 256:(j + 1) * 256], 2)
        m["bv1"] = np.ascontiguousarray(
            bv1[j * 256:(j + 1) * 256].reshape(1, 256)).astype(np.float32)
        for k in ("cosq", "sinq", "cosk", "sink"):
            m.pop(k)
        meta = np.zeros((1, 16), np.int32)
        meta[0, 0] = posA
        meta[0, 1] = posB
        meta[0, 2] = j * 8
        meta[0, 3] = (1 - j) * 8
        meta[0, 4] = nonce
        meta[0, 5] = j * MYH * SQPs
        meta[0, 6] = j
        m["meta"] = meta
        m["flagv"] = (nonce + np.arange(1, 9, dtype=np.int64)).astype(np.int32).reshape(1, 8)
        maps.append(m)
    return maps



SQPs = 128 * 4096
SQRs = 64 * 4096
PERM = [STRIPE[0][0], STRIPE[0][1], STRIPE[1][0], STRIPE[1][1]]


class Builder:

    def __init__(self, phases, fused):
        self.ph = set(phases)
        self.fused = fused
        self.nc = bass.Bass("TRN2", target_bir_lowering=False)
        self.P = Prog()
        self.es = contextlib.ExitStack()
        self.sb = {}
        self.ds = {}
        self.first = phases[0]
        self.last = phases[-1]
        self.innames = set()
        self.semq = {}

    def din(self, name, shape, dtype=F32):
        self.ds[name] = self.nc.dram_tensor(name, shape, dtype, kind="ExternalInput").ap()
        self.innames.add(name)
        return self.ds[name]

    def dout(self, name, shape, dtype=F32):
        self.ds[name] = self.nc.dram_tensor(name, shape, dtype, kind="ExternalOutput").ap()
        return self.ds[name]

    def dshared(self, name, shape, dtype):
        self.ds[name] = self.nc.dram_tensor(name, shape, dtype, addr_space="Shared").ap()
        return self.ds[name]

    def dlocal(self, name, shape, dtype):
        self.ds[name] = self.nc.dram_tensor(name, shape, dtype).ap()
        return self.ds[name]

    def S(self, name, shape, dtype):
        t = self.es.enter_context(self.nc.sbuf_tensor(name, shape, dtype))
        self.sb[name] = t
        return t

    def mm(self, out_ap, lhsT, rhs, start, stop, w=(), inc=None, skip=False):
        return self.P.emit("tensor",
                           lambda t, o=out_ap, a=lhsT, b=rhs, s1=start, s2=stop, sk=skip:
                           t.matmul(o, a, b, start=s1, stop=s2, skip_group_check=sk),
                           w, inc)

    def dma(self, q, out_ap, in_ap, w=(), sem=None):
        sem = sem or QSEM[q]
        assert self.semq.setdefault(sem, q) == q, f"sem {sem} on two queues"
        return self.P.emit(q, lambda e, o=out_ap, i=in_ap: e.dma_start(out=o, in_=i),
                           w, (sem, 16))

    def dyn_dma(self, dram_ap, mult, const, pattern, sbuf_or_dram, w, sem,
                is_out):
        assert self.semq.setdefault(sem, "sync") == "sync"
        def fn(y, dt=dram_ap, m=mult, co=const, pat=pattern, sap=sbuf_or_dram,
               out=is_out):
            y.reg_mul(REGS2["rT"], REGS2["rJ"], m)
            y.reg_add(REGS2["rT"], REGS2["rT"], co)
            dap = bass.AP(dt.tensor, REGS2["rT"], pat)
            if out:
                return y.dma_start(out=dap, in_=sap)
            return y.dma_start(out=sap, in_=dap)
        return self.P.emit("sync", fn, w, (sem, 16))

    def flag_write(self, event, w=()):
        if not self.fused:
            return
        assert self.semq.setdefault("flg", "sync") == "sync"
        fl = self.ds["SFlag"]
        def fn(y, e=event, fl=fl):
            y.reg_mul(REGS2["rT"], REGS2["rJ"], 8)
            y.reg_add(REGS2["rT"], REGS2["rT"], e)
            return y.dma_start(out=bass.AP(fl.tensor, REGS2["rT"], [[1, 1], [1, 1]]),
                               in_=self.sb["flagv_s"][0:1, e:e + 1])
        self.P.emit("sync", fn, w, ("flg", 16))

    def flag_poll(self, event):
        if not self.fused:
            return None
        fl = self.ds["SFlag"]
        def fn(g, e=event, fl=fl):
            g.reg_add(REGS["rT"], REGS["rFR"], e)
            g.reg_add(REGS["rExp"], REGS["rNon"], e + 1)
            g.reg_mov(REGS["rC"], 1)
            with g.While(REGS["rC"]):
                g.reg_add(REGS["rP"], REGS["rP"], 16)
                g.dma_start(out=self.sb["flin"][0:1, 0:1],
                            in_=bass.AP(fl.tensor, REGS["rT"], [[1, 1], [1, 1]])
                            ).then_inc(SEMH["PL"], 16)
                g.wait_ge(SEMH["PL"], REGS["rP"])
                g.reg_load(REGS["rF"], self.sb["flin"][0:1, 0:1])
                g.reg_alu(REGS["rC"], REGS["rF"], REGS["rExp"], ALU.not_equal)
            return g.nop()
        return self.P.emit("gpsimd", fn, (), ("PO", 1))


REGS = {}
REGS2 = {}
SEMH = {}


EV_L0, EV_ATT0, EV_X2, EV_ATT1 = 0, 1, 2, 3


class Arena:

    def __init__(self, handle, nbytes):
        self.h = handle
        self.hf = handle.bitcast(F32)
        self.h8 = handle.bitcast(FP8)
        self.nbytes = nbytes
        self.off = 0

    def reset(self):
        self.off = getattr(self, "floor", 0)

    def _take(self, nb):
        nb = (nb + 63) & ~63
        assert self.off + nb <= self.nbytes, f"arena overflow {self.off}+{nb}"
        o = self.off
        self.off += nb
        return o

    def b(self, parts, cols):
        o = self._take(cols * 2)
        return self.h[0:parts, o // 2: o // 2 + cols]

    def f(self, parts, cols):
        o = self._take(cols * 4)
        return self.hf[0:parts, o // 4: o // 4 + cols]

    def e8(self, parts, cols):
        o = self._take(cols)
        return self.h8[0:parts, o: o + cols]


def _replay(nc, prog):
    sems = {}
    es = contextlib.ExitStack()
    with es:
        for s in sorted(prog.semnames):
            sems[s] = es.enter_context(nc.semaphore(s))
        SEMH.clear()
        SEMH.update(sems)
        block = es.enter_context(nc.Block())

        def run(engname, e):
            for fn, waits, inc in prog.ops[engname]:
                for s, v in waits:
                    e.wait_ge(sems[s], v)
                ins = fn(e)
                if inc is not None:
                    ins.then_inc(sems[inc[0]], inc[1])

        @block.tensor
        def _(t):
            run("tensor", t)

        @block.vector
        def _(v):
            run("vector", v)

        @block.scalar
        def _(s):
            run("scalar", s)

        @block.sync
        def _(y):
            es3 = contextlib.ExitStack()
            with es3:
                REGS2.clear()
                for rn in ("rJ", "rT"):
                    REGS2[rn] = es3.enter_context(y.register("s" + rn))
                run("sync", y)

        @block.gpsimd
        def _(g):
            rnames = ("rJ", "rFW", "rFR", "rNon", "rT", "rF", "rC", "rP", "rExp")
            regs = [es2 := contextlib.ExitStack()]
            with es2:
                REGS.clear()
                for rn in rnames:
                    REGS[rn] = es2.enter_context(g.register(rn))
                run("gpsimd", g)
    return nc


class Phases(Builder):
    def p0(self):
        P, S = self.P, self.S
        nc = self.nc
        self.arena_t = S("arena", [128, 67584], BF16)
        self.ar = Arena(self.arena_t, 135168)
        xs = S("xs", [128, FC * TOK], BF16)
        mt = S("mt", [1, 16], I32)
        S("flagv_s", [1, 8], I32)
        S("flin", [1, 1], I32)
        S("onescol_b", [128, 1], BF16)
        S("onesrow_f", [1, 128], F32)
        S("onesrow_b", [1, 128], BF16)
        S("cst", [128, 4], F32)
        S("onescol_f", [128, 1], F32)
        S("qnw_sb", [128, 12], F32)
        S("kvnw_sb", [128, 4], F32)
        S("bo1_sb", [128, 16], F32)
        S("fb1_sb", [128, 16], F32)
        S("bq1_sb", [128, 8], F32)
        S("bk1_sb", [128, 2], F32)
        S("bv1_sb", [1, 256], BF16)
        if "P1" in self.ph:
            self._tabq = self.ar.b(64, TOK)
            self._tabk = self.ar.b(64, TOK)
            self.sb["tabq_s"] = self._tabq
            self.sb["tabk_s"] = self._tabk
            self.ar.floor = self.ar.off
        S("rstdkv", [1, TOK], F32)
        wm = self.dma("gpsimd", mt[:], self.ds["meta"], sem="cG")
        self.dma("gpsimd", self.sb["flagv_s"][:], self.ds["flagv"], sem="cG")
        wf = ("cG", P.val("cG"))

        def initregs(g):
            g.reg_load(REGS["rJ"], mt[0:1, 6:7])
            g.reg_load(REGS["rFW"], mt[0:1, 2:3])
            g.reg_load(REGS["rFR"], mt[0:1, 3:4])
            g.reg_load(REGS["rNon"], mt[0:1, 4:5])
            g.reg_mov(REGS["rP"], 0)
            return g.nop()
        P.emit("gpsimd", initregs, [wf], ("PO", 1))

        def initregs_sync(y):
            y.reg_load(REGS2["rJ"], mt[0:1, 6:7])
            return y.nop()
        P.emit("sync", initregs_sync, [wf], ("PO", 1))
        if self.first == "P1":
            for f in range(FC):
                self.dma("gpsimd", xs[:, f * TOK:(f + 1) * TOK],
                         self.ds["x_fm"][f * 128:(f + 1) * 128, :], sem="cG")
            self.w_stream = None
        else:
            si = self.ds["stream_in"]
            for f in range(FC):
                self.dma("sync", xs[:, f * TOK:(f + 1) * TOK], si[f], sem="cS")
        for n, srcn in (("qnw_sb", "qnw"), ("kvnw_sb", "kvnw"), ("bo1_sb", "bo1"),
                        ("fb1_sb", "fb1"), ("bq1_sb", "bq1"), ("bk1_sb", "bk1")):
            self.dma("sync", self.sb[n][:], self.ds[srcn], sem="cS")
        self.dma("gpsimd", self.sb["bv1_sb"][:], self.ds["bv1"], sem="cG")
        if "P1" in self.ph:
            self.dma("gpsimd", self._tabq, self.ds["tabq"], sem="cG")
            self.dma("gpsimd", self._tabk, self.ds["tabk"], sem="cG")
        pass
        P.emit("vector", lambda v: v.memset(self.sb["onescol_b"][:], 1.0), (), ("DV", 1))
        P.emit("vector", lambda v: v.memset(self.sb["onesrow_f"][:], 1.0), (), ("DV", 1))
        P.emit("vector", lambda v: v.memset(self.sb["onesrow_b"][:], 1.0), (), ("DV", 1))
        P.emit("vector", lambda v: v.memset(self.sb["onescol_f"][:], 1.0), (), ("DV", 1))
        P.emit("vector", lambda v: v.memset(self.sb["cst"][:, 0:1], EPSR), (), ("DV", 1))
        P.emit("vector", lambda v: v.memset(self.sb["cst"][:, 1:2], EPSL), (), ("DV", 1))
        P.emit("vector", lambda v: v.memset(self.sb["cst"][:, 2:3], -SHIFT), (), ("DV", 1))
        self.w_ones = ("DV", P.val("DV"))
        if "P1" in self.ph:
            self.wkr_load()
        self.w_cG = ("cG", P.val("cG"))
        self.w_cS = ("cS", P.val("cS"))
        self.w_stream = self.w_cG if self.first == "P1" else self.w_cS
        self.w_const = [self.w_cG, self.w_cS]
        self.w_krw = self.w_cG

    def prep_weights(self):
        names = []
        if "P1" in self.ph:
            names += ["wqa", "wkva", "wqb"]
        if "P2" in self.ph:
            names += ["wkvb"]
        if "P3" in self.ph:
            names += ["wo0"]
        if "P6" in self.ph:
            names += ["wq1", "wk1", "wv1"]
        if "P7" in self.ph:
            names += ["wo1"]
        if "P8" in self.ph:
            names += ["w1t"]
        self.wcast = {}
        for n in names:
            shp = list(self.ds[n].shape)
            b = self.dlocal("b_" + n, shp, BF16)
            nch = max(1, (shp[0] * shp[1]) // (1 << 21))
            rows = shp[0] // nch
            for r in range(nch):
                self.dma("gpsimd", b[r * rows:(r + 1) * rows, :],
                         self.ds[n][r * rows:(r + 1) * rows, :], (),
                         sem="c_" + n)
            self.wcast[n] = ("c_" + n, self.P.val("c_" + n))

    def prep_ffn0_scratch(self):
        P = self.P
        for srcn, dst, nr in (("wg0", self.ds["w8g0"], 4), ("wg1", self.ds["w8g1"], 4),
                              ("wlin", self.ds["w8lin"], 4)):
            rows = self.ds[srcn].shape[0] // nr
            for r in range(nr):
                self.dma("gpsimd", dst[r * rows:(r + 1) * rows, :],
                         self.ds[srcn][r * rows:(r + 1) * rows, :], (), sem="s8")
        self.w_scr = ("s8", P.val("s8"))

    def p1(self):
        P, ar = self.P, self.ar
        ar.reset()
        xs = self.sb["xs"]
        A, Bp, C, Dp = self.psA, self.psB, self.psC, self.psD
        latq = [ar.b(128, TOK) for _ in range(12)]
        Bq = ar.f(128, TOK)
        wqa = [ar.b(128, FC * 128) for _ in range(2)]
        wkva = [ar.b(128, FC * 128)]
        wkva.append(wkva[0])
        wqb = [ar.b(128, 12 * DQK) for _ in range(2)]
        wkr = self.sb["wkrt"]
        sq = [ar.b(128, 512) for _ in range(2)]
        lkv = [ar.b(128, 512) for _ in range(2)]
        qp = [ar.b(128, TOK) for _ in range(2)]
        qr = [ar.b(64, TOK) for _ in range(2)]
        krt = [ar.b(64, 512) for _ in range(2)]
        ropf = [ar.f(32, 512) for _ in range(3)]
        rows = ar.f(1, 4 * 512)
        ev_half = [self.w_ones, self.w_ones]
        st_q = []
        wconst = list(self.w_const)
        for oc in range(12):
            wld = self.dma("sync", wqa[oc % 2].rearrange("p (c m) -> p c m", c=FC),
                           self.ds["b_wqa"][:, oc * 128:(oc + 1) * 128]
                           .rearrange("(c p) m -> p c m", p=128),
                           [self.wcast["wqa"]] +
                           ([("MM", P.val("MM"))] if oc >= 2 else []),
                           sem=f"p1a{oc % 2}")
            for tc in range(4):
                ps = A[:, (tc % 2) * 512:(tc % 2) * 512 + 512]
                w = [wld, self.w_stream, ev_half[tc % 2]]
                for ic in range(FC):
                    self.mm(ps, wqa[oc % 2][:, ic * 128:(ic + 1) * 128],
                            xs[:, ic * TOK + tc * 512: ic * TOK + tc * 512 + 512],
                            ic == 0, ic == FC - 1, w if ic == 0 else (),
                            ("MM", 1) if ic == FC - 1 else None)
                wmm = ("MM", P.val("MM"))
                wv = P.emit("vector", lambda v, o=latq[oc], t=tc, p=ps, oc_=oc:
                            v.tensor_scalar(o[:, t * 512:(t + 1) * 512], p,
                                            self.sb["qnw_sb"][:, oc_:oc_ + 1], None,
                                            ALU.mult),
                            [wmm] + (wconst if oc == 0 else []), ("DV", 1))
                ev_half[tc % 2] = wv
                wa = P.emit("scalar", lambda s, q=sq[(oc * 4 + tc) % 2], o=latq[oc], t=tc:
                            s.activation(q[:], o[:, t * 512:(t + 1) * 512], AF.Square),
                            [wv], ("AC", 1))
                stat = (C if tc < 2 else Dp)[0:1, (tc % 2) * 512:(tc % 2) * 512 + 512]
                ws = self.mm(stat, self.sb["onescol_b"][:],
                             sq[(oc * 4 + tc) % 2][:], oc == 0, oc == 11,
                             [wa, self.w_ones], ("MM", 1) if oc == 11 else None,
                             skip=True)
                if oc == 11:
                    st_q.append(("MM", P.val("MM")))
        wBq = []
        for tc in range(4):
            stat = (C if tc < 2 else Dp)[0:1, (tc % 2) * 512:(tc % 2) * 512 + 512]
            r_sl = rows[0:1, tc * 512:(tc + 1) * 512]
            w1 = P.emit("vector", lambda v, s=stat, r=r_sl:
                        v.tensor_scalar(r, s, 1.0 / QL, None, ALU.mult),
                        [st_q[tc]], ("DV", 1))
            w2 = P.emit("scalar", lambda s, r=r_sl:
                        s.activation(r, r, AF.Sqrt, bias=self.sb["cst"][0:1, 0:1]), [w1], ("AC", 1))
            w3 = P.emit("vector", lambda v, r=r_sl: v.reciprocal(r, r), [w2], ("DV", 1))
            wb = self.mm(A[:, (tc % 2) * 512:(tc % 2) * 512 + 512],
                         self.sb["onesrow_f"][:], r_sl, True, True,
                         [w3, ev_half[tc % 2], self.w_ones], ("MM", 1))
            wc = P.emit("vector", lambda v, t=tc, p=A[:, (tc % 2) * 512:(tc % 2) * 512 + 512]:
                        v.tensor_copy(Bq[:, t * 512:(t + 1) * 512], p), [wb], ("DV", 1))
            ev_half[tc % 2] = wc
            wBq.append(wc)
        self.wBq = wBq
        st_kv = []
        ev_bhalf = [self.w_ones, self.w_ones]
        lkv_done = [None, None]
        for oc in range(4):
            wld = self.dma("sync", wkva[oc % 2].rearrange("p (c m) -> p c m", c=FC),
                           self.ds["b_wkva"][:, oc * 128:(oc + 1) * 128]
                           .rearrange("(c p) m -> p c m", p=128),
                           [self.wcast["wkva"]] +
                           ([("MM", P.val("MM"))] if oc >= 1 else []),
                           sem="p1b")
            for tc in range(4):
                ps = Bp[:, (tc % 2) * 512:(tc % 2) * 512 + 512]
                w = [wld, ev_bhalf[tc % 2]]
                for ic in range(FC):
                    self.mm(ps, wkva[oc % 2][:, ic * 128:(ic + 1) * 128],
                            xs[:, ic * TOK + tc * 512: ic * TOK + tc * 512 + 512],
                            ic == 0, ic == FC - 1, w if ic == 0 else (),
                            ("MM", 1) if ic == FC - 1 else None)
                wmm = ("MM", P.val("MM"))
                i2 = (oc * 4 + tc) % 2
                wlk = lkv_done[i2] or ()
                wv = P.emit("vector", lambda v, o=lkv[i2], p=ps, oc_=oc:
                            v.tensor_scalar(o[:], p, self.sb["kvnw_sb"][:, oc_:oc_ + 1],
                                            None, ALU.mult),
                            [wmm] + ([wlk] if wlk else []), ("DV", 1))
                ev_bhalf[tc % 2] = wv
                lkv_done[i2] = self.exch_lkv_write(oc, tc, lkv[i2], [wv], i2)
                wa = P.emit("scalar", lambda s, q=sq[i2], o=lkv[i2]:
                            s.activation(q[:], o[:], AF.Square), [wv], ("AC", 1))
                stat = (C if tc < 2 else Dp)[0:1, (tc % 2) * 512:(tc % 2) * 512 + 512]
                ws = self.mm(stat, self.sb["onescol_b"][:], sq[i2][:],
                             oc == 0, oc == 3, [wa], ("MM", 1) if oc == 3 else None,
                             skip=True)
                if oc == 3:
                    st_kv.append(("MM", P.val("MM")))
        for tc in range(4):
            stat = (C if tc < 2 else Dp)[0:1, (tc % 2) * 512:(tc % 2) * 512 + 512]
            r_sl = self.sb["rstdkv"][0:1, tc * 512:(tc + 1) * 512]
            w1 = P.emit("vector", lambda v, s=stat, r=r_sl:
                        v.tensor_scalar(r, s, 1.0 / KVL, None, ALU.mult),
                        [st_kv[tc]], ("DV", 1))
            w2 = P.emit("scalar", lambda s, r=r_sl:
                        s.activation(r, r, AF.Sqrt, bias=self.sb["cst"][0:1, 0:1]), [w1], ("AC", 1))
            w3 = P.emit("vector", lambda v, r=r_sl: v.reciprocal(r, r), [w2], ("DV", 1))
            self.exch_rstd_write(tc, r_sl, [w3])
        kr_done = [None, None]
        for tc in range(4):
            ps = Bp[:, (tc % 2) * 512:(tc % 2) * 512 + 512][0:64, :]
            for ic in range(FC):
                self.mm(ps, wkr[:, ic * DR:(ic + 1) * DR],
                        xs[:, ic * TOK + tc * 512: ic * TOK + tc * 512 + 512],
                        ic == 0, ic == FC - 1,
                        [ev_bhalf[tc % 2], self.w_krw] if ic == 0 else (),
                        ("MM", 1) if ic == FC - 1 else None)
            wmm = ("MM", P.val("MM"))
            wrp = self.rope(ps, krt[tc % 2][:], "tabk_s", tc, None, ropf,
                            [wmm] + ([kr_done[tc % 2]] if kr_done[tc % 2] else []))
            kr_done[tc % 2] = self.exch_krot_write(tc, krt[tc % 2], [wrp])
            ev_bhalf[tc % 2] = wrp
        for h in range(NH):
            wld = self.dma("sync", wqb[h % 2].rearrange("p (c m) -> p c m", c=12),
                           self.ds["b_wqb"][:, h * DQK:(h + 1) * DQK]
                           .rearrange("(c p) m -> p c m", p=128),
                           [self.wcast["wqb"]] +
                           ([("MM", P.val("MM"))] if h >= 2 else []),
                           sem=f"p1c{h % 2}")
            qp_done = getattr(self, "_qp_done", [None, None])
            ev_rot = getattr(self, "_ev_rot", [("DV", P.val("DV")), ("DV", P.val("DV"))])
            wq_evs = []
            for tc in range(4):
                psp = A[:, (tc % 2) * 512:(tc % 2) * 512 + 512]
                psr = C[0:64, (tc % 2) * 512:(tc % 2) * 512 + 512]
                wstart = [wld, ev_half[tc % 2]] + ([qp_done[h % 2]] if qp_done[h % 2] else [])
                for lc in range(12):
                    self.mm(psp, wqb[h % 2][:, lc * DQK: lc * DQK + DN],
                            latq[lc][:, tc * 512:(tc + 1) * 512],
                            lc == 0, lc == 11, wstart if lc == 0 else (),
                            ("MM", 1) if lc == 11 else None)
                wp = ("MM", P.val("MM"))
                for lc in range(12):
                    self.mm(psr, wqb[h % 2][:, lc * DQK + DN: lc * DQK + DQK],
                            latq[lc][:, tc * 512:(tc + 1) * 512],
                            lc == 0, lc == 11,
                            [ev_rot[tc % 2]] if lc == 0 else (),
                            ("MM", 1) if lc == 11 else None)
                wr = ("MM", P.val("MM"))
                we1 = P.emit("vector", lambda v, o=qp[h % 2], t=tc, p=psp:
                             v.scalar_tensor_tensor(o[:, t * 512:(t + 1) * 512], p, SC0,
                                                    Bq[:, t * 512:(t + 1) * 512],
                                                    ALU.mult, ALU.mult),
                             [wp] + wBq[tc:tc + 1], ("DV", 1))
                ev_half[tc % 2] = we1
                we2 = self.rope(psr, qr[h % 2][:, tc * 512:(tc + 1) * 512],
                                "tabq_s", tc, Bq[0:32, tc * 512:(tc + 1) * 512],
                                ropf, [wr])
                ev_rot[tc % 2] = we2
                wq_evs += [we1, we2]
            dn = self.exch_q_write(h, qp[h % 2], qr[h % 2], wq_evs)
            qp_done[h % 2] = dn
            self._qp_done = qp_done
            self._ev_rot = ev_rot


    def rope(self, ps, out512, tab, tc, Bsl, ropf, w0):
        P = self.P
        t = self.sb[tab]
        c_sl = t[0:32, tc * 512:(tc + 1) * 512]
        s_sl = t[32:64, tc * 512:(tc + 1) * 512]
        p1, p2 = ps[0:32, :], ps[32:64, :]
        a, b, d = ropf
        ch = getattr(self, "_rope_ch", None)
        wl = None
        for half in range(2):
            w0h = list(w0) if half == 0 else []
            if ch:
                w0h.append(ch)
            ch = P.emit("vector", lambda v, p=(p1 if half == 0 else p2), c=c_sl:
                        v.tensor_mul(a[:], p, c), w0h, ("DV", 1))
            ch = P.emit("vector", lambda v, p=(p2 if half == 0 else p1), s=s_sl:
                        v.tensor_mul(b[:], p, s), [ch], ("DV", 1))
            op = ALU.subtract if half == 0 else ALU.add
            tgt = out512[0:32, :] if half == 0 else out512[32:64, :]
            if Bsl is None:
                ch = P.emit("vector", lambda v, o=tgt, oo=op:
                            v.tensor_tensor(o, a[:], b[:], oo), [ch], ("DV", 1))
            else:
                ch = P.emit("vector", lambda v, oo=op: v.tensor_tensor(d[:], a[:], b[:], oo),
                            [ch], ("DV", 1))
                ch = P.emit("vector", lambda v, o=tgt:
                            v.tensor_mul(o, d[:], Bsl), [ch], ("DV", 1))
            wl = ch
        self._rope_ch = ch
        return wl



class Full(Phases):

    def decl_exchange(self):
        pv = (lambda n, s, d: self.dout(n, s, d)) if not self.fused else \
             (lambda n, s, d: self.dlocal(n, s, d))
        pr = (lambda n, s, d: self.din(n, s, d)) if not self.fused else \
             (lambda n, s, d: self.dlocal(n, s, d))
        if "P1" in self.ph:
            pv("Qp_o", [NH, 128, TOK], BF16)
            pv("Qr_o", [NH, 64, TOK], BF16)
            pv("Lkv_o", [4, 128, TOK], BF16)
            pv("Rt_o", [64, TOK], BF16)
            pv("Rstd_o", [1, TOK], F32)
        if "P2" in self.ph:
            pr("Qp_i", [MYH, 128, T], BF16)
            pr("Qr_i", [MYH, 64, T], BF16)
            if not self.fused:
                self.din("Lkv_i", [4, 128, T], BF16)
                self.din("Rt_i", [64, T], BF16)
                self.din("Rstd_i", [1, T], F32)
            pv("O_o", [MYH, 128, T], BF16)
        if "P3" in self.ph:
            pr("O_i", [NH, 128, TOK], BF16)
        if "P5" in self.ph:
            pv("X2_o", [FC, 128, TOK], BF16)
        if "P6" in self.ph:
            if not self.fused:
                self.din("X2_i", [FC, 128, T], BF16)
            pv("O1_o", [MYH, 128, T], BF16)
        if "P7" in self.ph:
            pr("O1_i", [NH, 128, TOK], BF16)
        if self.fused:
            self.dshared("SQp", [NH, 128, T], BF16)
            self.dshared("SQr", [NH, 64, T], BF16)
            self.dshared("SLkv", [4, 128, T], BF16)
            self.dshared("SRt", [64, T], BF16)
            self.dshared("SRstd", [1, T], F32)
            self.dshared("So", [NH, 128, T], BF16)
            self.dshared("Sx", [FC, 128, T], BF16)
            self.dshared("SFlag", [16, 1], I32)
            self.ds["Lkv_i"] = self.ds["SLkv"]
            self.ds["Rt_i"] = self.ds["SRt"]
            self.ds["Rstd_i"] = self.ds["SRstd"]
            self.ds["X2_i"] = self.ds["Sx"]

    def exch_q_write(self, h, qp, qr, waits):
        sem = f"xqw{h % 2}"
        self.dma("gpsimd", self.ds["Qp_o"][h], qp, waits, sem=sem)
        self.dma("gpsimd", self.ds["Qr_o"][h], qr, waits, sem=sem)
        return (sem, self.P.val(sem))

    def exch_lkv_write(self, lc, tc, tile, waits, slot):
        sem = f"xlk{slot}"
        self.dma("gpsimd", self.ds["Lkv_o"][lc][:, tc * 512:(tc + 1) * 512],
                 tile, waits, sem=sem)
        return (sem, self.P.val(sem))

    def exch_krot_write(self, tc, tile, waits):
        sem = f"xkr{tc % 2}"
        self.dma("gpsimd", self.ds["Rt_o"][:, tc * 512:(tc + 1) * 512],
                 tile, waits, sem=sem)
        return (sem, self.P.val(sem))

    def exch_rstd_write(self, tc, row, waits):
        self.dma("gpsimd", self.ds["Rstd_o"][0:1, tc * 512:(tc + 1) * 512],
                 row, waits, sem="xrs")
        return ("xrs", self.P.val("xrs"))

    def fuse_l0pre(self):
        if not self.fused:
            return
        P = self.P
        w_all = [(s, P.val(s)) for s in ("xlk0", "xlk1", "xkr0", "xkr1", "xrs",
                                         "xqw0", "xqw1") if P.val(s) > 0]
        for name, rows in (("Qp", NH * 128), ("Qr", NH * 64), ("Lkv", 4 * 128)):
            self.dyn_dma(self.ds["S" + name], TOK, 0, [[T, rows], [1, TOK]],
                         self.ds[name + "_o"].rearrange("a b t -> (a b) t"),
                         w_all, "xsc", True)
        self.dyn_dma(self.ds["SRt"], TOK, 0, [[T, 64], [1, TOK]],
                     self.ds["Rt_o"], w_all, "xsc", True)
        self.dyn_dma(self.ds["SRstd"], TOK, 0, [[T, 1], [1, TOK]],
                     self.ds["Rstd_o"], w_all, "xsc", True)
        self.flag_write(EV_L0, [("xsc", P.val("xsc"))])
        wp = self.flag_poll(EV_L0)
        self.dyn_dma(self.ds["SQp"], MYH * 128 * T, 0, [[T, MYH * 128], [1, T]],
                     self.ds["Qp_i"].rearrange("a b t -> (a b) t"), [wp], "xga",
                     False)
        self.dyn_dma(self.ds["SQr"], MYH * 64 * T, 0, [[T, MYH * 64], [1, T]],
                     self.ds["Qr_i"].rearrange("a b t -> (a b) t"), [wp], "xga",
                     False)
        self.w_qgather = ("xga", P.val("xga"))
        self.w_poll_l0 = wp

    def fuse_att0(self):
        if not self.fused:
            return
        P = self.P
        w_all = [(s, P.val(s)) for s in ("xow0", "xow1") if P.val(s) > 0]
        self.dyn_dma(self.ds["So"], MYH * 128 * T, 0, [[T, MYH * 128], [1, T]],
                     self.ds["O_o"].rearrange("a b t -> (a b) t"), w_all, "xsc",
                     True)
        self.flag_write(EV_ATT0, [("xsc", P.val("xsc"))])
        wp = self.flag_poll(EV_ATT0)
        self.dyn_dma(self.ds["So"], TOK, 0, [[T, NH * 128], [1, TOK]],
                     self.ds["O_i"].rearrange("a b t -> (a b) t"), [wp], "xga",
                     False)
        self.w_ogather0 = ("xga", P.val("xga"))

    def fuse_x2(self):
        if not self.fused:
            return
        P = self.P
        w_all = [("xxw", P.val("xxw"))]
        self.dyn_dma(self.ds["Sx"], TOK, 0, [[T, FC * 128], [1, TOK]],
                     self.ds["X2_o"].rearrange("a b t -> (a b) t"), w_all, "xsc",
                     True)
        self.flag_write(EV_X2, [("xsc", P.val("xsc"))])
        self.w_poll_x2 = self.flag_poll(EV_X2)

    def fuse_att1(self):
        if not self.fused:
            return
        P = self.P
        w_all = [(s, P.val(s)) for s in ("xo10", "xo11") if P.val(s) > 0]
        self.dyn_dma(self.ds["So"], MYH * 128 * T, 0, [[T, MYH * 128], [1, T]],
                     self.ds["O1_o"].rearrange("a b t -> (a b) t"), w_all, "xsc",
                     True)
        self.flag_write(EV_ATT1, [("xsc", P.val("xsc"))])
        wp = self.flag_poll(EV_ATT1)
        self.dyn_dma(self.ds["So"], TOK, 0, [[T, NH * 128], [1, TOK]],
                     self.ds["O1_i"].rearrange("a b t -> (a b) t"), [wp], "xga",
                     False)
        self.w_ogather1 = ("xga", P.val("xga"))


    def build(self):
        nc = self.nc
        self.din("meta", [1, 16], I32)
        self.din("flagv", [1, 8], I32)
        self.din("x_fm", [D, TOK])
        for n, sh in (("wqa", [D, QL]), ("wkva", [D, KVL]), ("wkr", [D, DR]),
                      ("wqb", [QL, NH * DQK]), ("wkvb", [KVL, MYH * 256]),
                      ("wo0", [D, D]), ("wg0", [D, HID]), ("wg1", [D, HID]),
                      ("wlin", [HID, D]), ("wq1", [D, MYH * DV]),
                      ("wk1", [D, 256]), ("wv1", [D, 256]), ("wo1", [D, D]),
                      ("w1t", [D, D])):
            self.din(n, sh)
        for n, c in (("qnw", 12), ("kvnw", 4), ("bo1", 16), ("fb1", 16),
                     ("bq1", 8), ("bk1", 2)):
            self.din(n, [128, c])
        self.din("bv1", [1, 256])
        self.din("tabq", [64, TOK])
        self.din("tabk", [64, TOK])
        if self.first != "P1":
            self.din("stream_in", [FC, 128, TOK], BF16)
        if self.last != "P8":
            self.dout("stream_out", [FC, 128, TOK], BF16)
        else:
            self.dout("out", [FC, 128, TOK])
        self.decl_exchange()
        if "P4" in self.ph:
            self.dlocal("w8g0", [D, HID], FP8)
            self.dlocal("w8g1", [D, HID], FP8)
            self.dlocal("w8lin", [HID, D], FP8)
        if "P6" in self.ph:
            self.dlocal("Qs", [MYH, 128, T], BF16)

        with self.es:
            self.psA = self.es.enter_context(self.nc.psum_tensor("psA", [128, 1024], F32))
            self.psB = self.es.enter_context(self.nc.psum_tensor("psB", [128, 1024], F32))
            self.psC = self.es.enter_context(self.nc.psum_tensor("psC", [128, 1024], F32))
            self.psD = self.es.enter_context(self.nc.psum_tensor("psD", [128, 1024], F32))
            self.p0()
            self.prep_weights()
            if "P1" in self.ph:
                self.p1()
                self.fuse_l0pre()
                self.ar.floor = 0
            if "P4" in self.ph:
                self.prep_ffn0_scratch()
            if "P2" in self.ph:
                self.P.barrier()
                self.p2()
                self.fuse_att0()
            if "P3" in self.ph:
                self.P.barrier()
                self.p3()
            if "P4" in self.ph:
                self.P.barrier()
                self.p4()
            if "P5" in self.ph:
                self.p5()
                self.fuse_x2()
            if "P6" in self.ph:
                self.P.barrier()
                self.p6()
                self.fuse_att1()
            if "P7" in self.ph:
                self.P.barrier()
                self.p7()
            if "P8" in self.ph:
                self.P.barrier()
                self.p8()
            if self.last != "P8":
                self.P.barrier()
                self.stream_save()
            self.finish()
            _replay(nc, self.P)
        return nc

    def wkr_load(self):
        self.sb["wkrt"] = self.es.enter_context(
            self.nc.sbuf_tensor("wkrt", [128, FC * DR], BF16))
        self.dma(
            "gpsimd", self.sb["wkrt"][:].rearrange("p (c m) -> p c m", c=FC),
            self.ds["wkr"].rearrange("(c p) m -> p c m", p=128), sem="cG")

    def stream_save(self):
        xs = self.sb["xs"]
        so = self.ds["stream_out"]
        for f in range(FC):
            self.dma("sync", so[f], xs[:, f * TOK:(f + 1) * TOK], sem="sso")

    def finish(self):
        P = self.P
        w = list(P.barrier_waits())
        P.emit("sync", lambda y: y.nop(), w)
        P.emit("gpsimd", lambda g: g.nop(), w)


class Full2(Full):
    def p2(self):
        P, ar = self.P, self.ar
        ar.reset()
        A, Bp, C, Dp = self.psA, self.psB, self.psC, self.psD
        latkvF = [ar.b(128, T) for _ in range(4)]
        krotF = ar.b(64, T)
        Bkv = ar.b(128, T)
        rstdc = ar.f(128, 32)
        rrow = ar.b(1, T)
        kp = ar.b(128, T)
        vtm = ar.b(128, T)
        qpb = [ar.b(128, T) for _ in range(2)]
        qrb = [ar.b(64, T)]
        qrb.append(qrb[0])
        wkvb = [ar.b(128, 4 * 256) for _ in range(2)]
        Eb = [ar.b(128, 1024) for _ in range(4)]
        Aacc = ar.b(128, 512)
        osb = [ar.b(128, 512) for _ in range(2)]
        bsb = [ar.f(128, 512) for _ in range(2)]
        srow = ar.f(1, 512)
        self._prevE = {}
        self._wexp_prev = {}
        wp = [self.w_poll_l0] if self.fused else []
        for lc in range(4):
            self.dma("sync", latkvF[lc], self.ds["Lkv_i"][lc], wp, sem="g2s")
        self.dma("sync", krotF, self.ds["Rt_i"], wp, sem="g2s")
        self.dma("gpsimd", rrow, self.ds["Rstd_i"], wp, sem="g2g")
        wgat = ("g2s", P.val("g2s"))
        wgat2 = ("g2g", P.val("g2g"))
        wBkv = []
        for kc in range(8):
            wb = self.mm(A[:, (kc % 2) * 512:(kc % 2) * 512 + 512],
                         self.sb["onesrow_b"][:], rrow[0:1, kc * 512:(kc + 1) * 512],
                         True, True,
                         [wgat, wgat2, self.w_ones] + ([wBkv[kc - 2]] if kc >= 2 else []),
                         ("MM", 1), skip=True)
            wc = P.emit("vector", lambda v, k=kc,
                        p=A[:, (kc % 2) * 512:(kc % 2) * 512 + 512]:
                        v.tensor_copy(Bkv[:, k * 512:(k + 1) * 512], p), [wb], ("DV", 1))
            wBkv.append(wc)
        wBkv_all = ("DV", P.val("DV"))
        wcp = [None, None]
        for tt in range(32):
            hb2 = tt % 2
            dst = Bp[:, hb2 * 512: hb2 * 512 + 1]
            wb = self.mm(dst, rrow[0:1, tt * 128:(tt + 1) * 128],
                         self.sb["onesrow_b"][0:1, 0:1], True, True,
                         [wgat, wgat2, self.w_ones] + ([wcp[hb2]] if wcp[hb2] else []),
                         ("MM", 1), skip=True)
            wcp[hb2] = P.emit("vector", lambda v, t=tt, p=dst:
                              v.tensor_copy(rstdc[:, t:t + 1], p), [wb], ("DV", 1))
        wrst = ("DV", P.val("DV"))
        wq_prev = [None, None]
        wkv_prev = None
        oev_half = [None, None]
        for h in range(MYH):
            hb = h % 2
            wqg = [self.w_qgather] if self.fused else []
            wqld1 = self.dma("sync", qpb[hb], self.ds["Qp_i"][h],
                             wqg + ([wq_prev[hb]] if wq_prev[hb] else []),
                             sem=f"q2l{hb}")
            wqld2 = self.dma("sync", qrb[hb], self.ds["Qr_i"][h],
                             [("MM", P.val("MM"))], sem=f"q2l{hb}")
            wqld1 = wqld2
            wwl = self.dma("sync", wkvb[hb].rearrange("p (c m) -> p c m", c=4),
                           self.ds["b_wkvb"][:, h * 256:(h + 1) * 256]
                           .rearrange("(c p) m -> p c m", p=128),
                           [self.wcast["wkvb"]] +
                           ([("MM", P.val("MM"))] if h >= 2 else []),
                           sem=f"p2w{hb}")
            wkp = []
            for kc in range(8):
                ps = A[:, (kc % 2) * 512:(kc % 2) * 512 + 512]
                w0 = ([wwl, wBkv_all, ("DV", P.val("DV")), ("AC", P.val("AC"))]
                      if kc < 2 else [wwl, wkp[kc - 2]])
                for lc in range(4):
                    self.mm(ps, wkvb[hb][:, lc * 256: lc * 256 + 128],
                            latkvF[lc][:, kc * 512:(kc + 1) * 512],
                            lc == 0, lc == 3,
                            (w0 + [wgat]) if lc == 0 else (),
                            ("MM", 1) if lc == 3 else None)
                wkp.append(P.emit("vector", lambda v, k=kc, p=ps:
                                  v.tensor_mul(kp[:, k * 512:(k + 1) * 512], p,
                                               Bkv[:, k * 512:(k + 1) * 512]),
                                  [("MM", P.val("MM"))], ("DV", 1)))
            wvt = []
            for g4 in range(8):
                ps = Bp[:, (g4 % 2) * 512:(g4 % 2) * 512 + 512]
                for s in range(4):
                    tt = g4 * 4 + s
                    if s == 0:
                        w0v = ([wrst, ("DV", P.val("DV"))] if g4 < 2
                               else [wvt[4 * (g4 - 2) + 3]])
                    for lc in range(4):
                        self.mm(ps[:, s * 128:(s + 1) * 128],
                                latkvF[lc][:, tt * 128:(tt + 1) * 128],
                                wkvb[hb][:, lc * 256 + 128: lc * 256 + 256],
                                lc == 0, lc == 3,
                                w0v if (s == 0 and lc == 0) else (),
                                ("MM", 1) if lc == 3 else None)
                wmm = ("MM", P.val("MM"))
                for s in range(4):
                    tt = g4 * 4 + s
                    wvt.append(P.emit("vector", lambda v, t=tt, p=ps, s_=s:
                                      v.tensor_scalar(vtm[:, t * 128:(t + 1) * 128],
                                                      p[:, s_ * 128:(s_ + 1) * 128],
                                                      rstdc[:, t:t + 1], None, ALU.mult),
                                      [wmm], ("DV", 1)))
            wkv_done = ("DV", P.val("DV"))
            wq1 = wqld1
            wq2 = wqld2
            for qc in range(8):
                qb = qc // 2
                pq = PERM[qb]
                klist = []
                for kb in range(4):
                    if PERM[kb] < pq:
                        klist += [(kb * 8 + i, None) for i in range(8)]
                for i in range((qc % 2) * 4 + 4):
                    b_ = (qc % 2) * 512 - i * 128
                    klist.append((qb * 8 + i, b_ if b_ < 512 else None))
                nkt = len(klist)
                o_ps = C[:, (qc % 2) * 512:(qc % 2) * 512 + 512]
                wA_last = None
                wexp_prev = getattr(self, "_wexp_prev", {})
                prevE = getattr(self, "_prevE", {})
                groups = list(range(0, nkt, 2))
                pend = []

                def emit_pv(ent, _oev=oev_half, _qc=qc, _ops=o_ps, _nkt=nkt):
                    ki0_, npair_, E_, wE_, eslot_ = ent
                    for sub in range(npair_):
                        kt_ = klist[ki0_ + sub][0]
                        ki_ = ki0_ + sub
                        self.mm(_ops, vtm[:, kt_ * 128:(kt_ + 1) * 128],
                                E_[:, sub * 512:(sub + 1) * 512],
                                ki_ == 0, ki_ == _nkt - 1,
                                [wE_] + ([_oev[_qc % 2]] if (ki_ == 0 and _oev[_qc % 2]) else []),
                                ("MM", 1) if ki_ == _nkt - 1 else None, skip=True)
                    prevE[eslot_] = tuple(prevE.get(eslot_, ())) + (("MM", P.val("MM")),)

                for gi, ki0 in enumerate(groups):
                    if len(pend) >= 2:
                        emit_pv(pend.pop(0))
                    sb_ix = gi % 2
                    eslot = gi % 4
                    S2 = (A if sb_ix == 0 else Bp)
                    wS = ([wq1, wq2, wkv_done] if ki0 == 0 and qc == 0 else []) + \
                         list(wexp_prev.get(sb_ix, ()))
                    npair = 2 if ki0 + 1 < nkt else 1
                    for sub in range(npair):
                        kt = klist[ki0 + sub][0]
                        sl = S2[:, sub * 512:(sub + 1) * 512]
                        self.mm(sl, kp[:, kt * 128:(kt + 1) * 128],
                                qpb[hb][:, qc * 512:(qc + 1) * 512],
                                True, False, wS if sub == 0 else (), None)
                        wS = []
                        self.mm(sl, krotF[:, kt * 128:(kt + 1) * 128],
                                qrb[hb][:, qc * 512:(qc + 1) * 512],
                                False, True, (), ("MM", 1))
                    wsc = ("MM", P.val("MM"))
                    E = Eb[eslot]
                    wexp = P.emit("scalar", lambda s, e=E, p=S2, n=npair:
                                  s.activation(e[:, 0:n * 512], p[:, 0:n * 512],
                                               AF.Exp, bias=self.sb["cst"][:, 2:3]),
                                  [wsc, self.w_ones] + list(prevE.get(eslot, ())),
                                  ("AC", 1))
                    prevE[eslot] = ()
                    wmask = []
                    for sub in range(npair):
                        mb = klist[ki0 + sub][1]
                        if mb is not None:
                            wmask.append(P.emit(
                                "gpsimd", lambda g, e=E, s_=sub, b_=mb:
                                g.affine_select(out=e[:, s_ * 512:(s_ + 1) * 512],
                                                in_=e[:, s_ * 512:(s_ + 1) * 512],
                                                pattern=[[1, 512]],
                                                compare_op=ALU.is_ge, fill=0.0,
                                                base=b_, channel_multiplier=-1),
                                [wexp], ("PO", 1)))
                    wE = wmask[-1] if wmask else wexp
                    for sub in range(npair):
                        if ki0 == 0 and sub == 0:
                            wA_last = P.emit("vector", lambda v, e=E:
                                             v.tensor_copy(Aacc[:], e[:, 0:512]),
                                             [wE, ("MM", P.val("MM"))], ("DV", 1))
                        else:
                            wA_last = P.emit("vector", lambda v, e=E, s_=sub:
                                             v.tensor_add(Aacc[:], Aacc[:],
                                                          e[:, s_ * 512:(s_ + 1) * 512]),
                                             [wE] + ([wA_last] if wA_last else []),
                                             ("DV", 1))
                    prevE[eslot] = tuple(prevE.get(eslot, ())) + \
                        (wA_last, ("PO", P.val("PO")))
                    wexp_prev[sb_ix] = (wexp,)
                    pend.append((ki0, npair, E, wE, eslot))
                while pend:
                    emit_pv(pend.pop(0))
                self._prevE = prevE
                self._wexp_prev = wexp_prev
                wpv = ("MM", P.val("MM"))
                hD = (qc % 2)
                ws = self.mm(Dp[0:1, hD * 512: hD * 512 + 512],
                             self.sb["onescol_b"][:], Aacc[:], True, True,
                             [wA_last, ("DV", P.val("DV"))], ("MM", 1), skip=True)
                w1 = P.emit("vector", lambda v, d=Dp[0:1, hD * 512: hD * 512 + 512]:
                            v.tensor_copy(srow[:], d), [ws], ("DV", 1))
                w2 = P.emit("vector", lambda v: v.reciprocal(srow[:], srow[:]),
                            [w1], ("DV", 1))
                wb = self.mm(Dp[:, (1 - hD) * 512:(1 - hD) * 512 + 512],
                             self.sb["onesrow_f"][:], srow[:], True, True,
                             [w2], ("MM", 1), skip=True)
                w3 = P.emit("vector", lambda v, d=Dp[:, (1 - hD) * 512:(1 - hD) * 512 + 512],
                            b=bsb[hD]: v.tensor_copy(b[:], d), [wb], ("DV", 1))
                ow_prev = getattr(self, "_ow_prev", [None, None])
                w4 = P.emit("vector", lambda v, o=osb[hD], b=bsb[hD], p=o_ps:
                            v.tensor_mul(o[:], p, b[:]),
                            [w3, wpv] + ([ow_prev[hD]] if ow_prev[hD] else []),
                            ("DV", 1))
                oev_half[qc % 2] = w4
                ow_prev[hD] = self.dma(
                    "gpsimd", self.ds["O_o"][h][:, qc * 512:(qc + 1) * 512],
                    osb[hD][:], [w4], sem=f"xow{hD}")
                self._ow_prev = ow_prev
            wq_prev[hb] = ("MM", P.val("MM"))

    def layernorm(self, get_r, put_out, tcs=4):
        P, ar = self.P, self.ar
        C, Dp = self.psC, self.psD
        sqt = [ar.b(128, 512) for _ in range(2)]
        mrow = ar.f(1, 512)
        vrow = ar.f(1, 512)
        trow = ar.f(1, 512)
        wap_prev = None
        sqg = getattr(self, "_ln_sqg", [None, None])
        for tc in range(tcs):
            for oc in range(FC):
                r_ap, wr = get_r(oc, tc)
                self.mm(C[0:1, 0:512], self.sb["onescol_b"][:], r_ap,
                        oc == 0, oc == FC - 1,
                        [wr, self.w_ones] + ([wap_prev] if (oc == 0 and wap_prev) else []),
                        ("MM", 1), skip=True)
                wa = P.emit("scalar", lambda s, q=sqt[oc % 2], r=r_ap:
                            s.activation(q[:], r, AF.Square),
                            [wr] + ([sqg[oc % 2]] if sqg[oc % 2] else []), ("AC", 1))
                sqg[oc % 2] = self.mm(C[0:1, 512:1024], self.sb["onescol_b"][:],
                                      sqt[oc % 2][:], oc == 0, oc == FC - 1, [wa],
                                      ("MM", 1), skip=True)
                self._ln_sqg = sqg
            wst = ("MM", P.val("MM"))
            w1 = P.emit("vector", lambda v: v.tensor_scalar(
                mrow[:], C[0:1, 0:512], 1.0 / D, None, ALU.mult), [wst], ("DV", 1))
            w2 = P.emit("vector", lambda v: v.tensor_scalar(
                vrow[:], C[0:1, 512:1024], 1.0 / D, None, ALU.mult), [wst], ("DV", 1))
            w3 = P.emit("vector", lambda v: v.tensor_mul(trow[:], mrow[:], mrow[:]),
                        [w1], ("DV", 1))
            w4 = P.emit("vector", lambda v: v.tensor_tensor(
                vrow[:], vrow[:], trow[:], ALU.subtract), [w2, w3], ("DV", 1))
            w5 = P.emit("scalar", lambda s: s.activation(
                vrow[:], vrow[:], AF.Sqrt,
                bias=self.sb["cst"][0:1, 1:2]), [w4], ("AC", 1))
            w6 = P.emit("vector", lambda v: v.reciprocal(vrow[:], vrow[:]),
                        [w5], ("DV", 1))
            w7 = P.emit("vector", lambda v: v.tensor_mul(trow[:], mrow[:], vrow[:]),
                        [w6], ("DV", 1))
            w8 = P.emit("vector", lambda v: v.tensor_scalar(
                trow[:], trow[:], -1.0, None, ALU.mult), [w7], ("DV", 1))
            wba = self.mm(Dp[:, 0:512], self.sb["onesrow_f"][:], vrow[:],
                          True, True, [w6], ("MM", 1), skip=True)
            wbb = self.mm(Dp[:, 512:1024], self.sb["onesrow_f"][:], trow[:],
                          True, True, [w8], ("MM", 1), skip=True)
            wlast = None
            for oc in range(FC):
                wlast = put_out(oc, tc, Dp[:, 0:512], Dp[:, 512:1024], [wba, wbb])
            wap_prev = wlast

    def p3(self, wo="wo0", obuf_src="O_i", ev=EV_ATT0, bo=None):
        P, ar = self.P, self.ar
        ar.reset()
        A, Bp = self.psA, self.psB
        xs = self.sb["xs"]
        obuf = [ar.b(128, TOK) for _ in range(NH)]
        wob = [ar.b(128, FC * 128) for _ in range(2)]
        ttile = [ar.b(128, 512) for _ in range(2)]
        wg = []
        if self.fused:
            wg = [self.w_ogather0 if ev == EV_ATT0 else self.w_ogather1]
        for gh in range(NH):
            self.dma("sync", obuf[gh], self.ds[obuf_src][gh], wg, sem="obr")
        wo_all = [("obr", P.val("obr"))]
        rdone = {}
        ev_half = {}
        for tc in range(4):
            for oc in range(FC):
                ps = (A if oc % 2 == 0 else Bp)[:, (tc % 2) * 512:(tc % 2) * 512 + 512]
                slot = (oc % 2, tc % 2)
                wld = self.dma("scalar" if oc % 2 else "sync",
                               wob[oc % 2].rearrange("p (c m) -> p c m", c=FC),
                               self.ds["b_" + wo][:, oc * 128:(oc + 1) * 128]
                               .rearrange("(c p) m -> p c m", p=128),
                               [self.wcast[wo]] +
                               ([("MM", P.val("MM"))] if (tc > 0 or oc >= 2) else []),
                               sem=f"p3w{oc % 2}")
                w0 = [wld] + wo_all + ([ev_half[slot]] if ev_half.get(slot) else [])
                for hc in range(NH):
                    self.mm(ps, wob[oc % 2][:, hc * 128:(hc + 1) * 128],
                            obuf[hc][:, tc * 512:(tc + 1) * 512],
                            hc == 0, hc == NH - 1, w0 if hc == 0 else (),
                            ("MM", 1) if hc == NH - 1 else None)
                wmm = ("MM", P.val("MM"))
                xsl = xs[:, oc * TOK + tc * 512: oc * TOK + tc * 512 + 512]
                if bo is None:
                    wv = P.emit("vector", lambda v, p=ps, x=xsl:
                                v.tensor_add(x, p, x), [wmm], ("DV", 1))
                else:
                    wv = P.emit("vector", lambda v, p=ps, x=xsl, oc_=oc:
                                v.scalar_tensor_tensor(
                                    x, p, self.sb[bo][:, oc_:oc_ + 1], x,
                                    ALU.add, ALU.add), [wmm], ("DV", 1))
                ev_half[slot] = wv
                rdone[(oc, tc)] = wv

        def get_r(oc, tc):
            return (xs[:, oc * TOK + tc * 512: oc * TOK + tc * 512 + 512],
                    rdone[(oc, tc)])

        def put_out(oc, tc, Aap, Bap, waits):
            xsl = xs[:, oc * TOK + tc * 512: oc * TOK + tc * 512 + 512]
            w1 = P.emit("vector", lambda v, x=xsl, a=Aap, t=ttile[oc % 2]:
                        v.tensor_mul(t[:], x, a), waits, ("DV", 1))
            return P.emit("vector", lambda v, x=xsl, b=Bap, t=ttile[oc % 2]:
                          v.tensor_add(x, t[:], b), [w1], ("DV", 1))
        self.layernorm(get_r, put_out)


class Full3(Full2):
    def p4(self):
        P, ar = self.P, self.ar
        ar.reset()
        A, Bp = self.psA, self.psB
        xs = self.sb["xs"]
        x8 = [ar.e8(128, 512) for _ in range(FC)]
        a8 = [ar.e8(128, 512) for _ in range(64)]
        wg0b = [ar.e8(128, FC * 128) for _ in range(2)]
        wg1b = [ar.e8(128, FC * 128) for _ in range(2)]
        wlb = [ar.e8(128, 64 * 128) for _ in range(2)]
        gt = [ar.b(128, 512) for _ in range(2)]
        ft = [ar.b(128, 512) for _ in range(2)]
        st = [ar.b(128, 512) for _ in range(2)]
        g3t = [ar.b(128, 512) for _ in range(2)]
        wt = [ar.b(128, 512) for _ in range(2)]
        tt_ = [ar.b(128, 512) for _ in range(2)]
        mt_ = [ar.b(128, 512) for _ in range(2)]
        rdone = {}
        wg_prev = {}
        wsc = [self.w_scr]
        for tt in range(4):
            wx8 = []
            for ic in range(FC):
                wx8.append(P.emit("scalar", lambda s, i=ic, t=tt:
                                  s.activation(x8[i][:], xs[:, i * TOK + t * 512:
                                                            i * TOK + t * 512 + 512],
                                               AF.Copy),
                                  [rdone[(ic, tt - 1)]] if tt > 0 else (), ("AC", 1)))
            wx8a = ("AC", P.val("AC"))
            for hc in range(64):
                hb = hc % 2
                wl0 = self.dma("sync", wg0b[hb].rearrange("p (c m) -> p c m", c=FC),
                               self.ds["w8g0"][:, hc * 128:(hc + 1) * 128]
                               .rearrange("(c p) m -> p c m", p=128),
                               wsc + ([wg_prev[hb]] if hb in wg_prev else []),
                               sem=f"p4g{hb}")
                wl1 = self.dma("scalar", wg1b[hb].rearrange("p (c m) -> p c m", c=FC),
                               self.ds["w8g1"][:, hc * 128:(hc + 1) * 128]
                               .rearrange("(c p) m -> p c m", p=128),
                               wsc + ([wg_prev[hb]] if hb in wg_prev else []),
                               sem=f"p4h{hb}")
                psg = A[:, hb * 512: hb * 512 + 512]
                psf = Bp[:, hb * 512: hb * 512 + 512]
                wevg = getattr(self, "_p4evg", {})
                w0 = [wl0, wl1, wx8a] + list(wevg.get(hb, ()))
                for ic in range(FC):
                    self.mm(psg, wg0b[hb][:, ic * 128:(ic + 1) * 128], x8[ic][:],
                            ic == 0, ic == FC - 1, w0 if ic == 0 else (),
                            ("MM", 1) if ic == FC - 1 else None, skip=True)
                for ic in range(FC):
                    self.mm(psf, wg1b[hb][:, ic * 128:(ic + 1) * 128], x8[ic][:],
                            ic == 0, ic == FC - 1, (),
                            ("MM", 1) if ic == FC - 1 else None, skip=True)
                wg_prev[hb] = ("MM", P.val("MM"))
                wmm = ("MM", P.val("MM"))
                w1 = P.emit("scalar", lambda s, g=gt[hb], p=psg:
                            s.activation(g[:], p, AF.Copy), [wmm], ("AC", 1))
                w2 = P.emit("vector", lambda v, f=ft[hb], p=psf:
                            v.tensor_copy(f[:], p), [wmm], ("DV", 1))
                self._p4evg = {**getattr(self, "_p4evg", {}), hb: (w1, w2)}
                w3 = P.emit("scalar", lambda s, q=st[hb], g=gt[hb]:
                            s.activation(q[:], g[:], AF.Square), [w1], ("AC", 1))
                w4 = P.emit("gpsimd", lambda g_, a=g3t[hb], g=gt[hb], q=st[hb]:
                            g_.tensor_mul(a[:], g[:], q[:]), [w3], ("PO", 1))
                w5 = P.emit("vector", lambda v, w=wt[hb], a=g3t[hb], g=gt[hb]:
                            v.scalar_tensor_tensor(w[:], a[:], GC2, g[:],
                                                   ALU.mult, ALU.add), [w4], ("DV", 1))
                w6 = P.emit("scalar", lambda s, t_=tt_[hb], w=wt[hb]:
                            s.activation(t_[:], w[:], AF.Tanh, scale=GC1),
                            [w5], ("AC", 1))
                w7 = P.emit("gpsimd", lambda g_, m=mt_[hb], g=gt[hb], f=ft[hb]:
                            g_.tensor_mul(m[:], g[:], f[:]), [w2], ("PO", 1))
                w8 = P.emit("vector", lambda v, o=a8[hc], t_=tt_[hb], m=mt_[hb]:
                            v.scalar_tensor_tensor(o[:], t_[:], 1.0, m[:],
                                                   ALU.add, ALU.mult),
                            [w6, w7], ("DV", 1))
                rdone[("a8", hc)] = w8
            wa8 = ("DV", P.val("DV"))
            wdn_prev = {}
            for oc in range(FC):
                hb = oc % 2
                wl = self.dma("scalar" if hb else "sync",
                              wlb[hb].rearrange("p (c m) -> p c m", c=64),
                              self.ds["w8lin"][:, oc * 128:(oc + 1) * 128]
                              .rearrange("(c p) m -> p c m", p=128),
                              wsc + ([wdn_prev[hb]] if hb in wdn_prev else []),
                              sem=f"p4l{hb}")
                ps = (A if hb == 0 else Bp)[:, 0:512]
                w0 = [wl, wa8] + ([rdone[("dn", oc - 2)]] if oc >= 2 else [])
                for hc in range(64):
                    self.mm(ps, wlb[hb][:, hc * 128:(hc + 1) * 128], a8[hc][:],
                            hc == 0, hc == 63, w0 if hc == 0 else (),
                            ("MM", 1) if hc == 63 else None, skip=True)
                wmm = ("MM", P.val("MM"))
                wdn_prev[hb] = wmm
                xsl = xs[:, oc * TOK + tt * 512: oc * TOK + tt * 512 + 512]
                wv = P.emit("vector", lambda v, p=ps, x=xsl:
                            v.scalar_tensor_tensor(x, p, 0.5, x, ALU.mult, ALU.add),
                            [wmm], ("DV", 1))
                rdone[("dn", oc)] = wv
                rdone[(oc, tt)] = wv

        def get_r(oc, tc):
            return (xs[:, oc * TOK + tc * 512: oc * TOK + tc * 512 + 512],
                    rdone[(oc, tc)])

        ttile = [ar.b(128, 512) for _ in range(2)]

        def put_out(oc, tc, Aap, Bap, waits):
            xsl = xs[:, oc * TOK + tc * 512: oc * TOK + tc * 512 + 512]
            w1 = P.emit("vector", lambda v, x=xsl, a=Aap, t=ttile[oc % 2]:
                        v.tensor_mul(t[:], x, a), waits, ("DV", 1))
            return P.emit("vector", lambda v, x=xsl, b=Bap, t=ttile[oc % 2]:
                          v.tensor_add(x, t[:], b), [w1], ("DV", 1))
        self.layernorm(get_r, put_out)

    def p5(self):
        P = self.P
        xs = self.sb["xs"]
        wd = ("DV", P.val("DV"))
        for fcc in range(FC):
            self.dma("gpsimd", self.ds["X2_o"][fcc],
                     xs[:, fcc * TOK:(fcc + 1) * TOK], [wd], sem="xxw")

    def p6(self):
        P, ar = self.P, self.ar
        ar.reset()
        self._prevE = {}
        self._wexp_prev = {}
        A, Bp, C, Dp = self.psA, self.psB, self.psC, self.psD
        wq1h = ar.b(128, FC * 512)
        wk1b = ar.b(128, FC * 256)
        wv1b = ar.b(128, FC * 256)
        xf = [[ar.b(128, 512) for _ in range(8)] for _ in range(2)]
        k1 = [ar.b(128, T) for _ in range(2)]
        v1 = ar.b(128, 32 * 256)
        qsb = [ar.b(128, 512) for _ in range(4)]
        qat = [ar.b(128, T) for _ in range(2)]
        Eb = [ar.b(128, 1024) for _ in range(4)]
        Aacc = ar.b(128, 512)
        osb = [ar.b(128, 512) for _ in range(2)]
        bsb = [ar.f(128, 512) for _ in range(2)]
        srow = ar.f(1, 512)
        wp = [self.w_poll_x2] if self.fused else []
        self.dma("sync", wk1b.rearrange("p (c m) -> p c m", c=FC),
                 self.ds["b_wk1"].rearrange("(c p) m -> p c m", p=128),
                 [self.wcast["wk1"]], sem="p6w")
        self.dma("sync", wv1b.rearrange("p (c m) -> p c m", c=FC),
                 self.ds["b_wv1"].rearrange("(c p) m -> p c m", p=128),
                 [self.wcast["wv1"]], sem="p6w")
        xsrc = self.ds["X2_i"]
        wxf_prev = [None, None]
        for pas in range(3):
            if pas < 2:
                wwq = self.dma(
                    "sync", wq1h.rearrange("p (c m) -> p c m", c=FC),
                    self.ds["b_wq1"][:, pas * 512:(pas + 1) * 512]
                    .rearrange("(c p) m -> p c m", p=128),
                    [self.wcast["wq1"]] +
                    ([("MM", P.val("MM"))] if pas > 0 else []), sem="p6q")
            ww = ("p6w", P.val("p6w"))
            for qc in range(8):
                xb = qc % 2
                pss = [A[:, 0:512], A[:, 512:1024], Bp[:, 0:512], Bp[:, 512:1024]]
                wevq = getattr(self, "_p6evq", None)
                wevk = getattr(self, "_p6evk", None)
                for hf in range(2):
                    wld = []
                    for f8_ in range(8):
                        fcc = hf * 8 + f8_
                        wld.append(self.dma(
                            "sync", xf[xb][f8_][:],
                            xsrc[fcc][:, qc * 512:(qc + 1) * 512],
                            wp + ([wxf_prev[xb]] if wxf_prev[xb] else []),
                            sem=f"p6x{xb}"))
                    wldl = (f"p6x{xb}", P.val(f"p6x{xb}"))
                    if pas < 2:
                        for f8_ in range(8):
                            fcc = hf * 8 + f8_
                            for g in range(4):
                                wxf_prev[xb] = self.mm(
                                    pss[g],
                                    wq1h[:, fcc * 512 + g * 128: fcc * 512 + g * 128 + 128],
                                    xf[xb][f8_][:], fcc == 0, fcc == FC - 1,
                                    ([wldl, ww, wwq] +
                                     (list(wevq) if (wevq and fcc == 0) else []))
                                    if f8_ == 0 else (),
                                    ("MM", 1) if (fcc == FC - 1 or f8_ == 7) else None,
                                    skip=True)
                    else:
                        for f8_ in range(8):
                            fcc = hf * 8 + f8_
                            for kvh in range(2):
                                self.mm(Dp[:, kvh * 512: kvh * 512 + 512],
                                        wk1b[:, fcc * 256 + kvh * 128: fcc * 256 + kvh * 128 + 128],
                                        xf[xb][f8_][:], fcc == 0, fcc == FC - 1,
                                        ([wldl, ww] + (list(wevk) if (wevk and fcc == 0) else []))
                                        if f8_ == 0 else (),
                                        ("MM", 1) if fcc == FC - 1 else None, skip=True)
                            for s in range(4):
                                wxf_prev[xb] = self.mm(
                                    C[:, s * 256:(s + 1) * 256],
                                    xf[xb][f8_][:, s * 128:(s + 1) * 128],
                                    wv1b[:, fcc * 256:(fcc + 1) * 256],
                                    fcc == 0, False, (),
                                    ("MM", 1) if (f8_ == 7 and s == 3) else None,
                                    skip=True)
                if pas < 2:
                    wmm = ("MM", P.val("MM"))
                    wevq = []
                    stq = getattr(self, "_p6stq", [None] * 4)
                    for g in range(4):
                        h = pas * 4 + g
                        wq_ = P.emit("vector", lambda v, g_=g, p=pss[g], h_=h:
                                     v.tensor_scalar(qsb[g_][:], p,
                                                     self.sb["bq1_sb"][:, h_:h_ + 1],
                                                     SC1, ALU.add, ALU.mult),
                                     [wmm] + ([stq[g]] if stq[g] else []), ("DV", 1))
                        wevq.append(wq_)
                        stq[g] = self.dma("sync",
                                          self.ds["Qs"][h][:, qc * 512:(qc + 1) * 512],
                                          qsb[g][:], [wq_], sem=f"qsw{g}")
                    self._p6stq = stq
                    self._p6evq = tuple(wevq)
                else:
                    for s in range(4):
                        self.mm(C[:, s * 256:(s + 1) * 256],
                                self.sb["onesrow_b"][:], self.sb["bv1_sb"][:],
                                False, True, [self.w_const[0]],
                                ("MM", 1) if s == 3 else None, skip=True)
                    wmm = ("MM", P.val("MM"))
                    wevk = []
                    for kvh in range(2):
                        wk_ = P.emit("vector", lambda v, k=kvh, p=Dp[:, kvh * 512: kvh * 512 + 512], q=qc:
                                     v.tensor_scalar(k1[k][:, q * 512:(q + 1) * 512], p,
                                                     self.sb["bk1_sb"][:, k:k + 1],
                                                     None, ALU.add),
                                     [wmm], ("DV", 1))
                        wevk.append(wk_)
                    for s in range(4):
                        tok = qc * 4 + s
                        wv_ = P.emit("vector", lambda v, t=tok, s_=s:
                                     v.tensor_copy(v1[:, t * 256:(t + 1) * 256],
                                                   C[:, s_ * 256:(s_ + 1) * 256]),
                                     [wmm], ("DV", 1))
                        wevk.append(wv_)
                    self._p6evk = tuple(wevk)
                wproj = ("DV", P.val("DV"))
        wq_prev = [None, None]
        oev_half = [None, None]
        for kvh in range(2):
            for g in range(4):
                h = kvh * 4 + g
                hb = h % 2
                wqld = self.dma("sync", qat[hb], self.ds["Qs"][h],
                                [(f"qsw{g_}", P.val(f"qsw{g_}")) for g_ in range(4)] +
                                ([wq_prev[hb]] if wq_prev[hb] else []),
                                sem=f"qal{hb}")
                for qc in range(8):
                    o_ps = C[:, (qc % 2) * 512:(qc % 2) * 512 + 512]
                    wA_last = None
                    prevE = self._prevE
                    wexp_prev = self._wexp_prev
                    pend = []

                    def emit_pv(ent, _oev=oev_half, _qc=qc, _ops=o_ps, _kvh=kvh):
                        kt0_, E_, wE_, eslot_ = ent
                        for sub in range(2):
                            kt_ = kt0_ + sub
                            self.mm(_ops, v1[:, kt_ * 256 + _kvh * 128: kt_ * 256 + _kvh * 128 + 128],
                                    E_[:, sub * 512:(sub + 1) * 512],
                                    kt_ == 0, kt_ == 31,
                                    [wE_] + ([_oev[_qc % 2]] if (kt_ == 0 and _oev[_qc % 2]) else []),
                                    ("MM", 1) if kt_ == 31 else None, skip=True)
                        prevE[eslot_] = tuple(prevE.get(eslot_, ())) + \
                            (("MM", P.val("MM")),)

                    for gi, kt2 in enumerate(range(0, 32, 2)):
                        if len(pend) >= 2:
                            emit_pv(pend.pop(0))
                        sb_ix = gi % 2
                        eslot = gi % 4
                        S2 = (A if sb_ix == 0 else Bp)
                        wS = ([wqld, wproj] if kt2 == 0 and qc == 0 and h == 0 else
                              ([wqld] if kt2 == 0 and qc == 0 else [])) + \
                             list(wexp_prev.get(sb_ix, ()))
                        for sub in range(2):
                            kt = kt2 + sub
                            self.mm(S2[:, sub * 512:(sub + 1) * 512],
                                    k1[kvh][:, kt * 128:(kt + 1) * 128],
                                    qat[hb][:, qc * 512:(qc + 1) * 512],
                                    True, True, wS if sub == 0 else (),
                                    ("MM", 1), skip=True)
                            wS = []
                        wsc = ("MM", P.val("MM"))
                        E = Eb[eslot]
                        wexp = P.emit("scalar", lambda s, e=E, p=S2:
                                      s.activation(e[:], p[:], AF.Exp,
                                                   bias=self.sb["cst"][:, 2:3]),
                                      [wsc, self.w_ones] + list(prevE.get(eslot, ())),
                                      ("AC", 1))
                        prevE[eslot] = ()
                        for sub in range(2):
                            if kt2 == 0 and sub == 0:
                                wA_last = P.emit("vector", lambda v, e=E:
                                                 v.tensor_copy(Aacc[:], e[:, 0:512]),
                                                 [wexp, ("MM", P.val("MM"))], ("DV", 1))
                            else:
                                wA_last = P.emit("vector", lambda v, e=E, s_=sub:
                                                 v.tensor_add(Aacc[:], Aacc[:],
                                                              e[:, s_ * 512:(s_ + 1) * 512]),
                                                 [wexp] + ([wA_last] if wA_last else []),
                                                 ("DV", 1))
                        prevE[eslot] = tuple(prevE.get(eslot, ())) + \
                            (wA_last, ("PO", P.val("PO")))
                        wexp_prev[sb_ix] = (wexp,)
                        pend.append((kt2, E, wexp, eslot))
                    while pend:
                        emit_pv(pend.pop(0))
                    wpv = ("MM", P.val("MM"))
                    hD = qc % 2
                    ws = self.mm(Dp[0:1, hD * 512: hD * 512 + 512],
                                 self.sb["onescol_b"][:], Aacc[:], True, True,
                                 [wA_last, ("DV", P.val("DV"))], ("MM", 1), skip=True)
                    w1 = P.emit("vector", lambda v, d=Dp[0:1, hD * 512: hD * 512 + 512]:
                                v.tensor_copy(srow[:], d), [ws], ("DV", 1))
                    w2 = P.emit("vector", lambda v: v.reciprocal(srow[:], srow[:]),
                                [w1], ("DV", 1))
                    wb = self.mm(Dp[:, (1 - hD) * 512:(1 - hD) * 512 + 512],
                                 self.sb["onesrow_f"][:], srow[:], True, True,
                                 [w2], ("MM", 1), skip=True)
                    w3 = P.emit("vector", lambda v, d=Dp[:, (1 - hD) * 512:(1 - hD) * 512 + 512],
                                b=bsb[hD]: v.tensor_copy(b[:], d), [wb], ("DV", 1))
                    ow_prev = getattr(self, "_ow1_prev", [None, None])
                    w4 = P.emit("vector", lambda v, o=osb[hD], b=bsb[hD], p=o_ps:
                                v.tensor_mul(o[:], p, b[:]),
                                [w3, wpv] + ([ow_prev[hD]] if ow_prev[hD] else []),
                                ("DV", 1))
                    oev_half[qc % 2] = w4
                    ow_prev[hD] = self.dma(
                        "gpsimd", self.ds["O1_o"][h][:, qc * 512:(qc + 1) * 512],
                        osb[hD][:], [w4], sem=f"xo1{hD}")
                    self._ow1_prev = ow_prev
                wq_prev[hb] = ("MM", P.val("MM"))

    def p7(self):
        self.p3(wo="wo1", obuf_src="O1_i", ev=EV_ATT1, bo="bo1_sb")

    def p8(self):
        P, ar = self.P, self.ar
        ar.reset()
        A, Bp = self.psA, self.psB
        xs = self.sb["xs"]
        w1b = [ar.b(128, FC * 128) for _ in range(2)]
        rbuf = [ar.b(128, 512) for _ in range(FC)]
        rdone = {}
        ev_half = {}
        for tc in range(4):
            for oc in range(FC):
                hb = oc % 2
                slot = (hb, tc % 2)
                wld = self.dma("scalar" if hb else "sync",
                               w1b[hb].rearrange("p (c m) -> p c m", c=FC),
                               self.ds["b_w1t"][:, oc * 128:(oc + 1) * 128]
                               .rearrange("(c p) m -> p c m", p=128),
                               [self.wcast["w1t"]] +
                               ([("MM", P.val("MM"))] if (tc > 0 or oc >= 2) else []),
                               sem=f"p8w{hb}")
                ps = (A if hb == 0 else Bp)[:, (tc % 2) * 512:(tc % 2) * 512 + 512]
                w0 = [wld] + ([ev_half[slot]] if ev_half.get(slot) else [])
                for ic in range(FC):
                    self.mm(ps, w1b[hb][:, ic * 128:(ic + 1) * 128],
                            xs[:, ic * TOK + tc * 512: ic * TOK + tc * 512 + 512],
                            ic == 0, ic == FC - 1, w0 if ic == 0 else (),
                            ("MM", 1) if ic == FC - 1 else None)
                wmm = ("MM", P.val("MM"))
                wv = P.emit("vector", lambda v, p=ps, oc_=oc,
                            x=xs[:, oc * TOK + tc * 512: oc * TOK + tc * 512 + 512],
                            r=rbuf[oc]:
                            v.scalar_tensor_tensor(r[:], p,
                                                   self.sb["fb1_sb"][:, oc_:oc_ + 1],
                                                   x, ALU.add, ALU.add),
                            [wmm], ("DV", 1))
                ev_half[slot] = wv
                rdone[(oc, tc)] = wv

            def get_r(oc, tc2, _tc=tc):
                return (rbuf[oc][:], rdone[(oc, _tc)])

            obf = [ar.f(128, 512) for _ in range(2)] if tc == 0 else self._p8obf
            self._p8obf = obf

            def put_out(oc, tc2, Aap, Bap, waits, _tc=tc):
                t = obf[oc % 2]
                wprev = getattr(self, "_p8ost", {}).get(oc % 2)
                w1 = P.emit("vector", lambda v, r=rbuf[oc], a=Aap, t_=t:
                            v.tensor_mul(t_[:], r[:], a), list(waits) +
                            ([wprev] if wprev else []), ("DV", 1))
                w2 = P.emit("vector", lambda v, b=Bap, t_=t:
                            v.tensor_add(t_[:], t_[:], b), [w1], ("DV", 1))
                wst = self.dma("sync", self.ds["out"][oc][:, _tc * 512:(_tc + 1) * 512],
                               t[:], [w2], sem=f"out{oc % 2}")
                self._p8ost = {**getattr(self, "_p8ost", {}), oc % 2: wst}
                return w2
            self.layernorm(get_r, put_out, tcs=1)



ALLPH = ["P1", "P2", "P3", "P4", "P5", "P6", "P7", "P8"]
LAUNCHES = [["P1"], ["P2"], ["P3", "P4", "P5"], ["P6"], ["P7", "P8"]]
FUSED = True


def _run(nc, in_maps):
    return run_bass_kernel_spmd(nc, in_maps, core_ids=list(range(NCORE)))


def _pick(maps, names):
    return [{k: m[k] for k in names} for m in maps]


def _assemble_full(per_core, heads_dim, rows, posmap=None):
    out = []
    for b in range(4):
        out.append(np.concatenate([per_core[2 * b], per_core[2 * b + 1]],
                                  axis=-1))
    return out


def kernel(**inputs):
    maps = host_inputs(inputs)
    posmap = [(STRIPE[j][0] * 1024, STRIPE[j][1] * 1024) for j in range(2)]
    x = np.asarray(inputs["L0_attn_hidden_states"], np.float32)

    if FUSED:
        bld = Full3(ALLPH, True)
        nc = bld.build()
        res = _run(nc, [{k: m[k] for k in bld.innames} for m in maps])
        outs = [r["out"] for r in res.results]
    else:
        needed = None
        state = [dict(m) for m in maps]
        for li, ph in enumerate(LAUNCHES):
            bld = Full3(ph, False)
            nc = bld.build()
            inm = [{k: state[c][k] for k in bld.innames} for c in range(NCORE)]
            res = _run(nc, inm)
            for c in range(NCORE):
                for k, v in res.results[c].items():
                    state[c][k] = v
            if ph[-1] == "P1":
                qp = _assemble_full([state[c]["Qp_o"] for c in range(NCORE)],
                                    NH, 128, posmap)
                qr = _assemble_full([state[c]["Qr_o"] for c in range(NCORE)],
                                    NH, 64, posmap)
                lkv = _assemble_full([state[c]["Lkv_o"] for c in range(NCORE)],
                                     4, 128, posmap)
                rt = _assemble_full([state[c]["Rt_o"][None] for c in range(NCORE)],
                                    1, 64, posmap)
                rs = _assemble_full([state[c]["Rstd_o"][None] for c in range(NCORE)],
                                    1, 1, posmap)
                for c in range(NCORE):
                    b, j = divmod(c, 2)
                    hsl = slice(j * MYH, (j + 1) * MYH)
                    state[c]["Qp_i"] = np.ascontiguousarray(qp[b][hsl])
                    state[c]["Qr_i"] = np.ascontiguousarray(qr[b][hsl])
                    state[c]["Lkv_i"] = lkv[b]
                    state[c]["Rt_i"] = rt[b][0]
                    state[c]["Rstd_i"] = rs[b][0]
            if ph[-1] == "P2":
                for b in range(4):
                    ob = np.concatenate([state[2 * b]["O_o"],
                                         state[2 * b + 1]["O_o"]], axis=0)
                    for j in range(2):
                        state[2 * b + j]["O_i"] = np.ascontiguousarray(
                            ob[:, :, j * TOK:(j + 1) * TOK])
            if ph[-1] == "P5":
                x2 = _assemble_full([state[c]["X2_o"] for c in range(NCORE)],
                                    FC, 128, posmap)
                for c in range(NCORE):
                    state[c]["X2_i"] = x2[c // 2]
            if ph[-1] == "P6":
                for b in range(4):
                    ob = np.concatenate([state[2 * b]["O1_o"],
                                         state[2 * b + 1]["O1_o"]], axis=0)
                    for j in range(2):
                        state[2 * b + j]["O1_i"] = np.ascontiguousarray(
                            ob[:, :, j * TOK:(j + 1) * TOK])
            if "stream_out" in (res.results[0] or {}):
                for c in range(NCORE):
                    state[c]["stream_in"] = state[c]["stream_out"]
        outs = [state[c]["out"] for c in range(NCORE)]

    y = np.empty((Bz, T, D), np.float32)
    for c in range(NCORE):
        b, j = divmod(c, 2)
        pA, pB = posmap[j]
        fm = outs[c].reshape(D, TOK)
        y[b, pA:pA + 1024] = fm[:, 0:1024].T
        y[b, pB:pB + 1024] = fm[:, 1024:2048].T
    return y


# version 06
# speedup vs baseline: 1.1525x; 1.1525x over previous
import numpy as np
import contextlib
import secrets
from collections import defaultdict
import ml_dtypes
import concourse.bass as bass
import concourse.mybir as mybir
from concourse.bass_utils import run_bass_kernel_spmd

F32 = mybir.dt.float32
BF16 = mybir.dt.bfloat16
FP8 = mybir.dt.float8e4
I32 = mybir.dt.int32
AF = mybir.ActivationFunctionType
ALU = mybir.AluOpType
AX = mybir.AxisListType.X

NCORE = 8
Bz, T, D = 4, 4096, 2048
FC = 16
TOK = 2048
NH, MYH = 16, 8
DN, DR, DQK, DV = 128, 64, 192, 128
QL, KVL = 1536, 512
HID = 8192
STRIPE = [(0, 3), (1, 2)]
SHIFT = 12.0
SC0 = 1.0 / float(np.sqrt(192.0))
SC1 = 1.0 / float(np.sqrt(128.0))
GC1 = 0.7978845608028654
GC2 = 0.044715
EPSR = 1e-6
EPSL = 1e-5

ENGS = ("tensor", "vector", "scalar", "gpsimd", "sync")
QSEM = {"gpsimd": "DG", "sync": "DS", "scalar": "DA"}


class Prog:

    def __init__(self):
        self.ops = {e: [] for e in ENGS}
        self.tot = defaultdict(int)
        self.semnames = set(["MM", "DV", "AC", "PO", "PL"])

    def emit(self, eng, fn, waits=(), inc=None):
        for s, _ in waits:
            self.semnames.add(s)
        self.ops[eng].append((fn, tuple(waits), inc))
        if inc is not None:
            self.semnames.add(inc[0])
            self.tot[inc[0]] += inc[1]
        return (inc[0], self.tot[inc[0]]) if inc is not None else None

    def val(self, sem):
        return self.tot[sem]

    def barrier_waits(self):
        return tuple((s, self.tot[s]) for s in sorted(self.tot)
                     if self.tot[s] > 0 and s != "PL")

    def barrier(self, name=""):
        w = self.barrier_waits()
        for e in ENGS:
            self.emit(e, lambda en: en.nop(), w)


def _rope_tables():
    inv = 1.0 / (10000.0 ** (np.arange(0, DR, 2, dtype=np.float64) / DR))
    t = np.arange(T, dtype=np.float64)
    fr = np.outer(inv, t)
    return np.cos(fr), np.sin(fr)


def _vec_tile(v, cols):
    return np.ascontiguousarray(v.reshape(cols, 128).T).astype(np.float32)


def host_inputs(inputs):
    x = np.asarray(inputs["L0_attn_hidden_states"], np.float32)
    cos, sin = _rope_tables()
    nonce = secrets.randbits(20) + 1
    f32 = lambda a: np.ascontiguousarray(a, dtype=np.float32)
    com = {
        "wqa": f32(inputs["L0_attn_Wq_a"]),
        "wkva": f32(inputs["L0_attn_Wkv_a"]),
        "wkr": f32(inputs["L0_attn_Wk_rope"]),
        "wo0": f32(inputs["L0_attn_Wo"]),
        "wg0": f32(inputs["L0_ffn_w_gating"][0]),
        "wg1": f32(inputs["L0_ffn_w_gating"][1]),
        "wlin": f32(inputs["L0_ffn_w_linear"]),
        "wo1": f32(inputs["L1_attn_Wo"]),
        "w1t": f32(np.asarray(inputs["L1_ffn_weight"]).T),
        "cosq": f32(cos * SC0), "sinq": f32(sin * SC0),
        "cosk": f32(cos), "sink": f32(sin),
        "qnw": _vec_tile(f32(inputs["L0_attn_q_norm_w"]), 12),
        "kvnw": _vec_tile(f32(inputs["L0_attn_kv_norm_w"]), 4),
        "bo1": _vec_tile(f32(inputs["L1_attn_bo"]), 16),
        "fb1": _vec_tile(f32(inputs["L1_ffn_bias"]), 16),
    }
    wqb = f32(inputs["L0_attn_Wq_b"])
    wkvb = f32(inputs["L0_attn_Wkv_b"])
    wq1 = f32(inputs["L1_attn_Wq"])
    wk1 = f32(inputs["L1_attn_Wk"])
    wv1 = f32(inputs["L1_attn_Wv"])
    bq1 = f32(inputs["L1_attn_bq"])
    bk1 = f32(inputs["L1_attn_bk"])
    bv1 = f32(inputs["L1_attn_bv"])
    maps = []
    for c in range(NCORE):
        b, j = divmod(c, 2)
        sA, sB = STRIPE[j]
        posA, posB = sA * 1024, sB * 1024
        xc = np.concatenate([x[b, posA:posA + 1024], x[b, posB:posB + 1024]], 0)
        hs = slice(j * MYH, j * MYH + MYH)
        m = dict(com)
        m["x_fm"] = np.ascontiguousarray(xc.T)
        m["wqb"] = wqb
        m["wkvb"] = np.ascontiguousarray(
            wkvb.reshape(KVL, NH, DN + DV)[:, hs].reshape(KVL, MYH * (DN + DV)))
        m["tabq"] = np.ascontiguousarray(np.concatenate(
            [com["cosq"][:, posA:posA + 1024], com["cosq"][:, posB:posB + 1024],
             com["sinq"][:, posA:posA + 1024], com["sinq"][:, posB:posB + 1024]],
            axis=1).reshape(32, 2, 2048).transpose(1, 0, 2).reshape(64, 2048))
        m["tabk"] = np.ascontiguousarray(np.concatenate(
            [com["cosk"][:, posA:posA + 1024], com["cosk"][:, posB:posB + 1024],
             com["sink"][:, posA:posA + 1024], com["sink"][:, posB:posB + 1024]],
            axis=1).reshape(32, 2, 2048).transpose(1, 0, 2).reshape(64, 2048))
        m["wq1"] = np.ascontiguousarray(
            wq1.reshape(D, NH, DV)[:, hs].reshape(D, MYH * DV))
        m["wk1"] = np.ascontiguousarray(wk1[:, j * 256:(j + 1) * 256])
        m["wv1"] = np.ascontiguousarray(wv1[:, j * 256:(j + 1) * 256])
        m["bq1"] = _vec_tile(bq1.reshape(NH, DV)[hs].reshape(-1), 8)
        m["bk1"] = _vec_tile(bk1[j * 256:(j + 1) * 256], 2)
        m["bv1"] = np.ascontiguousarray(
            bv1[j * 256:(j + 1) * 256].reshape(1, 256)).astype(np.float32)
        for k in ("cosq", "sinq", "cosk", "sink"):
            m.pop(k)
        meta = np.zeros((1, 16), np.int32)
        meta[0, 0] = posA
        meta[0, 1] = posB
        meta[0, 2] = j * 8
        meta[0, 3] = (1 - j) * 8
        meta[0, 4] = nonce
        meta[0, 5] = j * MYH * SQPs
        meta[0, 6] = j
        m["meta"] = meta
        m["flagv"] = (nonce + np.arange(1, 9, dtype=np.int64)).astype(np.int32).reshape(1, 8)
        maps.append(m)
    return maps



SQPs = 128 * 4096
SQRs = 64 * 4096
PERM = [STRIPE[0][0], STRIPE[0][1], STRIPE[1][0], STRIPE[1][1]]


class Builder:

    def __init__(self, phases, fused):
        self.ph = set(phases)
        self.fused = fused
        self.nc = bass.Bass("TRN2", target_bir_lowering=False)
        self.P = Prog()
        self.es = contextlib.ExitStack()
        self.sb = {}
        self.ds = {}
        self.first = phases[0]
        self.last = phases[-1]
        self.innames = set()
        self.semq = {}

    def din(self, name, shape, dtype=F32):
        self.ds[name] = self.nc.dram_tensor(name, shape, dtype, kind="ExternalInput").ap()
        self.innames.add(name)
        return self.ds[name]

    def dout(self, name, shape, dtype=F32):
        self.ds[name] = self.nc.dram_tensor(name, shape, dtype, kind="ExternalOutput").ap()
        return self.ds[name]

    def dshared(self, name, shape, dtype):
        self.ds[name] = self.nc.dram_tensor(name, shape, dtype, addr_space="Shared").ap()
        return self.ds[name]

    def dlocal(self, name, shape, dtype):
        self.ds[name] = self.nc.dram_tensor(name, shape, dtype).ap()
        return self.ds[name]

    def S(self, name, shape, dtype):
        t = self.es.enter_context(self.nc.sbuf_tensor(name, shape, dtype))
        self.sb[name] = t
        return t

    def mm(self, out_ap, lhsT, rhs, start, stop, w=(), inc=None, skip=False):
        return self.P.emit("tensor",
                           lambda t, o=out_ap, a=lhsT, b=rhs, s1=start, s2=stop, sk=skip:
                           t.matmul(o, a, b, start=s1, stop=s2, skip_group_check=sk),
                           w, inc)

    def dma(self, q, out_ap, in_ap, w=(), sem=None):
        sem = sem or QSEM[q]
        assert self.semq.setdefault(sem, q) == q, f"sem {sem} on two queues"
        return self.P.emit(q, lambda e, o=out_ap, i=in_ap: e.dma_start(out=o, in_=i),
                           w, (sem, 16))

    def dyn_dma(self, dram_ap, mult, const, pattern, sbuf_or_dram, w, sem,
                is_out):
        assert self.semq.setdefault(sem, "sync") == "sync"
        def fn(y, dt=dram_ap, m=mult, co=const, pat=pattern, sap=sbuf_or_dram,
               out=is_out):
            y.reg_mul(REGS2["rT"], REGS2["rJ"], m)
            y.reg_add(REGS2["rT"], REGS2["rT"], co)
            dap = bass.AP(dt.tensor, REGS2["rT"], pat)
            if out:
                return y.dma_start(out=dap, in_=sap)
            return y.dma_start(out=sap, in_=dap)
        return self.P.emit("sync", fn, w, (sem, 16))

    def flag_write(self, event, w=()):
        if not self.fused:
            return
        assert self.semq.setdefault("flg", "sync") == "sync"
        fl = self.ds["SFlag"]
        def fn(y, e=event, fl=fl):
            y.reg_mul(REGS2["rT"], REGS2["rJ"], 8)
            y.reg_add(REGS2["rT"], REGS2["rT"], e)
            return y.dma_start(out=bass.AP(fl.tensor, REGS2["rT"], [[1, 1], [1, 1]]),
                               in_=self.sb["flagv_s"][0:1, e:e + 1])
        self.P.emit("sync", fn, w, ("flg", 16))

    def flag_poll(self, event):
        if not self.fused:
            return None
        fl = self.ds["SFlag"]
        def fn(g, e=event, fl=fl):
            g.reg_add(REGS["rT"], REGS["rFR"], e)
            g.reg_add(REGS["rExp"], REGS["rNon"], e + 1)
            g.reg_mov(REGS["rC"], 1)
            with g.While(REGS["rC"]):
                g.reg_add(REGS["rP"], REGS["rP"], 16)
                g.dma_start(out=self.sb["flin"][0:1, 0:1],
                            in_=bass.AP(fl.tensor, REGS["rT"], [[1, 1], [1, 1]])
                            ).then_inc(SEMH["PL"], 16)
                g.wait_ge(SEMH["PL"], REGS["rP"])
                g.reg_load(REGS["rF"], self.sb["flin"][0:1, 0:1])
                g.reg_alu(REGS["rC"], REGS["rF"], REGS["rExp"], ALU.not_equal)
            return g.nop()
        return self.P.emit("gpsimd", fn, (), ("PO", 1))


REGS = {}
REGS2 = {}
SEMH = {}


EV_L0, EV_ATT0, EV_X2, EV_ATT1 = 0, 1, 2, 3


class Arena:

    def __init__(self, handle, nbytes):
        self.h = handle
        self.hf = handle.bitcast(F32)
        self.h8 = handle.bitcast(FP8)
        self.nbytes = nbytes
        self.off = 0

    def reset(self):
        self.off = getattr(self, "floor", 0)

    def _take(self, nb):
        nb = (nb + 63) & ~63
        assert self.off + nb <= self.nbytes, f"arena overflow {self.off}+{nb}"
        o = self.off
        self.off += nb
        return o

    def b(self, parts, cols):
        o = self._take(cols * 2)
        return self.h[0:parts, o // 2: o // 2 + cols]

    def f(self, parts, cols):
        o = self._take(cols * 4)
        return self.hf[0:parts, o // 4: o // 4 + cols]

    def e8(self, parts, cols):
        o = self._take(cols)
        return self.h8[0:parts, o: o + cols]


def _replay(nc, prog):
    sems = {}
    es = contextlib.ExitStack()
    with es:
        for s in sorted(prog.semnames):
            sems[s] = es.enter_context(nc.semaphore(s))
        SEMH.clear()
        SEMH.update(sems)
        block = es.enter_context(nc.Block())

        def run(engname, e):
            for fn, waits, inc in prog.ops[engname]:
                for s, v in waits:
                    e.wait_ge(sems[s], v)
                ins = fn(e)
                if inc is not None:
                    ins.then_inc(sems[inc[0]], inc[1])

        @block.tensor
        def _(t):
            run("tensor", t)

        @block.vector
        def _(v):
            run("vector", v)

        @block.scalar
        def _(s):
            run("scalar", s)

        @block.sync
        def _(y):
            es3 = contextlib.ExitStack()
            with es3:
                REGS2.clear()
                for rn in ("rJ", "rT"):
                    REGS2[rn] = es3.enter_context(y.register("s" + rn))
                run("sync", y)

        @block.gpsimd
        def _(g):
            rnames = ("rJ", "rFW", "rFR", "rNon", "rT", "rF", "rC", "rP", "rExp")
            regs = [es2 := contextlib.ExitStack()]
            with es2:
                REGS.clear()
                for rn in rnames:
                    REGS[rn] = es2.enter_context(g.register(rn))
                run("gpsimd", g)
    return nc


class Phases(Builder):
    def p0(self):
        P, S = self.P, self.S
        nc = self.nc
        self.arena_t = S("arena", [128, 67584], BF16)
        self.ar = Arena(self.arena_t, 135168)
        xs = S("xs", [128, FC * TOK], BF16)
        mt = S("mt", [1, 16], I32)
        S("flagv_s", [1, 8], I32)
        S("flin", [1, 1], I32)
        S("onescol_b", [128, 1], BF16)
        S("onesrow_f", [1, 128], F32)
        S("onesrow_b", [1, 128], BF16)
        S("cst", [128, 4], F32)
        S("onescol_f", [128, 1], F32)
        S("qnw_sb", [128, 12], F32)
        S("kvnw_sb", [128, 4], F32)
        S("bo1_sb", [128, 16], F32)
        S("fb1_sb", [128, 16], F32)
        S("bq1_sb", [128, 8], F32)
        S("bk1_sb", [128, 2], F32)
        S("bv1_sb", [1, 256], BF16)
        if "P1" in self.ph:
            self._tabq = self.ar.b(64, TOK)
            self._tabk = self.ar.b(64, TOK)
            self.sb["tabq_s"] = self._tabq
            self.sb["tabk_s"] = self._tabk
            self.ar.floor = self.ar.off
        S("rstdkv", [1, TOK], F32)
        wm = self.dma("gpsimd", mt[:], self.ds["meta"], sem="cG")
        self.dma("gpsimd", self.sb["flagv_s"][:], self.ds["flagv"], sem="cG")
        wf = ("cG", P.val("cG"))

        def initregs(g):
            g.reg_load(REGS["rJ"], mt[0:1, 6:7])
            g.reg_load(REGS["rFW"], mt[0:1, 2:3])
            g.reg_load(REGS["rFR"], mt[0:1, 3:4])
            g.reg_load(REGS["rNon"], mt[0:1, 4:5])
            g.reg_mov(REGS["rP"], 0)
            return g.nop()
        P.emit("gpsimd", initregs, [wf], ("PO", 1))

        def initregs_sync(y):
            y.reg_load(REGS2["rJ"], mt[0:1, 6:7])
            return y.nop()
        P.emit("sync", initregs_sync, [wf], ("PO", 1))
        if self.first == "P1":
            for f in range(FC):
                self.dma("gpsimd", xs[:, f * TOK:(f + 1) * TOK],
                         self.ds["x_fm"][f * 128:(f + 1) * 128, :], sem="cG")
            self.w_stream = None
        else:
            si = self.ds["stream_in"]
            for f in range(FC):
                self.dma("sync", xs[:, f * TOK:(f + 1) * TOK], si[f], sem="cS")
        for n, srcn in (("qnw_sb", "qnw"), ("kvnw_sb", "kvnw"), ("bo1_sb", "bo1"),
                        ("fb1_sb", "fb1"), ("bq1_sb", "bq1"), ("bk1_sb", "bk1")):
            self.dma("sync", self.sb[n][:], self.ds[srcn], sem="cS")
        self.dma("gpsimd", self.sb["bv1_sb"][:], self.ds["bv1"], sem="cG")
        if "P1" in self.ph:
            self.dma("gpsimd", self._tabq, self.ds["tabq"], sem="cG")
            self.dma("gpsimd", self._tabk, self.ds["tabk"], sem="cG")
        pass
        P.emit("vector", lambda v: v.memset(self.sb["onescol_b"][:], 1.0), (), ("DV", 1))
        P.emit("vector", lambda v: v.memset(self.sb["onesrow_f"][:], 1.0), (), ("DV", 1))
        P.emit("vector", lambda v: v.memset(self.sb["onesrow_b"][:], 1.0), (), ("DV", 1))
        P.emit("vector", lambda v: v.memset(self.sb["onescol_f"][:], 1.0), (), ("DV", 1))
        P.emit("vector", lambda v: v.memset(self.sb["cst"][:, 0:1], EPSR), (), ("DV", 1))
        P.emit("vector", lambda v: v.memset(self.sb["cst"][:, 1:2], EPSL), (), ("DV", 1))
        P.emit("vector", lambda v: v.memset(self.sb["cst"][:, 2:3], -SHIFT), (), ("DV", 1))
        self.w_ones = ("DV", P.val("DV"))
        if "P1" in self.ph:
            self.wkr_load()
        self.w_cG = ("cG", P.val("cG"))
        self.w_cS = ("cS", P.val("cS"))
        self.w_stream = self.w_cG if self.first == "P1" else self.w_cS
        self.w_const = [self.w_cG, self.w_cS]
        self.w_krw = self.w_cG

    def prep_weights(self):
        names = []
        if "P1" in self.ph:
            names += ["wqa", "wkva", "wqb"]
        if "P2" in self.ph:
            names += ["wkvb"]
        if "P3" in self.ph:
            names += ["wo0"]
        if "P6" in self.ph:
            names += ["wq1", "wk1", "wv1"]
        if "P7" in self.ph:
            names += ["wo1"]
        if "P8" in self.ph:
            names += ["w1t"]
        self.wcast = {}
        for n in names:
            shp = list(self.ds[n].shape)
            b = self.dlocal("b_" + n, shp, BF16)
            nch = max(1, (shp[0] * shp[1]) // (1 << 21))
            rows = shp[0] // nch
            for r in range(nch):
                self.dma("gpsimd", b[r * rows:(r + 1) * rows, :],
                         self.ds[n][r * rows:(r + 1) * rows, :], (),
                         sem="c_" + n)
            self.wcast[n] = ("c_" + n, self.P.val("c_" + n))

    def prep_ffn0_scratch(self):
        P = self.P
        for srcn, dstn, nr in (("wg0", "w8g0", 4), ("wg1", "w8g1", 4),
                               ("wlin", "w8lin", 4)):
            dst = self.ds[dstn]
            rows = dst.shape[0] // nr
            for r in range(nr):
                self.dma("gpsimd", dst[r * rows:(r + 1) * rows, :],
                         self.ds[srcn][r * rows:(r + 1) * rows, :], (), sem="s8")
        self.w_scr = ("s8", P.val("s8"))

    def p1(self):
        P, ar = self.P, self.ar
        ar.reset()
        xs = self.sb["xs"]
        A, Bp, C, Dp = self.psA, self.psB, self.psC, self.psD
        latq = [ar.b(128, TOK) for _ in range(12)]
        Bq = ar.f(128, TOK)
        wqa = [ar.b(128, FC * 128) for _ in range(2)]
        wkva = [ar.b(128, FC * 128)]
        wkva.append(wkva[0])
        wqb = [ar.b(128, 12 * DQK) for _ in range(2)]
        wkr = self.sb["wkrt"]
        sq = [ar.b(128, 512) for _ in range(2)]
        lkv = [ar.b(128, 512) for _ in range(2)]
        qp = [ar.b(128, TOK) for _ in range(2)]
        qr = [ar.b(64, TOK) for _ in range(2)]
        krt = [ar.b(64, 512) for _ in range(2)]
        ropf = [ar.f(32, 512) for _ in range(3)]
        rows = ar.f(1, 4 * 512)
        ev_half = [self.w_ones, self.w_ones]
        st_q = []
        wconst = list(self.w_const)
        for oc in range(12):
            wld = self.dma("sync", wqa[oc % 2].rearrange("p (c m) -> p c m", c=FC),
                           self.ds["b_wqa"][:, oc * 128:(oc + 1) * 128]
                           .rearrange("(c p) m -> p c m", p=128),
                           [self.wcast["wqa"]] +
                           ([("MM", P.val("MM"))] if oc >= 2 else []),
                           sem=f"p1a{oc % 2}")
            for tc in range(4):
                ps = A[:, (tc % 2) * 512:(tc % 2) * 512 + 512]
                w = [wld, self.w_stream, ev_half[tc % 2]]
                for ic in range(FC):
                    self.mm(ps, wqa[oc % 2][:, ic * 128:(ic + 1) * 128],
                            xs[:, ic * TOK + tc * 512: ic * TOK + tc * 512 + 512],
                            ic == 0, ic == FC - 1, w if ic == 0 else (),
                            ("MM", 1) if ic == FC - 1 else None)
                wmm = ("MM", P.val("MM"))
                wv = P.emit("vector", lambda v, o=latq[oc], t=tc, p=ps, oc_=oc:
                            v.tensor_scalar(o[:, t * 512:(t + 1) * 512], p,
                                            self.sb["qnw_sb"][:, oc_:oc_ + 1], None,
                                            ALU.mult),
                            [wmm] + (wconst if oc == 0 else []), ("DV", 1))
                ev_half[tc % 2] = wv
                wa = P.emit("scalar", lambda s, q=sq[(oc * 4 + tc) % 2], o=latq[oc], t=tc:
                            s.activation(q[:], o[:, t * 512:(t + 1) * 512], AF.Square),
                            [wv], ("AC", 1))
                stat = (C if tc < 2 else Dp)[0:1, (tc % 2) * 512:(tc % 2) * 512 + 512]
                ws = self.mm(stat, self.sb["onescol_b"][:],
                             sq[(oc * 4 + tc) % 2][:], oc == 0, oc == 11,
                             [wa, self.w_ones], ("MM", 1) if oc == 11 else None,
                             skip=True)
                if oc == 11:
                    st_q.append(("MM", P.val("MM")))
        wBq = []
        for tc in range(4):
            stat = (C if tc < 2 else Dp)[0:1, (tc % 2) * 512:(tc % 2) * 512 + 512]
            r_sl = rows[0:1, tc * 512:(tc + 1) * 512]
            w1 = P.emit("vector", lambda v, s=stat, r=r_sl:
                        v.tensor_scalar(r, s, 1.0 / QL, None, ALU.mult),
                        [st_q[tc]], ("DV", 1))
            w2 = P.emit("scalar", lambda s, r=r_sl:
                        s.activation(r, r, AF.Sqrt, bias=self.sb["cst"][0:1, 0:1]), [w1], ("AC", 1))
            w3 = P.emit("vector", lambda v, r=r_sl: v.reciprocal(r, r), [w2], ("DV", 1))
            wb = self.mm(A[:, (tc % 2) * 512:(tc % 2) * 512 + 512],
                         self.sb["onesrow_f"][:], r_sl, True, True,
                         [w3, ev_half[tc % 2], self.w_ones], ("MM", 1))
            wc = P.emit("vector", lambda v, t=tc, p=A[:, (tc % 2) * 512:(tc % 2) * 512 + 512]:
                        v.tensor_copy(Bq[:, t * 512:(t + 1) * 512], p), [wb], ("DV", 1))
            ev_half[tc % 2] = wc
            wBq.append(wc)
        self.wBq = wBq
        st_kv = []
        ev_bhalf = [self.w_ones, self.w_ones]
        lkv_done = [None, None]
        for oc in range(4):
            wld = self.dma("sync", wkva[oc % 2].rearrange("p (c m) -> p c m", c=FC),
                           self.ds["b_wkva"][:, oc * 128:(oc + 1) * 128]
                           .rearrange("(c p) m -> p c m", p=128),
                           [self.wcast["wkva"]] +
                           ([("MM", P.val("MM"))] if oc >= 1 else []),
                           sem="p1b")
            for tc in range(4):
                ps = Bp[:, (tc % 2) * 512:(tc % 2) * 512 + 512]
                w = [wld, ev_bhalf[tc % 2]]
                for ic in range(FC):
                    self.mm(ps, wkva[oc % 2][:, ic * 128:(ic + 1) * 128],
                            xs[:, ic * TOK + tc * 512: ic * TOK + tc * 512 + 512],
                            ic == 0, ic == FC - 1, w if ic == 0 else (),
                            ("MM", 1) if ic == FC - 1 else None)
                wmm = ("MM", P.val("MM"))
                i2 = (oc * 4 + tc) % 2
                wlk = lkv_done[i2] or ()
                wv = P.emit("vector", lambda v, o=lkv[i2], p=ps, oc_=oc:
                            v.tensor_scalar(o[:], p, self.sb["kvnw_sb"][:, oc_:oc_ + 1],
                                            None, ALU.mult),
                            [wmm] + ([wlk] if wlk else []), ("DV", 1))
                ev_bhalf[tc % 2] = wv
                lkv_done[i2] = self.exch_lkv_write(oc, tc, lkv[i2], [wv], i2)
                wa = P.emit("scalar", lambda s, q=sq[i2], o=lkv[i2]:
                            s.activation(q[:], o[:], AF.Square), [wv], ("AC", 1))
                stat = (C if tc < 2 else Dp)[0:1, (tc % 2) * 512:(tc % 2) * 512 + 512]
                ws = self.mm(stat, self.sb["onescol_b"][:], sq[i2][:],
                             oc == 0, oc == 3, [wa], ("MM", 1) if oc == 3 else None,
                             skip=True)
                if oc == 3:
                    st_kv.append(("MM", P.val("MM")))
        for tc in range(4):
            stat = (C if tc < 2 else Dp)[0:1, (tc % 2) * 512:(tc % 2) * 512 + 512]
            r_sl = self.sb["rstdkv"][0:1, tc * 512:(tc + 1) * 512]
            w1 = P.emit("vector", lambda v, s=stat, r=r_sl:
                        v.tensor_scalar(r, s, 1.0 / KVL, None, ALU.mult),
                        [st_kv[tc]], ("DV", 1))
            w2 = P.emit("scalar", lambda s, r=r_sl:
                        s.activation(r, r, AF.Sqrt, bias=self.sb["cst"][0:1, 0:1]), [w1], ("AC", 1))
            w3 = P.emit("vector", lambda v, r=r_sl: v.reciprocal(r, r), [w2], ("DV", 1))
            self.exch_rstd_write(tc, r_sl, [w3])
        kr_done = [None, None]
        for tc in range(4):
            ps = Bp[:, (tc % 2) * 512:(tc % 2) * 512 + 512][0:64, :]
            for ic in range(FC):
                self.mm(ps, wkr[:, ic * DR:(ic + 1) * DR],
                        xs[:, ic * TOK + tc * 512: ic * TOK + tc * 512 + 512],
                        ic == 0, ic == FC - 1,
                        [ev_bhalf[tc % 2], self.w_krw] if ic == 0 else (),
                        ("MM", 1) if ic == FC - 1 else None)
            wmm = ("MM", P.val("MM"))
            wrp = self.rope(ps, krt[tc % 2][:], "tabk_s", tc, None, ropf,
                            [wmm] + ([kr_done[tc % 2]] if kr_done[tc % 2] else []))
            kr_done[tc % 2] = self.exch_krot_write(tc, krt[tc % 2], [wrp])
            ev_bhalf[tc % 2] = wrp
        for h in range(NH):
            wld = self.dma("sync", wqb[h % 2].rearrange("p (c m) -> p c m", c=12),
                           self.ds["b_wqb"][:, h * DQK:(h + 1) * DQK]
                           .rearrange("(c p) m -> p c m", p=128),
                           [self.wcast["wqb"]] +
                           ([("MM", P.val("MM"))] if h >= 2 else []),
                           sem=f"p1c{h % 2}")
            qp_done = getattr(self, "_qp_done", [None, None])
            ev_rot = getattr(self, "_ev_rot", [("DV", P.val("DV")), ("DV", P.val("DV"))])
            wq_evs = []
            for tc in range(4):
                psp = A[:, (tc % 2) * 512:(tc % 2) * 512 + 512]
                psr = C[0:64, (tc % 2) * 512:(tc % 2) * 512 + 512]
                wstart = [wld, ev_half[tc % 2]] + ([qp_done[h % 2]] if qp_done[h % 2] else [])
                for lc in range(12):
                    self.mm(psp, wqb[h % 2][:, lc * DQK: lc * DQK + DN],
                            latq[lc][:, tc * 512:(tc + 1) * 512],
                            lc == 0, lc == 11, wstart if lc == 0 else (),
                            ("MM", 1) if lc == 11 else None)
                wp = ("MM", P.val("MM"))
                for lc in range(12):
                    self.mm(psr, wqb[h % 2][:, lc * DQK + DN: lc * DQK + DQK],
                            latq[lc][:, tc * 512:(tc + 1) * 512],
                            lc == 0, lc == 11,
                            [ev_rot[tc % 2]] if lc == 0 else (),
                            ("MM", 1) if lc == 11 else None)
                wr = ("MM", P.val("MM"))
                we1 = P.emit("vector", lambda v, o=qp[h % 2], t=tc, p=psp:
                             v.scalar_tensor_tensor(o[:, t * 512:(t + 1) * 512], p, SC0,
                                                    Bq[:, t * 512:(t + 1) * 512],
                                                    ALU.mult, ALU.mult),
                             [wp] + wBq[tc:tc + 1], ("DV", 1))
                ev_half[tc % 2] = we1
                we2 = self.rope(psr, qr[h % 2][:, tc * 512:(tc + 1) * 512],
                                "tabq_s", tc, Bq[0:32, tc * 512:(tc + 1) * 512],
                                ropf, [wr])
                ev_rot[tc % 2] = we2
                wq_evs += [we1, we2]
            dn = self.exch_q_write(h, qp[h % 2], qr[h % 2], wq_evs)
            qp_done[h % 2] = dn
            self._qp_done = qp_done
            self._ev_rot = ev_rot


    def rope(self, ps, out512, tab, tc, Bsl, ropf, w0):
        P = self.P
        t = self.sb[tab]
        c_sl = t[0:32, tc * 512:(tc + 1) * 512]
        s_sl = t[32:64, tc * 512:(tc + 1) * 512]
        p1, p2 = ps[0:32, :], ps[32:64, :]
        a, b, d = ropf
        ch = getattr(self, "_rope_ch", None)
        wl = None
        for half in range(2):
            w0h = list(w0) if half == 0 else []
            if ch:
                w0h.append(ch)
            ch = P.emit("vector", lambda v, p=(p1 if half == 0 else p2), c=c_sl:
                        v.tensor_mul(a[:], p, c), w0h, ("DV", 1))
            ch = P.emit("vector", lambda v, p=(p2 if half == 0 else p1), s=s_sl:
                        v.tensor_mul(b[:], p, s), [ch], ("DV", 1))
            op = ALU.subtract if half == 0 else ALU.add
            tgt = out512[0:32, :] if half == 0 else out512[32:64, :]
            if Bsl is None:
                ch = P.emit("vector", lambda v, o=tgt, oo=op:
                            v.tensor_tensor(o, a[:], b[:], oo), [ch], ("DV", 1))
            else:
                ch = P.emit("vector", lambda v, oo=op: v.tensor_tensor(d[:], a[:], b[:], oo),
                            [ch], ("DV", 1))
                ch = P.emit("vector", lambda v, o=tgt:
                            v.tensor_mul(o, d[:], Bsl), [ch], ("DV", 1))
            wl = ch
        self._rope_ch = ch
        return wl



class Full(Phases):

    def decl_exchange(self):
        pv = (lambda n, s, d: self.dout(n, s, d)) if not self.fused else \
             (lambda n, s, d: self.dlocal(n, s, d))
        pr = (lambda n, s, d: self.din(n, s, d)) if not self.fused else \
             (lambda n, s, d: self.dlocal(n, s, d))
        if "P1" in self.ph:
            pv("Qp_o", [NH, 128, TOK], BF16)
            pv("Qr_o", [NH, 64, TOK], BF16)
            pv("Lkv_o", [4, 128, TOK], BF16)
            pv("Rt_o", [64, TOK], BF16)
            pv("Rstd_o", [1, TOK], F32)
        if "P2" in self.ph:
            pr("Qp_i", [MYH, 128, T], BF16)
            pr("Qr_i", [MYH, 64, T], BF16)
            if not self.fused:
                self.din("Lkv_i", [4, 128, T], BF16)
                self.din("Rt_i", [64, T], BF16)
                self.din("Rstd_i", [1, T], F32)
            pv("O_o", [MYH, 128, T], BF16)
        if "P3" in self.ph:
            pr("O_i", [NH, 128, TOK], BF16)
        if "P5" in self.ph:
            pv("X2_o", [FC, 128, TOK], BF16)
        if "P6" in self.ph:
            if not self.fused:
                self.din("X2_i", [FC, 128, T], BF16)
            pv("O1_o", [MYH, 128, T], BF16)
        if "P7" in self.ph:
            pr("O1_i", [NH, 128, TOK], BF16)
        if self.fused:
            self.dshared("SQp", [NH, 128, T], BF16)
            self.dshared("SQr", [NH, 64, T], BF16)
            self.dshared("SLkv", [4, 128, T], BF16)
            self.dshared("SRt", [64, T], BF16)
            self.dshared("SRstd", [1, T], F32)
            self.dshared("So", [NH, 128, T], BF16)
            self.dshared("Sx", [FC, 128, T], BF16)
            self.dshared("SFlag", [16, 1], I32)
            self.ds["Lkv_i"] = self.ds["SLkv"]
            self.ds["Rt_i"] = self.ds["SRt"]
            self.ds["Rstd_i"] = self.ds["SRstd"]
            self.ds["X2_i"] = self.ds["Sx"]

    def exch_q_write(self, h, qp, qr, waits):
        sem = f"xqw{h % 2}"
        self.dma("gpsimd", self.ds["Qp_o"][h], qp, waits, sem=sem)
        self.dma("gpsimd", self.ds["Qr_o"][h], qr, waits, sem=sem)
        return (sem, self.P.val(sem))

    def exch_lkv_write(self, lc, tc, tile, waits, slot):
        sem = f"xlk{slot}"
        self.dma("gpsimd", self.ds["Lkv_o"][lc][:, tc * 512:(tc + 1) * 512],
                 tile, waits, sem=sem)
        return (sem, self.P.val(sem))

    def exch_krot_write(self, tc, tile, waits):
        sem = f"xkr{tc % 2}"
        self.dma("gpsimd", self.ds["Rt_o"][:, tc * 512:(tc + 1) * 512],
                 tile, waits, sem=sem)
        return (sem, self.P.val(sem))

    def exch_rstd_write(self, tc, row, waits):
        self.dma("gpsimd", self.ds["Rstd_o"][0:1, tc * 512:(tc + 1) * 512],
                 row, waits, sem="xrs")
        return ("xrs", self.P.val("xrs"))

    def fuse_l0pre(self):
        if not self.fused:
            return
        P = self.P
        w_all = [(s, P.val(s)) for s in ("xlk0", "xlk1", "xkr0", "xkr1", "xrs",
                                         "xqw0", "xqw1") if P.val(s) > 0]
        for name, rows in (("Qp", NH * 128), ("Qr", NH * 64), ("Lkv", 4 * 128)):
            self.dyn_dma(self.ds["S" + name], TOK, 0, [[T, rows], [1, TOK]],
                         self.ds[name + "_o"].rearrange("a b t -> (a b) t"),
                         w_all, "xsc", True)
        self.dyn_dma(self.ds["SRt"], TOK, 0, [[T, 64], [1, TOK]],
                     self.ds["Rt_o"], w_all, "xsc", True)
        self.dyn_dma(self.ds["SRstd"], TOK, 0, [[T, 1], [1, TOK]],
                     self.ds["Rstd_o"], w_all, "xsc", True)
        self.flag_write(EV_L0, [("xsc", P.val("xsc"))])
        wp = self.flag_poll(EV_L0)
        self.dyn_dma(self.ds["SQp"], MYH * 128 * T, 0, [[T, MYH * 128], [1, T]],
                     self.ds["Qp_i"].rearrange("a b t -> (a b) t"), [wp], "xga",
                     False)
        self.dyn_dma(self.ds["SQr"], MYH * 64 * T, 0, [[T, MYH * 64], [1, T]],
                     self.ds["Qr_i"].rearrange("a b t -> (a b) t"), [wp], "xga",
                     False)
        self.w_qgather = ("xga", P.val("xga"))
        self.w_poll_l0 = wp

    def fuse_att0(self):
        if not self.fused:
            return
        P = self.P
        w_all = [(s, P.val(s)) for s in ("xow0", "xow1") if P.val(s) > 0]
        self.dyn_dma(self.ds["So"], MYH * 128 * T, 0, [[T, MYH * 128], [1, T]],
                     self.ds["O_o"].rearrange("a b t -> (a b) t"), w_all, "xsc",
                     True)
        self.flag_write(EV_ATT0, [("xsc", P.val("xsc"))])
        wp = self.flag_poll(EV_ATT0)
        self.dyn_dma(self.ds["So"], TOK, 0, [[T, NH * 128], [1, TOK]],
                     self.ds["O_i"].rearrange("a b t -> (a b) t"), [wp], "xga",
                     False)
        self.w_ogather0 = ("xga", P.val("xga"))

    def fuse_x2(self):
        if not self.fused:
            return
        P = self.P
        w_all = [("xxw", P.val("xxw"))]
        self.dyn_dma(self.ds["Sx"], TOK, 0, [[T, FC * 128], [1, TOK]],
                     self.ds["X2_o"].rearrange("a b t -> (a b) t"), w_all, "xsc",
                     True)
        self.flag_write(EV_X2, [("xsc", P.val("xsc"))])
        self.w_poll_x2 = self.flag_poll(EV_X2)

    def fuse_att1(self):
        if not self.fused:
            return
        P = self.P
        w_all = [(s, P.val(s)) for s in ("xo10", "xo11") if P.val(s) > 0]
        self.dyn_dma(self.ds["So"], MYH * 128 * T, 0, [[T, MYH * 128], [1, T]],
                     self.ds["O1_o"].rearrange("a b t -> (a b) t"), w_all, "xsc",
                     True)
        self.flag_write(EV_ATT1, [("xsc", P.val("xsc"))])
        wp = self.flag_poll(EV_ATT1)
        self.dyn_dma(self.ds["So"], TOK, 0, [[T, NH * 128], [1, TOK]],
                     self.ds["O1_i"].rearrange("a b t -> (a b) t"), [wp], "xga",
                     False)
        self.w_ogather1 = ("xga", P.val("xga"))


    def build(self):
        nc = self.nc
        self.din("meta", [1, 16], I32)
        self.din("flagv", [1, 8], I32)
        self.din("x_fm", [D, TOK])
        for n, sh in (("wqa", [D, QL]), ("wkva", [D, KVL]), ("wkr", [D, DR]),
                      ("wqb", [QL, NH * DQK]), ("wkvb", [KVL, MYH * 256]),
                      ("wo0", [D, D]), ("wg0", [D, HID]), ("wg1", [D, HID]),
                      ("wlin", [HID, D]), ("wq1", [D, MYH * DV]),
                      ("wk1", [D, 256]), ("wv1", [D, 256]), ("wo1", [D, D]),
                      ("w1t", [D, D])):
            self.din(n, sh)
        for n, c in (("qnw", 12), ("kvnw", 4), ("bo1", 16), ("fb1", 16),
                     ("bq1", 8), ("bk1", 2)):
            self.din(n, [128, c])
        self.din("bv1", [1, 256])
        self.din("tabq", [64, TOK])
        self.din("tabk", [64, TOK])
        if self.first != "P1":
            self.din("stream_in", [FC, 128, TOK], BF16)
        if self.last != "P8":
            self.dout("stream_out", [FC, 128, TOK], BF16)
        else:
            self.dout("out", [FC, 128, TOK])
        self.decl_exchange()
        if "P4" in self.ph:
            self.dlocal("w8g0", [D, HID], FP8)
            self.dlocal("w8g1", [D, HID], FP8)
            self.dlocal("w8lin", [HID, D], FP8)
        if "P6" in self.ph:
            self.dlocal("Qs", [MYH, 128, T], BF16)

        with self.es:
            self.psA = self.es.enter_context(self.nc.psum_tensor("psA", [128, 1024], F32))
            self.psB = self.es.enter_context(self.nc.psum_tensor("psB", [128, 1024], F32))
            self.psC = self.es.enter_context(self.nc.psum_tensor("psC", [128, 1024], F32))
            self.psD = self.es.enter_context(self.nc.psum_tensor("psD", [128, 1024], F32))
            self.p0()
            self.prep_weights()
            if "P1" in self.ph:
                self.p1()
                self.fuse_l0pre()
                self.ar.floor = 0
            if "P4" in self.ph:
                self.prep_ffn0_scratch()
            if "P2" in self.ph:
                self.P.barrier()
                self.p2()
                self.fuse_att0()
            if "P3" in self.ph:
                self.P.barrier()
                self.p3()
            if "P4" in self.ph:
                self.P.barrier()
                self.p4()
            if "P5" in self.ph:
                self.p5()
                self.fuse_x2()
            if "P6" in self.ph:
                self.P.barrier()
                self.p6()
                self.fuse_att1()
            if "P7" in self.ph:
                self.P.barrier()
                self.p7()
            if "P8" in self.ph:
                self.P.barrier()
                self.p8()
            if self.last != "P8":
                self.P.barrier()
                self.stream_save()
            self.finish()
            _replay(nc, self.P)
        return nc

    def wkr_load(self):
        self.sb["wkrt"] = self.es.enter_context(
            self.nc.sbuf_tensor("wkrt", [128, FC * DR], BF16))
        self.dma(
            "gpsimd", self.sb["wkrt"][:].rearrange("p (c m) -> p c m", c=FC),
            self.ds["wkr"].rearrange("(c p) m -> p c m", p=128), sem="cG")

    def stream_save(self):
        xs = self.sb["xs"]
        so = self.ds["stream_out"]
        for f in range(FC):
            self.dma("sync", so[f], xs[:, f * TOK:(f + 1) * TOK], sem="sso")

    def finish(self):
        P = self.P
        w = list(P.barrier_waits())
        P.emit("sync", lambda y: y.nop(), w)
        P.emit("gpsimd", lambda g: g.nop(), w)


class Full2(Full):
    def p2(self):
        P, ar = self.P, self.ar
        ar.reset()
        A, Bp, C, Dp = self.psA, self.psB, self.psC, self.psD
        latkvF = [ar.b(128, T) for _ in range(4)]
        krotF = ar.b(64, T)
        Bkv = ar.b(128, T)
        rstdc = ar.f(128, 32)
        rrow = ar.b(1, T)
        kp = ar.b(128, T)
        vtm = ar.b(128, T)
        qpb = [ar.b(128, T) for _ in range(2)]
        qrb = [ar.b(64, T)]
        qrb.append(qrb[0])
        wkvb = [ar.b(128, 4 * 256) for _ in range(2)]
        Eb = [ar.b(128, 1024) for _ in range(4)]
        Aacc = ar.b(128, 512)
        osb = [ar.b(128, 512) for _ in range(2)]
        bsb = [ar.f(128, 512) for _ in range(2)]
        srow = ar.f(1, 512)
        self._prevE = {}
        self._wexp_prev = {}
        wp = [self.w_poll_l0] if self.fused else []
        for lc in range(4):
            self.dma("sync", latkvF[lc], self.ds["Lkv_i"][lc], wp, sem="g2s")
        self.dma("sync", krotF, self.ds["Rt_i"], wp, sem="g2s")
        self.dma("gpsimd", rrow, self.ds["Rstd_i"], wp, sem="g2g")
        wgat = ("g2s", P.val("g2s"))
        wgat2 = ("g2g", P.val("g2g"))
        wBkv = []
        for kc in range(8):
            wb = self.mm(A[:, (kc % 2) * 512:(kc % 2) * 512 + 512],
                         self.sb["onesrow_b"][:], rrow[0:1, kc * 512:(kc + 1) * 512],
                         True, True,
                         [wgat, wgat2, self.w_ones] + ([wBkv[kc - 2]] if kc >= 2 else []),
                         ("MM", 1), skip=True)
            wc = P.emit("vector", lambda v, k=kc,
                        p=A[:, (kc % 2) * 512:(kc % 2) * 512 + 512]:
                        v.tensor_copy(Bkv[:, k * 512:(k + 1) * 512], p), [wb], ("DV", 1))
            wBkv.append(wc)
        wBkv_all = ("DV", P.val("DV"))
        wcp = [None, None]
        for tt in range(32):
            hb2 = tt % 2
            dst = Bp[:, hb2 * 512: hb2 * 512 + 1]
            wb = self.mm(dst, rrow[0:1, tt * 128:(tt + 1) * 128],
                         self.sb["onesrow_b"][0:1, 0:1], True, True,
                         [wgat, wgat2, self.w_ones] + ([wcp[hb2]] if wcp[hb2] else []),
                         ("MM", 1), skip=True)
            wcp[hb2] = P.emit("vector", lambda v, t=tt, p=dst:
                              v.tensor_copy(rstdc[:, t:t + 1], p), [wb], ("DV", 1))
        wrst = ("DV", P.val("DV"))
        wq_prev = [None, None]
        wkv_prev = None
        oev_half = [None, None]
        for h in range(MYH):
            hb = h % 2
            wqg = [self.w_qgather] if self.fused else []
            wqld1 = self.dma("sync", qpb[hb], self.ds["Qp_i"][h],
                             wqg + ([wq_prev[hb]] if wq_prev[hb] else []),
                             sem=f"q2l{hb}")
            wqld2 = self.dma("sync", qrb[hb], self.ds["Qr_i"][h],
                             [("MM", P.val("MM"))], sem=f"q2l{hb}")
            wqld1 = wqld2
            wwl = self.dma("sync", wkvb[hb].rearrange("p (c m) -> p c m", c=4),
                           self.ds["b_wkvb"][:, h * 256:(h + 1) * 256]
                           .rearrange("(c p) m -> p c m", p=128),
                           [self.wcast["wkvb"]] +
                           ([("MM", P.val("MM"))] if h >= 2 else []),
                           sem=f"p2w{hb}")
            wkp = []
            for kc in range(8):
                ps = A[:, (kc % 2) * 512:(kc % 2) * 512 + 512]
                w0 = ([wwl, wBkv_all, ("DV", P.val("DV")), ("AC", P.val("AC"))]
                      if kc < 2 else [wwl, wkp[kc - 2]])
                for lc in range(4):
                    self.mm(ps, wkvb[hb][:, lc * 256: lc * 256 + 128],
                            latkvF[lc][:, kc * 512:(kc + 1) * 512],
                            lc == 0, lc == 3,
                            (w0 + [wgat]) if lc == 0 else (),
                            ("MM", 1) if lc == 3 else None)
                wkp.append(P.emit("vector", lambda v, k=kc, p=ps:
                                  v.tensor_mul(kp[:, k * 512:(k + 1) * 512], p,
                                               Bkv[:, k * 512:(k + 1) * 512]),
                                  [("MM", P.val("MM"))], ("DV", 1)))
            wvt = []
            for g4 in range(8):
                ps = Bp[:, (g4 % 2) * 512:(g4 % 2) * 512 + 512]
                for s in range(4):
                    tt = g4 * 4 + s
                    if s == 0:
                        w0v = ([wrst, ("DV", P.val("DV"))] if g4 < 2
                               else [wvt[4 * (g4 - 2) + 3]])
                    for lc in range(4):
                        self.mm(ps[:, s * 128:(s + 1) * 128],
                                latkvF[lc][:, tt * 128:(tt + 1) * 128],
                                wkvb[hb][:, lc * 256 + 128: lc * 256 + 256],
                                lc == 0, lc == 3,
                                w0v if (s == 0 and lc == 0) else (),
                                ("MM", 1) if lc == 3 else None)
                wmm = ("MM", P.val("MM"))
                for s in range(4):
                    tt = g4 * 4 + s
                    wvt.append(P.emit("vector", lambda v, t=tt, p=ps, s_=s:
                                      v.tensor_scalar(vtm[:, t * 128:(t + 1) * 128],
                                                      p[:, s_ * 128:(s_ + 1) * 128],
                                                      rstdc[:, t:t + 1], None, ALU.mult),
                                      [wmm], ("DV", 1)))
            wkv_done = ("DV", P.val("DV"))
            wq1 = wqld1
            wq2 = wqld2
            for qc in range(8):
                qb = qc // 2
                pq = PERM[qb]
                klist = []
                for kb in range(4):
                    if PERM[kb] < pq:
                        klist += [(kb * 8 + i, None) for i in range(8)]
                for i in range((qc % 2) * 4 + 4):
                    b_ = (qc % 2) * 512 - i * 128
                    klist.append((qb * 8 + i, b_ if b_ < 512 else None))
                nkt = len(klist)
                o_ps = C[:, (qc % 2) * 512:(qc % 2) * 512 + 512]
                wA_last = None
                wexp_prev = getattr(self, "_wexp_prev", {})
                prevE = getattr(self, "_prevE", {})
                groups = list(range(0, nkt, 2))
                pend = []

                def emit_pv(ent, _oev=oev_half, _qc=qc, _ops=o_ps, _nkt=nkt):
                    ki0_, npair_, E_, wE_, eslot_ = ent
                    for sub in range(npair_):
                        kt_ = klist[ki0_ + sub][0]
                        ki_ = ki0_ + sub
                        self.mm(_ops, vtm[:, kt_ * 128:(kt_ + 1) * 128],
                                E_[:, sub * 512:(sub + 1) * 512],
                                ki_ == 0, ki_ == _nkt - 1,
                                [wE_] + ([_oev[_qc % 2]] if (ki_ == 0 and _oev[_qc % 2]) else []),
                                ("MM", 1) if ki_ == _nkt - 1 else None, skip=True)
                    prevE[eslot_] = tuple(prevE.get(eslot_, ())) + (("MM", P.val("MM")),)

                for gi, ki0 in enumerate(groups):
                    if len(pend) >= 2:
                        emit_pv(pend.pop(0))
                    sb_ix = gi % 2
                    eslot = gi % 4
                    S2 = (A if sb_ix == 0 else Bp)
                    wS = ([wq1, wq2, wkv_done] if ki0 == 0 and qc == 0 else []) + \
                         list(wexp_prev.get(sb_ix, ()))
                    npair = 2 if ki0 + 1 < nkt else 1
                    for sub in range(npair):
                        kt = klist[ki0 + sub][0]
                        sl = S2[:, sub * 512:(sub + 1) * 512]
                        self.mm(sl, kp[:, kt * 128:(kt + 1) * 128],
                                qpb[hb][:, qc * 512:(qc + 1) * 512],
                                True, False, wS if sub == 0 else (), None)
                        wS = []
                        self.mm(sl, krotF[:, kt * 128:(kt + 1) * 128],
                                qrb[hb][:, qc * 512:(qc + 1) * 512],
                                False, True, (), ("MM", 1))
                    wsc = ("MM", P.val("MM"))
                    E = Eb[eslot]
                    wexp = P.emit("scalar", lambda s, e=E, p=S2, n=npair:
                                  s.activation(e[:, 0:n * 512], p[:, 0:n * 512],
                                               AF.Exp, bias=self.sb["cst"][:, 2:3]),
                                  [wsc, self.w_ones] + list(prevE.get(eslot, ())),
                                  ("AC", 1))
                    prevE[eslot] = ()
                    wmask = []
                    for sub in range(npair):
                        mb = klist[ki0 + sub][1]
                        if mb is not None:
                            wmask.append(P.emit(
                                "gpsimd", lambda g, e=E, s_=sub, b_=mb:
                                g.affine_select(out=e[:, s_ * 512:(s_ + 1) * 512],
                                                in_=e[:, s_ * 512:(s_ + 1) * 512],
                                                pattern=[[1, 512]],
                                                compare_op=ALU.is_ge, fill=0.0,
                                                base=b_, channel_multiplier=-1),
                                [wexp], ("PO", 1)))
                    wE = wmask[-1] if wmask else wexp
                    for sub in range(npair):
                        if ki0 == 0 and sub == 0:
                            wA_last = P.emit("vector", lambda v, e=E:
                                             v.tensor_copy(Aacc[:], e[:, 0:512]),
                                             [wE, ("MM", P.val("MM"))], ("DV", 1))
                        else:
                            wA_last = P.emit("vector", lambda v, e=E, s_=sub:
                                             v.tensor_add(Aacc[:], Aacc[:],
                                                          e[:, s_ * 512:(s_ + 1) * 512]),
                                             [wE] + ([wA_last] if wA_last else []),
                                             ("DV", 1))
                    prevE[eslot] = tuple(prevE.get(eslot, ())) + \
                        (wA_last, ("PO", P.val("PO")))
                    wexp_prev[sb_ix] = (wexp,)
                    pend.append((ki0, npair, E, wE, eslot))
                while pend:
                    emit_pv(pend.pop(0))
                self._prevE = prevE
                self._wexp_prev = wexp_prev
                wpv = ("MM", P.val("MM"))
                hD = (qc % 2)
                ws = self.mm(Dp[0:1, hD * 512: hD * 512 + 512],
                             self.sb["onescol_b"][:], Aacc[:], True, True,
                             [wA_last, ("DV", P.val("DV"))], ("MM", 1), skip=True)
                w1 = P.emit("vector", lambda v, d=Dp[0:1, hD * 512: hD * 512 + 512]:
                            v.tensor_copy(srow[:], d), [ws], ("DV", 1))
                w2 = P.emit("vector", lambda v: v.reciprocal(srow[:], srow[:]),
                            [w1], ("DV", 1))
                wb = self.mm(Dp[:, (1 - hD) * 512:(1 - hD) * 512 + 512],
                             self.sb["onesrow_f"][:], srow[:], True, True,
                             [w2], ("MM", 1), skip=True)
                w3 = P.emit("vector", lambda v, d=Dp[:, (1 - hD) * 512:(1 - hD) * 512 + 512],
                            b=bsb[hD]: v.tensor_copy(b[:], d), [wb], ("DV", 1))
                ow_prev = getattr(self, "_ow_prev", [None, None])
                w4 = P.emit("vector", lambda v, o=osb[hD], b=bsb[hD], p=o_ps:
                            v.tensor_mul(o[:], p, b[:]),
                            [w3, wpv] + ([ow_prev[hD]] if ow_prev[hD] else []),
                            ("DV", 1))
                oev_half[qc % 2] = w4
                ow_prev[hD] = self.dma(
                    "gpsimd", self.ds["O_o"][h][:, qc * 512:(qc + 1) * 512],
                    osb[hD][:], [w4], sem=f"xow{hD}")
                self._ow_prev = ow_prev
            wq_prev[hb] = ("MM", P.val("MM"))

    def layernorm(self, get_r, put_out, tcs=4):
        P, ar = self.P, self.ar
        C, Dp = self.psC, self.psD
        sqt = [ar.b(128, 512) for _ in range(2)]
        mrow = ar.f(1, 512)
        vrow = ar.f(1, 512)
        trow = ar.f(1, 512)
        wap_prev = None
        sqg = getattr(self, "_ln_sqg", [None, None])
        for tc in range(tcs):
            for oc in range(FC):
                r_ap, wr = get_r(oc, tc)
                self.mm(C[0:1, 0:512], self.sb["onescol_b"][:], r_ap,
                        oc == 0, oc == FC - 1,
                        [wr, self.w_ones] + ([wap_prev] if (oc == 0 and wap_prev) else []),
                        ("MM", 1), skip=True)
                wa = P.emit("scalar", lambda s, q=sqt[oc % 2], r=r_ap:
                            s.activation(q[:], r, AF.Square),
                            [wr] + ([sqg[oc % 2]] if sqg[oc % 2] else []), ("AC", 1))
                sqg[oc % 2] = self.mm(C[0:1, 512:1024], self.sb["onescol_b"][:],
                                      sqt[oc % 2][:], oc == 0, oc == FC - 1, [wa],
                                      ("MM", 1), skip=True)
                self._ln_sqg = sqg
            wst = ("MM", P.val("MM"))
            w1 = P.emit("vector", lambda v: v.tensor_scalar(
                mrow[:], C[0:1, 0:512], 1.0 / D, None, ALU.mult), [wst], ("DV", 1))
            w2 = P.emit("vector", lambda v: v.tensor_scalar(
                vrow[:], C[0:1, 512:1024], 1.0 / D, None, ALU.mult), [wst], ("DV", 1))
            w3 = P.emit("vector", lambda v: v.tensor_mul(trow[:], mrow[:], mrow[:]),
                        [w1], ("DV", 1))
            w4 = P.emit("vector", lambda v: v.tensor_tensor(
                vrow[:], vrow[:], trow[:], ALU.subtract), [w2, w3], ("DV", 1))
            w5 = P.emit("scalar", lambda s: s.activation(
                vrow[:], vrow[:], AF.Sqrt,
                bias=self.sb["cst"][0:1, 1:2]), [w4], ("AC", 1))
            w6 = P.emit("vector", lambda v: v.reciprocal(vrow[:], vrow[:]),
                        [w5], ("DV", 1))
            w7 = P.emit("vector", lambda v: v.tensor_mul(trow[:], mrow[:], vrow[:]),
                        [w6], ("DV", 1))
            w8 = P.emit("vector", lambda v: v.tensor_scalar(
                trow[:], trow[:], -1.0, None, ALU.mult), [w7], ("DV", 1))
            wba = self.mm(Dp[:, 0:512], self.sb["onesrow_f"][:], vrow[:],
                          True, True, [w6], ("MM", 1), skip=True)
            wbb = self.mm(Dp[:, 512:1024], self.sb["onesrow_f"][:], trow[:],
                          True, True, [w8], ("MM", 1), skip=True)
            wlast = None
            for oc in range(FC):
                wlast = put_out(oc, tc, Dp[:, 0:512], Dp[:, 512:1024], [wba, wbb])
            wap_prev = wlast

    def p3(self, wo="wo0", obuf_src="O_i", ev=EV_ATT0, bo=None):
        P, ar = self.P, self.ar
        ar.reset()
        A, Bp = self.psA, self.psB
        xs = self.sb["xs"]
        obuf = [ar.b(128, TOK) for _ in range(NH)]
        wob = [ar.b(128, FC * 128) for _ in range(2)]
        ttile = [ar.b(128, 512) for _ in range(2)]
        wg = []
        if self.fused:
            wg = [self.w_ogather0 if ev == EV_ATT0 else self.w_ogather1]
        for gh in range(NH):
            self.dma("sync", obuf[gh], self.ds[obuf_src][gh], wg, sem="obr")
        wo_all = [("obr", P.val("obr"))]
        rdone = {}
        ev_half = {}
        for tc in range(4):
            for oc in range(FC):
                ps = (A if oc % 2 == 0 else Bp)[:, (tc % 2) * 512:(tc % 2) * 512 + 512]
                slot = (oc % 2, tc % 2)
                wld = self.dma("sync",
                               wob[oc % 2].rearrange("p (c m) -> p c m", c=FC),
                               self.ds["b_" + wo][:, oc * 128:(oc + 1) * 128]
                               .rearrange("(c p) m -> p c m", p=128),
                               [self.wcast[wo]] +
                               ([("MM", P.val("MM"))] if (tc > 0 or oc >= 2) else []),
                               sem=f"p3w{oc % 2}")
                w0 = [wld] + wo_all + ([ev_half[slot]] if ev_half.get(slot) else [])
                for hc in range(NH):
                    self.mm(ps, wob[oc % 2][:, hc * 128:(hc + 1) * 128],
                            obuf[hc][:, tc * 512:(tc + 1) * 512],
                            hc == 0, hc == NH - 1, w0 if hc == 0 else (),
                            ("MM", 1) if hc == NH - 1 else None)
                wmm = ("MM", P.val("MM"))
                xsl = xs[:, oc * TOK + tc * 512: oc * TOK + tc * 512 + 512]
                if bo is None:
                    wv = P.emit("vector", lambda v, p=ps, x=xsl:
                                v.tensor_add(x, p, x), [wmm], ("DV", 1))
                else:
                    wv = P.emit("vector", lambda v, p=ps, x=xsl, oc_=oc:
                                v.scalar_tensor_tensor(
                                    x, p, self.sb[bo][:, oc_:oc_ + 1], x,
                                    ALU.add, ALU.add), [wmm], ("DV", 1))
                ev_half[slot] = wv
                rdone[(oc, tc)] = wv

        def get_r(oc, tc):
            return (xs[:, oc * TOK + tc * 512: oc * TOK + tc * 512 + 512],
                    rdone[(oc, tc)])

        def put_out(oc, tc, Aap, Bap, waits):
            xsl = xs[:, oc * TOK + tc * 512: oc * TOK + tc * 512 + 512]
            w1 = P.emit("vector", lambda v, x=xsl, a=Aap, t=ttile[oc % 2]:
                        v.tensor_mul(t[:], x, a), waits, ("DV", 1))
            return P.emit("vector", lambda v, x=xsl, b=Bap, t=ttile[oc % 2]:
                          v.tensor_add(x, t[:], b), [w1], ("DV", 1))
        self.layernorm(get_r, put_out)


class Full3(Full2):
    def p4(self):
        P, ar = self.P, self.ar
        ar.reset()
        A, Bp = self.psA, self.psB
        xs = self.sb["xs"]
        x8 = [ar.e8(128, 512) for _ in range(FC)]
        a8 = [ar.e8(128, 512) for _ in range(64)]
        wg0b = [ar.e8(128, FC * 256) for _ in range(2)]
        wg1b = [ar.e8(128, FC * 256) for _ in range(2)]
        wlb = [ar.e8(128, 64 * 128) for _ in range(2)]
        gt = [ar.b(128, 512) for _ in range(2)]
        ft = [ar.b(128, 512) for _ in range(2)]
        st = [ar.b(128, 512) for _ in range(2)]
        g3t = [ar.b(128, 512) for _ in range(2)]
        wt = [ar.b(128, 512) for _ in range(2)]
        tt_ = [ar.b(128, 512) for _ in range(2)]
        mt_ = [ar.b(128, 512) for _ in range(2)]
        rdone = {}
        wg_prev = {}
        wsc = [self.w_scr]
        for tt in range(4):
            wx8 = []
            for ic in range(FC):
                wx8.append(P.emit("scalar", lambda s, i=ic, t=tt:
                                  s.activation(x8[i][:], xs[:, i * TOK + t * 512:
                                                            i * TOK + t * 512 + 512],
                                               AF.Copy),
                                  [rdone[(ic, tt - 1)]] if tt > 0 else (), ("AC", 1)))
            wx8a = ("AC", P.val("AC"))
            for hc in range(64):
                hb = hc % 2
                hp = (hc // 2) % 2
                if hc % 2 == 0:
                    self.dma("sync", wg0b[hp].rearrange("p (c m) -> p c m", c=FC),
                             self.ds["w8g0"][:, hc * 128:(hc + 2) * 128]
                             .rearrange("(c p) m -> p c m", p=128),
                             wsc + ([wg_prev[hp]] if hp in wg_prev else []),
                             sem=f"p4g{hp}")
                    wl1 = self.dma("sync", wg1b[hp].rearrange("p (c m) -> p c m", c=FC),
                                   self.ds["w8g1"][:, hc * 128:(hc + 2) * 128]
                                   .rearrange("(c p) m -> p c m", p=128), (),
                                   sem=f"p4g{hp}")
                    self._p4wl = wl1
                sub2 = hc % 2
                psg = A[:, hb * 512: hb * 512 + 512]
                psf = Bp[:, hb * 512: hb * 512 + 512]
                wevg = getattr(self, "_p4evg", {})
                w0 = [self._p4wl, wx8a] + list(wevg.get(hb, ()))
                for ic in range(FC):
                    self.mm(psg, wg0b[hp][:, ic * 256 + sub2 * 128:
                                          ic * 256 + sub2 * 128 + 128], x8[ic][:],
                            ic == 0, ic == FC - 1, w0 if ic == 0 else (),
                            ("MM", 1) if ic == FC - 1 else None, skip=True)
                for ic in range(FC):
                    self.mm(psf, wg1b[hp][:, ic * 256 + sub2 * 128:
                                          ic * 256 + sub2 * 128 + 128], x8[ic][:],
                            ic == 0, ic == FC - 1, (),
                            ("MM", 1) if ic == FC - 1 else None, skip=True)
                if hc % 2 == 1:
                    wg_prev[hp] = ("MM", P.val("MM"))
                wmm = ("MM", P.val("MM"))
                w1 = P.emit("scalar", lambda s, g=gt[hb], p=psg:
                            s.activation(g[:], p, AF.Copy), [wmm], ("AC", 1))
                w2 = P.emit("vector", lambda v, f=ft[hb], p=psf:
                            v.tensor_copy(f[:], p), [wmm], ("DV", 1))
                self._p4evg = {**getattr(self, "_p4evg", {}), hb: (w1, w2)}
                w3 = P.emit("scalar", lambda s, q=st[hb], g=gt[hb]:
                            s.activation(q[:], g[:], AF.Square), [w1], ("AC", 1))
                w4 = P.emit("gpsimd", lambda g_, a=g3t[hb], g=gt[hb], q=st[hb]:
                            g_.tensor_mul(a[:], g[:], q[:]), [w3], ("PO", 1))
                w5 = P.emit("vector", lambda v, w=wt[hb], a=g3t[hb], g=gt[hb]:
                            v.scalar_tensor_tensor(w[:], a[:], GC2, g[:],
                                                   ALU.mult, ALU.add), [w4], ("DV", 1))
                w6 = P.emit("scalar", lambda s, t_=tt_[hb], w=wt[hb]:
                            s.activation(t_[:], w[:], AF.Tanh, scale=GC1),
                            [w5], ("AC", 1))
                w7 = P.emit("gpsimd", lambda g_, m=mt_[hb], g=gt[hb], f=ft[hb]:
                            g_.tensor_mul(m[:], g[:], f[:]), [w2], ("PO", 1))
                w8 = P.emit("vector", lambda v, o=a8[hc], t_=tt_[hb], m=mt_[hb]:
                            v.scalar_tensor_tensor(o[:], t_[:], 1.0, m[:],
                                                   ALU.add, ALU.mult),
                            [w6, w7], ("DV", 1))
                rdone[("a8", hc)] = w8
            wa8 = ("DV", P.val("DV"))
            wdn_prev = {}
            for oc in range(FC):
                hb = oc % 2
                wl = self.dma("sync",
                              wlb[hb].rearrange("p (c m) -> p c m", c=64),
                              self.ds["w8lin"][:, oc * 128:(oc + 1) * 128]
                              .rearrange("(c p) m -> p c m", p=128),
                              wsc + ([wdn_prev[hb]] if hb in wdn_prev else []),
                              sem=f"p4l{hb}")
                ps = (A if hb == 0 else Bp)[:, 0:512]
                w0 = [wl, wa8] + ([rdone[("dn", oc - 2)]] if oc >= 2 else [])
                for hc in range(64):
                    self.mm(ps, wlb[hb][:, hc * 128:(hc + 1) * 128], a8[hc][:],
                            hc == 0, hc == 63, w0 if hc == 0 else (),
                            ("MM", 1) if hc == 63 else None, skip=True)
                wmm = ("MM", P.val("MM"))
                wdn_prev[hb] = wmm
                xsl = xs[:, oc * TOK + tt * 512: oc * TOK + tt * 512 + 512]
                wv = P.emit("vector", lambda v, p=ps, x=xsl:
                            v.scalar_tensor_tensor(x, p, 0.5, x, ALU.mult, ALU.add),
                            [wmm], ("DV", 1))
                rdone[("dn", oc)] = wv
                rdone[(oc, tt)] = wv

        def get_r(oc, tc):
            return (xs[:, oc * TOK + tc * 512: oc * TOK + tc * 512 + 512],
                    rdone[(oc, tc)])

        ttile = [ar.b(128, 512) for _ in range(2)]

        def put_out(oc, tc, Aap, Bap, waits):
            xsl = xs[:, oc * TOK + tc * 512: oc * TOK + tc * 512 + 512]
            w1 = P.emit("vector", lambda v, x=xsl, a=Aap, t=ttile[oc % 2]:
                        v.tensor_mul(t[:], x, a), waits, ("DV", 1))
            return P.emit("vector", lambda v, x=xsl, b=Bap, t=ttile[oc % 2]:
                          v.tensor_add(x, t[:], b), [w1], ("DV", 1))
        self.layernorm(get_r, put_out)

    def p5(self):
        P = self.P
        xs = self.sb["xs"]
        wd = ("DV", P.val("DV"))
        for fcc in range(FC):
            self.dma("gpsimd", self.ds["X2_o"][fcc],
                     xs[:, fcc * TOK:(fcc + 1) * TOK], [wd], sem="xxw")

    def p6(self):
        P, ar = self.P, self.ar
        ar.reset()
        self._prevE = {}
        self._wexp_prev = {}
        A, Bp, C, Dp = self.psA, self.psB, self.psC, self.psD
        wq1h = ar.b(128, FC * 512)
        wk1b = ar.b(128, FC * 256)
        wv1b = ar.b(128, FC * 256)
        xf = [[ar.b(128, 512) for _ in range(8)] for _ in range(2)]
        k1 = [ar.b(128, T) for _ in range(2)]
        v1 = ar.b(128, 32 * 256)
        qsb = [ar.b(128, 512) for _ in range(4)]
        qat = [ar.b(128, T) for _ in range(2)]
        Eb = [ar.b(128, 1024) for _ in range(4)]
        Aacc = ar.b(128, 512)
        osb = [ar.b(128, 512) for _ in range(2)]
        bsb = [ar.f(128, 512) for _ in range(2)]
        srow = ar.f(1, 512)
        wp = [self.w_poll_x2] if self.fused else []
        self.dma("sync", wk1b.rearrange("p (c m) -> p c m", c=FC),
                 self.ds["b_wk1"].rearrange("(c p) m -> p c m", p=128),
                 [self.wcast["wk1"]], sem="p6w")
        self.dma("sync", wv1b.rearrange("p (c m) -> p c m", c=FC),
                 self.ds["b_wv1"].rearrange("(c p) m -> p c m", p=128),
                 [self.wcast["wv1"]], sem="p6w")
        xsrc = self.ds["X2_i"]
        wxf_prev = [None, None]
        for pas in range(3):
            if pas < 2:
                wwq = self.dma(
                    "sync", wq1h.rearrange("p (c m) -> p c m", c=FC),
                    self.ds["b_wq1"][:, pas * 512:(pas + 1) * 512]
                    .rearrange("(c p) m -> p c m", p=128),
                    [self.wcast["wq1"]] +
                    ([("MM", P.val("MM"))] if pas > 0 else []), sem="p6q")
            ww = ("p6w", P.val("p6w"))
            for qc in range(8):
                xb = qc % 2
                pss = [A[:, 0:512], A[:, 512:1024], Bp[:, 0:512], Bp[:, 512:1024]]
                wevq = getattr(self, "_p6evq", None)
                wevk = getattr(self, "_p6evk", None)
                for hf in range(2):
                    wld = []
                    for f8_ in range(8):
                        fcc = hf * 8 + f8_
                        wld.append(self.dma(
                            "sync", xf[xb][f8_][:],
                            xsrc[fcc][:, qc * 512:(qc + 1) * 512],
                            wp + ([wxf_prev[xb]] if wxf_prev[xb] else []),
                            sem=f"p6x{xb}"))
                    wldl = (f"p6x{xb}", P.val(f"p6x{xb}"))
                    if pas < 2:
                        for f8_ in range(8):
                            fcc = hf * 8 + f8_
                            for g in range(4):
                                wxf_prev[xb] = self.mm(
                                    pss[g],
                                    wq1h[:, fcc * 512 + g * 128: fcc * 512 + g * 128 + 128],
                                    xf[xb][f8_][:], fcc == 0, fcc == FC - 1,
                                    ([wldl, ww, wwq] +
                                     (list(wevq) if (wevq and fcc == 0) else []))
                                    if f8_ == 0 else (),
                                    ("MM", 1) if (fcc == FC - 1 or f8_ == 7) else None,
                                    skip=True)
                    else:
                        for f8_ in range(8):
                            fcc = hf * 8 + f8_
                            for kvh in range(2):
                                self.mm(Dp[:, kvh * 512: kvh * 512 + 512],
                                        wk1b[:, fcc * 256 + kvh * 128: fcc * 256 + kvh * 128 + 128],
                                        xf[xb][f8_][:], fcc == 0, fcc == FC - 1,
                                        ([wldl, ww] + (list(wevk) if (wevk and fcc == 0) else []))
                                        if f8_ == 0 else (),
                                        ("MM", 1) if fcc == FC - 1 else None, skip=True)
                            for s in range(4):
                                wxf_prev[xb] = self.mm(
                                    C[:, s * 256:(s + 1) * 256],
                                    xf[xb][f8_][:, s * 128:(s + 1) * 128],
                                    wv1b[:, fcc * 256:(fcc + 1) * 256],
                                    fcc == 0, False, (),
                                    ("MM", 1) if (f8_ == 7 and s == 3) else None,
                                    skip=True)
                if pas < 2:
                    wmm = ("MM", P.val("MM"))
                    wevq = []
                    stq = getattr(self, "_p6stq", [None] * 4)
                    for g in range(4):
                        h = pas * 4 + g
                        wq_ = P.emit("vector", lambda v, g_=g, p=pss[g], h_=h:
                                     v.tensor_scalar(qsb[g_][:], p,
                                                     self.sb["bq1_sb"][:, h_:h_ + 1],
                                                     SC1, ALU.add, ALU.mult),
                                     [wmm] + ([stq[g]] if stq[g] else []), ("DV", 1))
                        wevq.append(wq_)
                        stq[g] = self.dma("sync",
                                          self.ds["Qs"][h][:, qc * 512:(qc + 1) * 512],
                                          qsb[g][:], [wq_], sem=f"qsw{g}")
                    self._p6stq = stq
                    self._p6evq = tuple(wevq)
                else:
                    for s in range(4):
                        self.mm(C[:, s * 256:(s + 1) * 256],
                                self.sb["onesrow_b"][:], self.sb["bv1_sb"][:],
                                False, True, [self.w_const[0]],
                                ("MM", 1) if s == 3 else None, skip=True)
                    wmm = ("MM", P.val("MM"))
                    wevk = []
                    for kvh in range(2):
                        wk_ = P.emit("vector", lambda v, k=kvh, p=Dp[:, kvh * 512: kvh * 512 + 512], q=qc:
                                     v.tensor_scalar(k1[k][:, q * 512:(q + 1) * 512], p,
                                                     self.sb["bk1_sb"][:, k:k + 1],
                                                     None, ALU.add),
                                     [wmm], ("DV", 1))
                        wevk.append(wk_)
                    for s in range(4):
                        tok = qc * 4 + s
                        wv_ = P.emit("vector", lambda v, t=tok, s_=s:
                                     v.tensor_copy(v1[:, t * 256:(t + 1) * 256],
                                                   C[:, s_ * 256:(s_ + 1) * 256]),
                                     [wmm], ("DV", 1))
                        wevk.append(wv_)
                    self._p6evk = tuple(wevk)
                wproj = ("DV", P.val("DV"))
        wq_prev = [None, None]
        oev_half = [None, None]
        for kvh in range(2):
            for g in range(4):
                h = kvh * 4 + g
                hb = h % 2
                wqld = self.dma("sync", qat[hb], self.ds["Qs"][h],
                                [(f"qsw{g_}", P.val(f"qsw{g_}")) for g_ in range(4)] +
                                ([wq_prev[hb]] if wq_prev[hb] else []),
                                sem=f"qal{hb}")
                for qc in range(8):
                    o_ps = C[:, (qc % 2) * 512:(qc % 2) * 512 + 512]
                    wA_last = None
                    prevE = self._prevE
                    wexp_prev = self._wexp_prev
                    pend = []

                    def emit_pv(ent, _oev=oev_half, _qc=qc, _ops=o_ps, _kvh=kvh):
                        kt0_, E_, wE_, eslot_ = ent
                        for sub in range(2):
                            kt_ = kt0_ + sub
                            self.mm(_ops, v1[:, kt_ * 256 + _kvh * 128: kt_ * 256 + _kvh * 128 + 128],
                                    E_[:, sub * 512:(sub + 1) * 512],
                                    kt_ == 0, kt_ == 31,
                                    [wE_] + ([_oev[_qc % 2]] if (kt_ == 0 and _oev[_qc % 2]) else []),
                                    ("MM", 1) if kt_ == 31 else None, skip=True)
                        prevE[eslot_] = tuple(prevE.get(eslot_, ())) + \
                            (("MM", P.val("MM")),)

                    for gi, kt2 in enumerate(range(0, 32, 2)):
                        if len(pend) >= 2:
                            emit_pv(pend.pop(0))
                        sb_ix = gi % 2
                        eslot = gi % 4
                        S2 = (A if sb_ix == 0 else Bp)
                        wS = ([wqld, wproj] if kt2 == 0 and qc == 0 and h == 0 else
                              ([wqld] if kt2 == 0 and qc == 0 else [])) + \
                             list(wexp_prev.get(sb_ix, ()))
                        for sub in range(2):
                            kt = kt2 + sub
                            self.mm(S2[:, sub * 512:(sub + 1) * 512],
                                    k1[kvh][:, kt * 128:(kt + 1) * 128],
                                    qat[hb][:, qc * 512:(qc + 1) * 512],
                                    True, True, wS if sub == 0 else (),
                                    ("MM", 1), skip=True)
                            wS = []
                        wsc = ("MM", P.val("MM"))
                        E = Eb[eslot]
                        wexp = P.emit("scalar", lambda s, e=E, p=S2:
                                      s.activation(e[:], p[:], AF.Exp,
                                                   bias=self.sb["cst"][:, 2:3]),
                                      [wsc, self.w_ones] + list(prevE.get(eslot, ())),
                                      ("AC", 1))
                        prevE[eslot] = ()
                        for sub in range(2):
                            if kt2 == 0 and sub == 0:
                                wA_last = P.emit("vector", lambda v, e=E:
                                                 v.tensor_copy(Aacc[:], e[:, 0:512]),
                                                 [wexp, ("MM", P.val("MM"))], ("DV", 1))
                            else:
                                wA_last = P.emit("vector", lambda v, e=E, s_=sub:
                                                 v.tensor_add(Aacc[:], Aacc[:],
                                                              e[:, s_ * 512:(s_ + 1) * 512]),
                                                 [wexp] + ([wA_last] if wA_last else []),
                                                 ("DV", 1))
                        prevE[eslot] = tuple(prevE.get(eslot, ())) + \
                            (wA_last, ("PO", P.val("PO")))
                        wexp_prev[sb_ix] = (wexp,)
                        pend.append((kt2, E, wexp, eslot))
                    while pend:
                        emit_pv(pend.pop(0))
                    wpv = ("MM", P.val("MM"))
                    hD = qc % 2
                    ws = self.mm(Dp[0:1, hD * 512: hD * 512 + 512],
                                 self.sb["onescol_b"][:], Aacc[:], True, True,
                                 [wA_last, ("DV", P.val("DV"))], ("MM", 1), skip=True)
                    w1 = P.emit("vector", lambda v, d=Dp[0:1, hD * 512: hD * 512 + 512]:
                                v.tensor_copy(srow[:], d), [ws], ("DV", 1))
                    w2 = P.emit("vector", lambda v: v.reciprocal(srow[:], srow[:]),
                                [w1], ("DV", 1))
                    wb = self.mm(Dp[:, (1 - hD) * 512:(1 - hD) * 512 + 512],
                                 self.sb["onesrow_f"][:], srow[:], True, True,
                                 [w2], ("MM", 1), skip=True)
                    w3 = P.emit("vector", lambda v, d=Dp[:, (1 - hD) * 512:(1 - hD) * 512 + 512],
                                b=bsb[hD]: v.tensor_copy(b[:], d), [wb], ("DV", 1))
                    ow_prev = getattr(self, "_ow1_prev", [None, None])
                    w4 = P.emit("vector", lambda v, o=osb[hD], b=bsb[hD], p=o_ps:
                                v.tensor_mul(o[:], p, b[:]),
                                [w3, wpv] + ([ow_prev[hD]] if ow_prev[hD] else []),
                                ("DV", 1))
                    oev_half[qc % 2] = w4
                    ow_prev[hD] = self.dma(
                        "gpsimd", self.ds["O1_o"][h][:, qc * 512:(qc + 1) * 512],
                        osb[hD][:], [w4], sem=f"xo1{hD}")
                    self._ow1_prev = ow_prev
                wq_prev[hb] = ("MM", P.val("MM"))

    def p7(self):
        self.p3(wo="wo1", obuf_src="O1_i", ev=EV_ATT1, bo="bo1_sb")

    def p8(self):
        P, ar = self.P, self.ar
        ar.reset()
        A, Bp = self.psA, self.psB
        xs = self.sb["xs"]
        w1b = [ar.b(128, FC * 128) for _ in range(2)]
        rbuf = [ar.b(128, 512) for _ in range(FC)]
        rdone = {}
        ev_half = {}
        for tc in range(4):
            for oc in range(FC):
                hb = oc % 2
                slot = (hb, tc % 2)
                wld = self.dma("sync",
                               w1b[hb].rearrange("p (c m) -> p c m", c=FC),
                               self.ds["b_w1t"][:, oc * 128:(oc + 1) * 128]
                               .rearrange("(c p) m -> p c m", p=128),
                               [self.wcast["w1t"]] +
                               ([("MM", P.val("MM"))] if (tc > 0 or oc >= 2) else []),
                               sem=f"p8w{hb}")
                ps = (A if hb == 0 else Bp)[:, (tc % 2) * 512:(tc % 2) * 512 + 512]
                w0 = [wld] + ([ev_half[slot]] if ev_half.get(slot) else [])
                for ic in range(FC):
                    self.mm(ps, w1b[hb][:, ic * 128:(ic + 1) * 128],
                            xs[:, ic * TOK + tc * 512: ic * TOK + tc * 512 + 512],
                            ic == 0, ic == FC - 1, w0 if ic == 0 else (),
                            ("MM", 1) if ic == FC - 1 else None)
                wmm = ("MM", P.val("MM"))
                wv = P.emit("vector", lambda v, p=ps, oc_=oc,
                            x=xs[:, oc * TOK + tc * 512: oc * TOK + tc * 512 + 512],
                            r=rbuf[oc]:
                            v.scalar_tensor_tensor(r[:], p,
                                                   self.sb["fb1_sb"][:, oc_:oc_ + 1],
                                                   x, ALU.add, ALU.add),
                            [wmm], ("DV", 1))
                ev_half[slot] = wv
                rdone[(oc, tc)] = wv

            def get_r(oc, tc2, _tc=tc):
                return (rbuf[oc][:], rdone[(oc, _tc)])

            obf = [ar.f(128, 512) for _ in range(2)] if tc == 0 else self._p8obf
            self._p8obf = obf

            def put_out(oc, tc2, Aap, Bap, waits, _tc=tc):
                t = obf[oc % 2]
                wprev = getattr(self, "_p8ost", {}).get(oc % 2)
                w1 = P.emit("vector", lambda v, r=rbuf[oc], a=Aap, t_=t:
                            v.tensor_mul(t_[:], r[:], a), list(waits) +
                            ([wprev] if wprev else []), ("DV", 1))
                w2 = P.emit("vector", lambda v, b=Bap, t_=t:
                            v.tensor_add(t_[:], t_[:], b), [w1], ("DV", 1))
                wst = self.dma("sync", self.ds["out"][oc][:, _tc * 512:(_tc + 1) * 512],
                               t[:], [w2], sem=f"out{oc % 2}")
                self._p8ost = {**getattr(self, "_p8ost", {}), oc % 2: wst}
                return w2
            self.layernorm(get_r, put_out, tcs=1)



ALLPH = ["P1", "P2", "P3", "P4", "P5", "P6", "P7", "P8"]
LAUNCHES = [["P1"], ["P2"], ["P3", "P4", "P5"], ["P6"], ["P7", "P8"]]
FUSED = True


def _run(nc, in_maps):
    return run_bass_kernel_spmd(nc, in_maps, core_ids=list(range(NCORE)))


def _pick(maps, names):
    return [{k: m[k] for k in names} for m in maps]


def _assemble_full(per_core, heads_dim, rows, posmap=None):
    out = []
    for b in range(4):
        out.append(np.concatenate([per_core[2 * b], per_core[2 * b + 1]],
                                  axis=-1))
    return out


def kernel(**inputs):
    maps = host_inputs(inputs)
    posmap = [(STRIPE[j][0] * 1024, STRIPE[j][1] * 1024) for j in range(2)]
    x = np.asarray(inputs["L0_attn_hidden_states"], np.float32)

    if FUSED:
        bld = Full3(ALLPH, True)
        nc = bld.build()
        res = _run(nc, [{k: m[k] for k in bld.innames} for m in maps])
        outs = [r["out"] for r in res.results]
    else:
        needed = None
        state = [dict(m) for m in maps]
        for li, ph in enumerate(LAUNCHES):
            bld = Full3(ph, False)
            nc = bld.build()
            inm = [{k: state[c][k] for k in bld.innames} for c in range(NCORE)]
            res = _run(nc, inm)
            for c in range(NCORE):
                for k, v in res.results[c].items():
                    state[c][k] = v
            if ph[-1] == "P1":
                qp = _assemble_full([state[c]["Qp_o"] for c in range(NCORE)],
                                    NH, 128, posmap)
                qr = _assemble_full([state[c]["Qr_o"] for c in range(NCORE)],
                                    NH, 64, posmap)
                lkv = _assemble_full([state[c]["Lkv_o"] for c in range(NCORE)],
                                     4, 128, posmap)
                rt = _assemble_full([state[c]["Rt_o"][None] for c in range(NCORE)],
                                    1, 64, posmap)
                rs = _assemble_full([state[c]["Rstd_o"][None] for c in range(NCORE)],
                                    1, 1, posmap)
                for c in range(NCORE):
                    b, j = divmod(c, 2)
                    hsl = slice(j * MYH, (j + 1) * MYH)
                    state[c]["Qp_i"] = np.ascontiguousarray(qp[b][hsl])
                    state[c]["Qr_i"] = np.ascontiguousarray(qr[b][hsl])
                    state[c]["Lkv_i"] = lkv[b]
                    state[c]["Rt_i"] = rt[b][0]
                    state[c]["Rstd_i"] = rs[b][0]
            if ph[-1] == "P2":
                for b in range(4):
                    ob = np.concatenate([state[2 * b]["O_o"],
                                         state[2 * b + 1]["O_o"]], axis=0)
                    for j in range(2):
                        state[2 * b + j]["O_i"] = np.ascontiguousarray(
                            ob[:, :, j * TOK:(j + 1) * TOK])
            if ph[-1] == "P5":
                x2 = _assemble_full([state[c]["X2_o"] for c in range(NCORE)],
                                    FC, 128, posmap)
                for c in range(NCORE):
                    state[c]["X2_i"] = x2[c // 2]
            if ph[-1] == "P6":
                for b in range(4):
                    ob = np.concatenate([state[2 * b]["O1_o"],
                                         state[2 * b + 1]["O1_o"]], axis=0)
                    for j in range(2):
                        state[2 * b + j]["O1_i"] = np.ascontiguousarray(
                            ob[:, :, j * TOK:(j + 1) * TOK])
            if "stream_out" in (res.results[0] or {}):
                for c in range(NCORE):
                    state[c]["stream_in"] = state[c]["stream_out"]
        outs = [state[c]["out"] for c in range(NCORE)]

    y = np.empty((Bz, T, D), np.float32)
    for c in range(NCORE):
        b, j = divmod(c, 2)
        pA, pB = posmap[j]
        fm = outs[c].reshape(D, TOK)
        y[b, pA:pA + 1024] = fm[:, 0:1024].T
        y[b, pB:pB + 1024] = fm[:, 1024:2048].T
    return y


# version 07
# speedup vs baseline: 1.2663x; 1.0988x over previous
import numpy as np
import contextlib
import secrets
from collections import defaultdict
import ml_dtypes
import concourse.bass as bass
import concourse.mybir as mybir
from concourse.bass_utils import run_bass_kernel_spmd

F32 = mybir.dt.float32
BF16 = mybir.dt.bfloat16
FP8 = mybir.dt.float8e4
I32 = mybir.dt.int32
AF = mybir.ActivationFunctionType
ALU = mybir.AluOpType
AX = mybir.AxisListType.X

NCORE = 8
Bz, T, D = 4, 4096, 2048
FC = 16
TOK = 2048
NH, MYH = 16, 8
DN, DR, DQK, DV = 128, 64, 192, 128
QL, KVL = 1536, 512
HID = 8192
STRIPE = [(0, 3), (1, 2)]
SHIFT = 12.0
SC0 = 1.0 / float(np.sqrt(192.0))
SC1 = 1.0 / float(np.sqrt(128.0))
GC1 = 0.7978845608028654
GC2 = 0.044715
EPSR = 1e-6
EPSL = 1e-5

ENGS = ("tensor", "vector", "scalar", "gpsimd", "sync")
QSEM = {"gpsimd": "DG", "sync": "DS", "scalar": "DA"}


class Prog:

    def __init__(self):
        self.ops = {e: [] for e in ENGS}
        self.tot = defaultdict(int)
        self.semnames = set(["MM", "DV", "AC", "PO", "PL"])

    def emit(self, eng, fn, waits=(), inc=None):
        for s, _ in waits:
            self.semnames.add(s)
        self.ops[eng].append((fn, tuple(waits), inc))
        if inc is not None:
            self.semnames.add(inc[0])
            self.tot[inc[0]] += inc[1]
        return (inc[0], self.tot[inc[0]]) if inc is not None else None

    def val(self, sem):
        return self.tot[sem]

    def barrier_waits(self):
        return tuple((s, self.tot[s]) for s in sorted(self.tot)
                     if self.tot[s] > 0 and s != "PL")

    def barrier(self, name=""):
        w = self.barrier_waits()
        for e in ENGS:
            self.emit(e, lambda en: en.nop(), w)


def _rope_tables():
    inv = 1.0 / (10000.0 ** (np.arange(0, DR, 2, dtype=np.float64) / DR))
    t = np.arange(T, dtype=np.float64)
    fr = np.outer(inv, t)
    return np.cos(fr), np.sin(fr)


def _vec_tile(v, cols):
    return np.ascontiguousarray(v.reshape(cols, 128).T).astype(np.float32)


def host_inputs(inputs):
    x = np.asarray(inputs["L0_attn_hidden_states"], np.float32)
    cos, sin = _rope_tables()
    nonce = secrets.randbits(20) + 1
    f32 = lambda a: np.ascontiguousarray(a, dtype=np.float32)
    com = {
        "wqa": f32(inputs["L0_attn_Wq_a"]),
        "wkva": f32(inputs["L0_attn_Wkv_a"]),
        "wkr": f32(inputs["L0_attn_Wk_rope"]),
        "wo0": f32(inputs["L0_attn_Wo"]),
        "wg0": f32(inputs["L0_ffn_w_gating"][0]),
        "wg1": f32(inputs["L0_ffn_w_gating"][1]),
        "wlin": f32(inputs["L0_ffn_w_linear"]),
        "wo1": f32(inputs["L1_attn_Wo"]),
        "w1t": f32(np.asarray(inputs["L1_ffn_weight"]).T),
        "cosq": f32(cos * SC0), "sinq": f32(sin * SC0),
        "cosk": f32(cos), "sink": f32(sin),
        "qnw": _vec_tile(f32(inputs["L0_attn_q_norm_w"]), 12),
        "kvnw": _vec_tile(f32(inputs["L0_attn_kv_norm_w"]), 4),
        "bo1": _vec_tile(f32(inputs["L1_attn_bo"]), 16),
        "fb1": _vec_tile(f32(inputs["L1_ffn_bias"]), 16),
    }
    wqb = f32(inputs["L0_attn_Wq_b"])
    wkvb = f32(inputs["L0_attn_Wkv_b"])
    wq1 = f32(inputs["L1_attn_Wq"])
    wk1 = f32(inputs["L1_attn_Wk"])
    wv1 = f32(inputs["L1_attn_Wv"])
    bq1 = f32(inputs["L1_attn_bq"])
    bk1 = f32(inputs["L1_attn_bk"])
    bv1 = f32(inputs["L1_attn_bv"])
    maps = []
    for c in range(NCORE):
        b, j = divmod(c, 2)
        sA, sB = STRIPE[j]
        posA, posB = sA * 1024, sB * 1024
        xc = np.concatenate([x[b, posA:posA + 1024], x[b, posB:posB + 1024]], 0)
        hs = slice(j * MYH, j * MYH + MYH)
        m = dict(com)
        m["x_fm"] = np.ascontiguousarray(xc.T)
        m["wqb"] = wqb
        m["wkvb"] = np.ascontiguousarray(
            wkvb.reshape(KVL, NH, DN + DV)[:, hs].reshape(KVL, MYH * (DN + DV)))
        m["tabq"] = np.ascontiguousarray(np.concatenate(
            [com["cosq"][:, posA:posA + 1024], com["cosq"][:, posB:posB + 1024],
             com["sinq"][:, posA:posA + 1024], com["sinq"][:, posB:posB + 1024]],
            axis=1).reshape(32, 2, 2048).transpose(1, 0, 2).reshape(64, 2048))
        m["tabk"] = np.ascontiguousarray(np.concatenate(
            [com["cosk"][:, posA:posA + 1024], com["cosk"][:, posB:posB + 1024],
             com["sink"][:, posA:posA + 1024], com["sink"][:, posB:posB + 1024]],
            axis=1).reshape(32, 2, 2048).transpose(1, 0, 2).reshape(64, 2048))
        m["wq1"] = np.ascontiguousarray(
            wq1.reshape(D, NH, DV)[:, hs].reshape(D, MYH * DV))
        m["wk1"] = np.ascontiguousarray(wk1[:, j * 256:(j + 1) * 256])
        m["wv1"] = np.ascontiguousarray(wv1[:, j * 256:(j + 1) * 256])
        m["bq1"] = _vec_tile(bq1.reshape(NH, DV)[hs].reshape(-1), 8)
        m["bk1"] = _vec_tile(bk1[j * 256:(j + 1) * 256], 2)
        m["bv1"] = np.ascontiguousarray(
            bv1[j * 256:(j + 1) * 256].reshape(1, 256)).astype(np.float32)
        for k in ("cosq", "sinq", "cosk", "sink"):
            m.pop(k)
        meta = np.zeros((1, 16), np.int32)
        meta[0, 0] = posA
        meta[0, 1] = posB
        meta[0, 2] = j * 8
        meta[0, 3] = (1 - j) * 8
        meta[0, 4] = nonce
        meta[0, 5] = j * MYH * SQPs
        meta[0, 6] = j
        m["meta"] = meta
        m["flagv"] = (nonce + np.arange(1, 9, dtype=np.int64)).astype(np.int32).reshape(1, 8)
        maps.append(m)
    return maps



SQPs = 128 * 4096
SQRs = 64 * 4096
PERM = [STRIPE[0][0], STRIPE[0][1], STRIPE[1][0], STRIPE[1][1]]


class Builder:

    def __init__(self, phases, fused):
        self.ph = set(phases)
        self.fused = fused
        self.nc = bass.Bass("TRN2", target_bir_lowering=False)
        self.P = Prog()
        self.es = contextlib.ExitStack()
        self.sb = {}
        self.ds = {}
        self.first = phases[0]
        self.last = phases[-1]
        self.innames = set()
        self.semq = {}

    def din(self, name, shape, dtype=F32):
        self.ds[name] = self.nc.dram_tensor(name, shape, dtype, kind="ExternalInput").ap()
        self.innames.add(name)
        return self.ds[name]

    def dout(self, name, shape, dtype=F32):
        self.ds[name] = self.nc.dram_tensor(name, shape, dtype, kind="ExternalOutput").ap()
        return self.ds[name]

    def dshared(self, name, shape, dtype):
        self.ds[name] = self.nc.dram_tensor(name, shape, dtype, addr_space="Shared").ap()
        return self.ds[name]

    def dlocal(self, name, shape, dtype):
        self.ds[name] = self.nc.dram_tensor(name, shape, dtype).ap()
        return self.ds[name]

    def S(self, name, shape, dtype):
        t = self.es.enter_context(self.nc.sbuf_tensor(name, shape, dtype))
        self.sb[name] = t
        return t

    def mm(self, out_ap, lhsT, rhs, start, stop, w=(), inc=None, skip=False):
        return self.P.emit("tensor",
                           lambda t, o=out_ap, a=lhsT, b=rhs, s1=start, s2=stop, sk=skip:
                           t.matmul(o, a, b, start=s1, stop=s2, skip_group_check=sk),
                           w, inc)

    def dma(self, q, out_ap, in_ap, w=(), sem=None):
        sem = sem or QSEM[q]
        assert self.semq.setdefault(sem, q) == q, f"sem {sem} on two queues"
        return self.P.emit(q, lambda e, o=out_ap, i=in_ap: e.dma_start(out=o, in_=i),
                           w, (sem, 16))

    def dyn_dma(self, dram_ap, mult, const, pattern, sbuf_or_dram, w, sem,
                is_out):
        assert self.semq.setdefault(sem, "sync") == "sync"
        def fn(y, dt=dram_ap, m=mult, co=const, pat=pattern, sap=sbuf_or_dram,
               out=is_out):
            y.reg_mul(REGS2["rT"], REGS2["rJ"], m)
            y.reg_add(REGS2["rT"], REGS2["rT"], co)
            dap = bass.AP(dt.tensor, REGS2["rT"], pat)
            if out:
                return y.dma_start(out=dap, in_=sap)
            return y.dma_start(out=sap, in_=dap)
        return self.P.emit("sync", fn, w, (sem, 16))

    def flag_write(self, event, w=()):
        if not self.fused:
            return
        assert self.semq.setdefault("flg", "sync") == "sync"
        fl = self.ds["SFlag"]
        def fn(y, e=event, fl=fl):
            y.reg_mul(REGS2["rT"], REGS2["rJ"], 8)
            y.reg_add(REGS2["rT"], REGS2["rT"], e)
            return y.dma_start(out=bass.AP(fl.tensor, REGS2["rT"], [[1, 1], [1, 1]]),
                               in_=self.sb["flagv_s"][0:1, e:e + 1])
        self.P.emit("sync", fn, w, ("flg", 16))

    def flag_poll(self, event):
        if not self.fused:
            return None
        fl = self.ds["SFlag"]
        def fn(g, e=event, fl=fl):
            g.reg_add(REGS["rT"], REGS["rFR"], e)
            g.reg_add(REGS["rExp"], REGS["rNon"], e + 1)
            g.reg_mov(REGS["rC"], 1)
            with g.While(REGS["rC"]):
                g.reg_add(REGS["rP"], REGS["rP"], 16)
                g.dma_start(out=self.sb["flin"][0:1, 0:1],
                            in_=bass.AP(fl.tensor, REGS["rT"], [[1, 1], [1, 1]])
                            ).then_inc(SEMH["PL"], 16)
                g.wait_ge(SEMH["PL"], REGS["rP"])
                g.reg_load(REGS["rF"], self.sb["flin"][0:1, 0:1])
                g.reg_alu(REGS["rC"], REGS["rF"], REGS["rExp"], ALU.not_equal)
            return g.nop()
        return self.P.emit("gpsimd", fn, (), ("PO", 1))


REGS = {}
REGS2 = {}
SEMH = {}


EV_L0, EV_ATT0, EV_X2, EV_ATT1 = 0, 1, 2, 3


class Arena:

    def __init__(self, handle, nbytes):
        self.h = handle
        self.hf = handle.bitcast(F32)
        self.h8 = handle.bitcast(FP8)
        self.nbytes = nbytes
        self.off = 0

    def reset(self):
        self.off = getattr(self, "floor", 0)

    def _take(self, nb):
        nb = (nb + 63) & ~63
        assert self.off + nb <= self.nbytes, f"arena overflow {self.off}+{nb}"
        o = self.off
        self.off += nb
        return o

    def b(self, parts, cols):
        o = self._take(cols * 2)
        return self.h[0:parts, o // 2: o // 2 + cols]

    def f(self, parts, cols):
        o = self._take(cols * 4)
        return self.hf[0:parts, o // 4: o // 4 + cols]

    def e8(self, parts, cols):
        o = self._take(cols)
        return self.h8[0:parts, o: o + cols]


def _replay(nc, prog):
    sems = {}
    es = contextlib.ExitStack()
    with es:
        for s in sorted(prog.semnames):
            sems[s] = es.enter_context(nc.semaphore(s))
        SEMH.clear()
        SEMH.update(sems)
        block = es.enter_context(nc.Block())

        def run(engname, e):
            for fn, waits, inc in prog.ops[engname]:
                for s, v in waits:
                    e.wait_ge(sems[s], v)
                ins = fn(e)
                if inc is not None:
                    ins.then_inc(sems[inc[0]], inc[1])

        @block.tensor
        def _(t):
            run("tensor", t)

        @block.vector
        def _(v):
            run("vector", v)

        @block.scalar
        def _(s):
            run("scalar", s)

        @block.sync
        def _(y):
            es3 = contextlib.ExitStack()
            with es3:
                REGS2.clear()
                for rn in ("rJ", "rT"):
                    REGS2[rn] = es3.enter_context(y.register("s" + rn))
                run("sync", y)

        @block.gpsimd
        def _(g):
            rnames = ("rJ", "rFW", "rFR", "rNon", "rT", "rF", "rC", "rP", "rExp")
            regs = [es2 := contextlib.ExitStack()]
            with es2:
                REGS.clear()
                for rn in rnames:
                    REGS[rn] = es2.enter_context(g.register(rn))
                run("gpsimd", g)
    return nc


class Phases(Builder):
    def p0(self):
        P, S = self.P, self.S
        nc = self.nc
        self.arena_t = S("arena", [128, 67584], BF16)
        self.ar = Arena(self.arena_t, 135168)
        xs = S("xs", [128, FC * TOK], BF16)
        mt = S("mt", [1, 16], I32)
        S("flagv_s", [1, 8], I32)
        S("flin", [1, 1], I32)
        S("onescol_b", [128, 1], BF16)
        S("onesrow_f", [1, 128], F32)
        S("onesrow_b", [1, 128], BF16)
        S("cst", [128, 4], F32)
        S("onescol_f", [128, 1], F32)
        S("qnw_sb", [128, 12], F32)
        S("kvnw_sb", [128, 4], F32)
        S("bo1_sb", [128, 16], F32)
        S("fb1_sb", [128, 16], F32)
        S("bq1_sb", [128, 8], F32)
        S("bk1_sb", [128, 2], F32)
        S("bv1_sb", [1, 256], BF16)
        if "P1" in self.ph:
            self._tabq = self.ar.b(64, TOK)
            self._tabk = self.ar.b(64, TOK)
            self.sb["tabq_s"] = self._tabq
            self.sb["tabk_s"] = self._tabk
            self.ar.floor = self.ar.off
        S("rstdkv", [1, TOK], F32)
        wm = self.dma("gpsimd", mt[:], self.ds["meta"], sem="cG")
        self.dma("gpsimd", self.sb["flagv_s"][:], self.ds["flagv"], sem="cG")
        wf = ("cG", P.val("cG"))

        def initregs(g):
            g.reg_load(REGS["rJ"], mt[0:1, 6:7])
            g.reg_load(REGS["rFW"], mt[0:1, 2:3])
            g.reg_load(REGS["rFR"], mt[0:1, 3:4])
            g.reg_load(REGS["rNon"], mt[0:1, 4:5])
            g.reg_mov(REGS["rP"], 0)
            return g.nop()
        P.emit("gpsimd", initregs, [wf], ("PO", 1))

        def initregs_sync(y):
            y.reg_load(REGS2["rJ"], mt[0:1, 6:7])
            return y.nop()
        P.emit("sync", initregs_sync, [wf], ("PO", 1))
        if self.first == "P1":
            for f in range(FC):
                self.dma("gpsimd", xs[:, f * TOK:(f + 1) * TOK],
                         self.ds["x_fm"][f * 128:(f + 1) * 128, :], sem="cG")
            self.w_stream = None
        else:
            si = self.ds["stream_in"]
            for f in range(FC):
                self.dma("sync", xs[:, f * TOK:(f + 1) * TOK], si[f], sem="cS")
        for n, srcn in (("qnw_sb", "qnw"), ("kvnw_sb", "kvnw"), ("bo1_sb", "bo1"),
                        ("fb1_sb", "fb1"), ("bq1_sb", "bq1"), ("bk1_sb", "bk1")):
            self.dma("sync", self.sb[n][:], self.ds[srcn], sem="cS")
        self.dma("gpsimd", self.sb["bv1_sb"][:], self.ds["bv1"], sem="cG")
        if "P1" in self.ph:
            self.dma("gpsimd", self._tabq, self.ds["tabq"], sem="cG")
            self.dma("gpsimd", self._tabk, self.ds["tabk"], sem="cG")
        pass
        P.emit("vector", lambda v: v.memset(self.sb["onescol_b"][:], 1.0), (), ("DV", 1))
        P.emit("vector", lambda v: v.memset(self.sb["onesrow_f"][:], 1.0), (), ("DV", 1))
        P.emit("vector", lambda v: v.memset(self.sb["onesrow_b"][:], 1.0), (), ("DV", 1))
        P.emit("vector", lambda v: v.memset(self.sb["onescol_f"][:], 1.0), (), ("DV", 1))
        P.emit("vector", lambda v: v.memset(self.sb["cst"][:, 0:1], EPSR), (), ("DV", 1))
        P.emit("vector", lambda v: v.memset(self.sb["cst"][:, 1:2], EPSL), (), ("DV", 1))
        P.emit("vector", lambda v: v.memset(self.sb["cst"][:, 2:3], -SHIFT), (), ("DV", 1))
        self.w_ones = ("DV", P.val("DV"))
        if "P1" in self.ph:
            self.wkr_load()
        self.w_cG = ("cG", P.val("cG"))
        self.w_cS = ("cS", P.val("cS"))
        self.w_stream = self.w_cG if self.first == "P1" else self.w_cS
        self.w_const = [self.w_cG, self.w_cS]
        self.w_krw = self.w_cG

    def prep_weights(self):
        names = []
        if "P1" in self.ph:
            names += ["wqa", "wkva", "wqb"]
        if "P2" in self.ph:
            names += ["wkvb"]
        if "P3" in self.ph:
            names += ["wo0"]
        if "P6" in self.ph:
            names += ["wq1", "wk1", "wv1"]
        if "P7" in self.ph:
            names += ["wo1"]
        if "P8" in self.ph:
            names += ["w1t"]
        self.wcast = {}
        for n in names:
            shp = list(self.ds[n].shape)
            b = self.dlocal("b_" + n, shp, BF16)
            nch = max(1, (shp[0] * shp[1]) // (1 << 21))
            rows = shp[0] // nch
            for r in range(nch):
                self.dma("gpsimd", b[r * rows:(r + 1) * rows, :],
                         self.ds[n][r * rows:(r + 1) * rows, :], (),
                         sem="c_" + n)
            self.wcast[n] = ("c_" + n, self.P.val("c_" + n))

    def prep_ffn0_scratch(self):
        P = self.P
        for srcn, dstn, nr in (("wg0", "w8g0", 4), ("wg1", "w8g1", 4),
                               ("wlin", "w8lin", 4)):
            dst = self.ds[dstn]
            rows = dst.shape[0] // nr
            for r in range(nr):
                self.dma("gpsimd", dst[r * rows:(r + 1) * rows, :],
                         self.ds[srcn][r * rows:(r + 1) * rows, :], (), sem="s8")
        self.w_scr = ("s8", P.val("s8"))

    def p1(self):
        P, ar = self.P, self.ar
        ar.reset()
        xs = self.sb["xs"]
        A, Bp, C, Dp = self.psA, self.psB, self.psC, self.psD
        latq = [ar.b(128, TOK) for _ in range(12)]
        Bq = ar.f(128, TOK)
        wqa = [ar.b(128, FC * 128) for _ in range(2)]
        wkva = [ar.b(128, FC * 128)]
        wkva.append(wkva[0])
        wqb = [ar.b(128, 12 * DQK) for _ in range(2)]
        wkr = self.sb["wkrt"]
        sq = [ar.b(128, 512) for _ in range(2)]
        lkv = [ar.b(128, 512) for _ in range(2)]
        qp = [ar.b(128, TOK) for _ in range(2)]
        qr = [ar.b(64, TOK) for _ in range(2)]
        krt = [ar.b(64, 512) for _ in range(2)]
        ropf = [ar.f(32, 512) for _ in range(3)]
        rows = ar.f(1, 4 * 512)
        ev_half = [self.w_ones, self.w_ones]
        st_q = []
        wconst = list(self.w_const)
        for oc in range(12):
            wld = self.dma("sync", wqa[oc % 2].rearrange("p (c m) -> p c m", c=FC),
                           self.ds["b_wqa"][:, oc * 128:(oc + 1) * 128]
                           .rearrange("(c p) m -> p c m", p=128),
                           [self.wcast["wqa"]] +
                           ([("MM", P.val("MM"))] if oc >= 2 else []),
                           sem=f"p1a{oc % 2}")
            for tc in range(4):
                ps = A[:, (tc % 2) * 512:(tc % 2) * 512 + 512]
                w = [wld, self.w_stream, ev_half[tc % 2]]
                for ic in range(FC):
                    self.mm(ps, wqa[oc % 2][:, ic * 128:(ic + 1) * 128],
                            xs[:, ic * TOK + tc * 512: ic * TOK + tc * 512 + 512],
                            ic == 0, ic == FC - 1, w if ic == 0 else (),
                            ("MM", 1) if ic == FC - 1 else None)
                wmm = ("MM", P.val("MM"))
                wv = P.emit("vector", lambda v, o=latq[oc], t=tc, p=ps, oc_=oc:
                            v.tensor_scalar(o[:, t * 512:(t + 1) * 512], p,
                                            self.sb["qnw_sb"][:, oc_:oc_ + 1], None,
                                            ALU.mult),
                            [wmm] + (wconst if oc == 0 else []), ("DV", 1))
                ev_half[tc % 2] = wv
                wa = P.emit("scalar", lambda s, q=sq[(oc * 4 + tc) % 2], o=latq[oc], t=tc:
                            s.activation(q[:], o[:, t * 512:(t + 1) * 512], AF.Square),
                            [wv], ("AC", 1))
                stat = (C if tc < 2 else Dp)[0:1, (tc % 2) * 512:(tc % 2) * 512 + 512]
                ws = self.mm(stat, self.sb["onescol_b"][:],
                             sq[(oc * 4 + tc) % 2][:], oc == 0, oc == 11,
                             [wa, self.w_ones], ("MM", 1) if oc == 11 else None,
                             skip=True)
                if oc == 11:
                    st_q.append(("MM", P.val("MM")))
        wBq = []
        for tc in range(4):
            stat = (C if tc < 2 else Dp)[0:1, (tc % 2) * 512:(tc % 2) * 512 + 512]
            r_sl = rows[0:1, tc * 512:(tc + 1) * 512]
            w1 = P.emit("vector", lambda v, s=stat, r=r_sl:
                        v.tensor_scalar(r, s, 1.0 / QL, None, ALU.mult),
                        [st_q[tc]], ("DV", 1))
            w2 = P.emit("scalar", lambda s, r=r_sl:
                        s.activation(r, r, AF.Sqrt, bias=self.sb["cst"][0:1, 0:1]), [w1], ("AC", 1))
            w3 = P.emit("vector", lambda v, r=r_sl: v.reciprocal(r, r), [w2], ("DV", 1))
            wb = self.mm(A[:, (tc % 2) * 512:(tc % 2) * 512 + 512],
                         self.sb["onesrow_f"][:], r_sl, True, True,
                         [w3, ev_half[tc % 2], self.w_ones], ("MM", 1))
            wc = P.emit("vector", lambda v, t=tc, p=A[:, (tc % 2) * 512:(tc % 2) * 512 + 512]:
                        v.tensor_copy(Bq[:, t * 512:(t + 1) * 512], p), [wb], ("DV", 1))
            ev_half[tc % 2] = wc
            wBq.append(wc)
        self.wBq = wBq
        st_kv = []
        ev_bhalf = [self.w_ones, self.w_ones]
        lkv_done = [None, None]
        for oc in range(4):
            wld = self.dma("sync", wkva[oc % 2].rearrange("p (c m) -> p c m", c=FC),
                           self.ds["b_wkva"][:, oc * 128:(oc + 1) * 128]
                           .rearrange("(c p) m -> p c m", p=128),
                           [self.wcast["wkva"]] +
                           ([("MM", P.val("MM"))] if oc >= 1 else []),
                           sem="p1b")
            for tc in range(4):
                ps = Bp[:, (tc % 2) * 512:(tc % 2) * 512 + 512]
                w = [wld, ev_bhalf[tc % 2]]
                for ic in range(FC):
                    self.mm(ps, wkva[oc % 2][:, ic * 128:(ic + 1) * 128],
                            xs[:, ic * TOK + tc * 512: ic * TOK + tc * 512 + 512],
                            ic == 0, ic == FC - 1, w if ic == 0 else (),
                            ("MM", 1) if ic == FC - 1 else None)
                wmm = ("MM", P.val("MM"))
                i2 = (oc * 4 + tc) % 2
                wlk = lkv_done[i2] or ()
                wv = P.emit("vector", lambda v, o=lkv[i2], p=ps, oc_=oc:
                            v.tensor_scalar(o[:], p, self.sb["kvnw_sb"][:, oc_:oc_ + 1],
                                            None, ALU.mult),
                            [wmm] + ([wlk] if wlk else []), ("DV", 1))
                ev_bhalf[tc % 2] = wv
                lkv_done[i2] = self.exch_lkv_write(oc, tc, lkv[i2], [wv], i2)
                wa = P.emit("scalar", lambda s, q=sq[i2], o=lkv[i2]:
                            s.activation(q[:], o[:], AF.Square), [wv], ("AC", 1))
                stat = (C if tc < 2 else Dp)[0:1, (tc % 2) * 512:(tc % 2) * 512 + 512]
                ws = self.mm(stat, self.sb["onescol_b"][:], sq[i2][:],
                             oc == 0, oc == 3, [wa], ("MM", 1) if oc == 3 else None,
                             skip=True)
                if oc == 3:
                    st_kv.append(("MM", P.val("MM")))
        for tc in range(4):
            stat = (C if tc < 2 else Dp)[0:1, (tc % 2) * 512:(tc % 2) * 512 + 512]
            r_sl = self.sb["rstdkv"][0:1, tc * 512:(tc + 1) * 512]
            w1 = P.emit("vector", lambda v, s=stat, r=r_sl:
                        v.tensor_scalar(r, s, 1.0 / KVL, None, ALU.mult),
                        [st_kv[tc]], ("DV", 1))
            w2 = P.emit("scalar", lambda s, r=r_sl:
                        s.activation(r, r, AF.Sqrt, bias=self.sb["cst"][0:1, 0:1]), [w1], ("AC", 1))
            w3 = P.emit("vector", lambda v, r=r_sl: v.reciprocal(r, r), [w2], ("DV", 1))
            self.exch_rstd_write(tc, r_sl, [w3])
        kr_done = [None, None]
        for tc in range(4):
            ps = Bp[:, (tc % 2) * 512:(tc % 2) * 512 + 512][0:64, :]
            for ic in range(FC):
                self.mm(ps, wkr[:, ic * DR:(ic + 1) * DR],
                        xs[:, ic * TOK + tc * 512: ic * TOK + tc * 512 + 512],
                        ic == 0, ic == FC - 1,
                        [ev_bhalf[tc % 2], self.w_krw] if ic == 0 else (),
                        ("MM", 1) if ic == FC - 1 else None)
            wmm = ("MM", P.val("MM"))
            wrp = self.rope(ps, krt[tc % 2][:], "tabk_s", tc, None, ropf,
                            [wmm] + ([kr_done[tc % 2]] if kr_done[tc % 2] else []))
            kr_done[tc % 2] = self.exch_krot_write(tc, krt[tc % 2], [wrp])
            ev_bhalf[tc % 2] = wrp
        for h in range(NH):
            wld = self.dma("sync", wqb[h % 2].rearrange("p (c m) -> p c m", c=12),
                           self.ds["b_wqb"][:, h * DQK:(h + 1) * DQK]
                           .rearrange("(c p) m -> p c m", p=128),
                           [self.wcast["wqb"]] +
                           ([("MM", P.val("MM"))] if h >= 2 else []),
                           sem=f"p1c{h % 2}")
            qp_done = getattr(self, "_qp_done", [None, None])
            ev_rot = getattr(self, "_ev_rot", [("DV", P.val("DV")), ("DV", P.val("DV"))])
            wq_evs = []
            for tc in range(4):
                psp = A[:, (tc % 2) * 512:(tc % 2) * 512 + 512]
                psr = C[0:64, (tc % 2) * 512:(tc % 2) * 512 + 512]
                wstart = [wld, ev_half[tc % 2]] + ([qp_done[h % 2]] if qp_done[h % 2] else [])
                for lc in range(12):
                    self.mm(psp, wqb[h % 2][:, lc * DQK: lc * DQK + DN],
                            latq[lc][:, tc * 512:(tc + 1) * 512],
                            lc == 0, lc == 11, wstart if lc == 0 else (),
                            ("MM", 1) if lc == 11 else None)
                wp = ("MM", P.val("MM"))
                for lc in range(12):
                    self.mm(psr, wqb[h % 2][:, lc * DQK + DN: lc * DQK + DQK],
                            latq[lc][:, tc * 512:(tc + 1) * 512],
                            lc == 0, lc == 11,
                            [ev_rot[tc % 2]] if lc == 0 else (),
                            ("MM", 1) if lc == 11 else None)
                wr = ("MM", P.val("MM"))
                we1 = P.emit("vector", lambda v, o=qp[h % 2], t=tc, p=psp:
                             v.scalar_tensor_tensor(o[:, t * 512:(t + 1) * 512], p, SC0,
                                                    Bq[:, t * 512:(t + 1) * 512],
                                                    ALU.mult, ALU.mult),
                             [wp] + wBq[tc:tc + 1], ("DV", 1))
                ev_half[tc % 2] = we1
                we2 = self.rope(psr, qr[h % 2][:, tc * 512:(tc + 1) * 512],
                                "tabq_s", tc, Bq[0:32, tc * 512:(tc + 1) * 512],
                                ropf, [wr])
                ev_rot[tc % 2] = we2
                wq_evs += [we1, we2]
            dn = self.exch_q_write(h, qp[h % 2], qr[h % 2], wq_evs)
            qp_done[h % 2] = dn
            self._qp_done = qp_done
            self._ev_rot = ev_rot


    def rope(self, ps, out512, tab, tc, Bsl, ropf, w0):
        P = self.P
        t = self.sb[tab]
        c_sl = t[0:32, tc * 512:(tc + 1) * 512]
        s_sl = t[32:64, tc * 512:(tc + 1) * 512]
        p1, p2 = ps[0:32, :], ps[32:64, :]
        a, b, d = ropf
        ch = getattr(self, "_rope_ch", None)
        wl = None
        for half in range(2):
            w0h = list(w0) if half == 0 else []
            if ch:
                w0h.append(ch)
            ch = P.emit("vector", lambda v, p=(p1 if half == 0 else p2), c=c_sl:
                        v.tensor_mul(a[:], p, c), w0h, ("DV", 1))
            ch = P.emit("vector", lambda v, p=(p2 if half == 0 else p1), s=s_sl:
                        v.tensor_mul(b[:], p, s), [ch], ("DV", 1))
            op = ALU.subtract if half == 0 else ALU.add
            tgt = out512[0:32, :] if half == 0 else out512[32:64, :]
            if Bsl is None:
                ch = P.emit("vector", lambda v, o=tgt, oo=op:
                            v.tensor_tensor(o, a[:], b[:], oo), [ch], ("DV", 1))
            else:
                ch = P.emit("vector", lambda v, oo=op: v.tensor_tensor(d[:], a[:], b[:], oo),
                            [ch], ("DV", 1))
                ch = P.emit("vector", lambda v, o=tgt:
                            v.tensor_mul(o, d[:], Bsl), [ch], ("DV", 1))
            wl = ch
        self._rope_ch = ch
        return wl



class Full(Phases):

    def decl_exchange(self):
        pv = (lambda n, s, d: self.dout(n, s, d)) if not self.fused else \
             (lambda n, s, d: self.dlocal(n, s, d))
        pr = (lambda n, s, d: self.din(n, s, d)) if not self.fused else \
             (lambda n, s, d: self.dlocal(n, s, d))
        if "P1" in self.ph:
            pv("Qp_o", [NH, 128, TOK], BF16)
            pv("Qr_o", [NH, 64, TOK], BF16)
            pv("Lkv_o", [4, 128, TOK], BF16)
            pv("Rt_o", [64, TOK], BF16)
            pv("Rstd_o", [1, TOK], F32)
        if "P2" in self.ph:
            pr("Qp_i", [MYH, 128, T], BF16)
            pr("Qr_i", [MYH, 64, T], BF16)
            if not self.fused:
                self.din("Lkv_i", [4, 128, T], BF16)
                self.din("Rt_i", [64, T], BF16)
                self.din("Rstd_i", [1, T], F32)
            pv("O_o", [MYH, 128, T], BF16)
        if "P3" in self.ph:
            pr("O_i", [NH, 128, TOK], BF16)
        if "P5" in self.ph:
            pv("X2_o", [FC, 128, TOK], BF16)
        if "P6" in self.ph:
            if not self.fused:
                self.din("X2_i", [FC, 128, T], BF16)
            pv("O1_o", [MYH, 128, T], BF16)
        if "P7" in self.ph:
            pr("O1_i", [NH, 128, TOK], BF16)
        if self.fused:
            self.dshared("SQp", [NH, 128, T], BF16)
            self.dshared("SQr", [NH, 64, T], BF16)
            self.dshared("SLkv", [4, 128, T], BF16)
            self.dshared("SRt", [64, T], BF16)
            self.dshared("SRstd", [1, T], F32)
            self.dshared("So", [NH, 128, T], BF16)
            self.dshared("Sx", [FC, 128, T], BF16)
            self.dshared("SFlag", [16, 1], I32)
            self.ds["Lkv_i"] = self.ds["SLkv"]
            self.ds["Rt_i"] = self.ds["SRt"]
            self.ds["Rstd_i"] = self.ds["SRstd"]
            self.ds["X2_i"] = self.ds["Sx"]

    def exch_q_write(self, h, qp, qr, waits):
        sem = f"xqw{h % 2}"
        self.dma("gpsimd", self.ds["Qp_o"][h], qp, waits, sem=sem)
        self.dma("gpsimd", self.ds["Qr_o"][h], qr, waits, sem=sem)
        return (sem, self.P.val(sem))

    def exch_lkv_write(self, lc, tc, tile, waits, slot):
        sem = f"xlk{slot}"
        self.dma("gpsimd", self.ds["Lkv_o"][lc][:, tc * 512:(tc + 1) * 512],
                 tile, waits, sem=sem)
        return (sem, self.P.val(sem))

    def exch_krot_write(self, tc, tile, waits):
        sem = f"xkr{tc % 2}"
        self.dma("gpsimd", self.ds["Rt_o"][:, tc * 512:(tc + 1) * 512],
                 tile, waits, sem=sem)
        return (sem, self.P.val(sem))

    def exch_rstd_write(self, tc, row, waits):
        self.dma("gpsimd", self.ds["Rstd_o"][0:1, tc * 512:(tc + 1) * 512],
                 row, waits, sem="xrs")
        return ("xrs", self.P.val("xrs"))

    def fuse_l0pre(self):
        if not self.fused:
            return
        P = self.P
        w_all = [(s, P.val(s)) for s in ("xlk0", "xlk1", "xkr0", "xkr1", "xrs",
                                         "xqw0", "xqw1") if P.val(s) > 0]
        for name, rows in (("Qp", NH * 128), ("Qr", NH * 64), ("Lkv", 4 * 128)):
            self.dyn_dma(self.ds["S" + name], TOK, 0, [[T, rows], [1, TOK]],
                         self.ds[name + "_o"].rearrange("a b t -> (a b) t"),
                         w_all, "xsc", True)
        self.dyn_dma(self.ds["SRt"], TOK, 0, [[T, 64], [1, TOK]],
                     self.ds["Rt_o"], w_all, "xsc", True)
        self.dyn_dma(self.ds["SRstd"], TOK, 0, [[T, 1], [1, TOK]],
                     self.ds["Rstd_o"], w_all, "xsc", True)
        self.flag_write(EV_L0, [("xsc", P.val("xsc"))])
        wp = self.flag_poll(EV_L0)
        self.dyn_dma(self.ds["SQp"], MYH * 128 * T, 0, [[T, MYH * 128], [1, T]],
                     self.ds["Qp_i"].rearrange("a b t -> (a b) t"), [wp], "xga",
                     False)
        self.dyn_dma(self.ds["SQr"], MYH * 64 * T, 0, [[T, MYH * 64], [1, T]],
                     self.ds["Qr_i"].rearrange("a b t -> (a b) t"), [wp], "xga",
                     False)
        self.w_qgather = ("xga", P.val("xga"))
        self.w_poll_l0 = wp

    def fuse_att0(self):
        if not self.fused:
            return
        P = self.P
        w_all = [(s, P.val(s)) for s in ("xow0", "xow1") if P.val(s) > 0]
        self.dyn_dma(self.ds["So"], MYH * 128 * T, 0, [[T, MYH * 128], [1, T]],
                     self.ds["O_o"].rearrange("a b t -> (a b) t"), w_all, "xsc",
                     True)
        self.flag_write(EV_ATT0, [("xsc", P.val("xsc"))])
        wp = self.flag_poll(EV_ATT0)
        self.dyn_dma(self.ds["So"], TOK, 0, [[T, NH * 128], [1, TOK]],
                     self.ds["O_i"].rearrange("a b t -> (a b) t"), [wp], "xga",
                     False)
        self.w_ogather0 = ("xga", P.val("xga"))

    def fuse_x2(self):
        if not self.fused:
            return
        P = self.P
        w_all = [("xxw", P.val("xxw"))]
        self.dyn_dma(self.ds["Sx"], TOK, 0, [[T, FC * 128], [1, TOK]],
                     self.ds["X2_o"].rearrange("a b t -> (a b) t"), w_all, "xsc",
                     True)
        self.flag_write(EV_X2, [("xsc", P.val("xsc"))])
        self.w_poll_x2 = self.flag_poll(EV_X2)

    def fuse_att1(self):
        if not self.fused:
            return
        P = self.P
        w_all = [(s, P.val(s)) for s in ("xo10", "xo11") if P.val(s) > 0]
        self.dyn_dma(self.ds["So"], MYH * 128 * T, 0, [[T, MYH * 128], [1, T]],
                     self.ds["O1_o"].rearrange("a b t -> (a b) t"), w_all, "xsc",
                     True)
        self.flag_write(EV_ATT1, [("xsc", P.val("xsc"))])
        wp = self.flag_poll(EV_ATT1)
        self.dyn_dma(self.ds["So"], TOK, 0, [[T, NH * 128], [1, TOK]],
                     self.ds["O1_i"].rearrange("a b t -> (a b) t"), [wp], "xga",
                     False)
        self.w_ogather1 = ("xga", P.val("xga"))


    def build(self):
        nc = self.nc
        self.din("meta", [1, 16], I32)
        self.din("flagv", [1, 8], I32)
        self.din("x_fm", [D, TOK])
        for n, sh in (("wqa", [D, QL]), ("wkva", [D, KVL]), ("wkr", [D, DR]),
                      ("wqb", [QL, NH * DQK]), ("wkvb", [KVL, MYH * 256]),
                      ("wo0", [D, D]), ("wg0", [D, HID]), ("wg1", [D, HID]),
                      ("wlin", [HID, D]), ("wq1", [D, MYH * DV]),
                      ("wk1", [D, 256]), ("wv1", [D, 256]), ("wo1", [D, D]),
                      ("w1t", [D, D])):
            self.din(n, sh)
        for n, c in (("qnw", 12), ("kvnw", 4), ("bo1", 16), ("fb1", 16),
                     ("bq1", 8), ("bk1", 2)):
            self.din(n, [128, c])
        self.din("bv1", [1, 256])
        self.din("tabq", [64, TOK])
        self.din("tabk", [64, TOK])
        if self.first != "P1":
            self.din("stream_in", [FC, 128, TOK], BF16)
        if self.last != "P8":
            self.dout("stream_out", [FC, 128, TOK], BF16)
        else:
            self.dout("out", [FC, 128, TOK])
        self.decl_exchange()
        if "P4" in self.ph:
            self.dlocal("w8g0", [D, HID], FP8)
            self.dlocal("w8g1", [D, HID], FP8)
            self.dlocal("w8lin", [HID, D], FP8)
        if "P6" in self.ph:
            self.dlocal("Qs", [MYH, 128, T], BF16)

        with self.es:
            self.psA = self.es.enter_context(self.nc.psum_tensor("psA", [128, 1024], F32))
            self.psB = self.es.enter_context(self.nc.psum_tensor("psB", [128, 1024], F32))
            self.psC = self.es.enter_context(self.nc.psum_tensor("psC", [128, 1024], F32))
            self.psD = self.es.enter_context(self.nc.psum_tensor("psD", [128, 1024], F32))
            self.p0()
            self.prep_weights()
            if "P1" in self.ph:
                self.p1()
                self.fuse_l0pre()
                self.ar.floor = 0
            if "P4" in self.ph:
                self.prep_ffn0_scratch()
            if "P2" in self.ph:
                self.P.barrier()
                self.p2()
                self.fuse_att0()
            if "P3" in self.ph:
                self.P.barrier()
                self.p3()
            if "P4" in self.ph:
                self.P.barrier()
                self.p4()
            if "P5" in self.ph:
                self.p5()
                self.fuse_x2()
            if "P6" in self.ph:
                self.P.barrier()
                self.p6()
                self.fuse_att1()
            if "P7" in self.ph:
                self.P.barrier()
                self.p7()
            if "P8" in self.ph:
                self.P.barrier()
                self.p8()
            if self.last != "P8":
                self.P.barrier()
                self.stream_save()
            self.finish()
            _replay(nc, self.P)
        return nc

    def wkr_load(self):
        self.sb["wkrt"] = self.es.enter_context(
            self.nc.sbuf_tensor("wkrt", [128, FC * DR], BF16))
        self.dma(
            "gpsimd", self.sb["wkrt"][:].rearrange("p (c m) -> p c m", c=FC),
            self.ds["wkr"].rearrange("(c p) m -> p c m", p=128), sem="cG")

    def stream_save(self):
        xs = self.sb["xs"]
        so = self.ds["stream_out"]
        for f in range(FC):
            self.dma("sync", so[f], xs[:, f * TOK:(f + 1) * TOK], sem="sso")

    def finish(self):
        P = self.P
        w = list(P.barrier_waits())
        P.emit("sync", lambda y: y.nop(), w)
        P.emit("gpsimd", lambda g: g.nop(), w)


class Full2(Full):
    def p2(self):
        P, ar = self.P, self.ar
        ar.reset()
        A, Bp, C, Dp = self.psA, self.psB, self.psC, self.psD
        latkvF = [ar.b(128, T) for _ in range(4)]
        krotF = ar.b(64, T)
        Bkv = ar.b(128, T)
        rstdc = ar.f(128, 32)
        rrow = ar.b(1, T)
        kp = ar.b(128, T)
        vtm = ar.b(128, T)
        qpb = [ar.b(128, T) for _ in range(2)]
        qrb = [ar.b(64, T)]
        qrb.append(qrb[0])
        wkvb = [ar.b(128, 4 * 256) for _ in range(2)]
        Eb = [ar.b(128, 1024) for _ in range(4)]
        Aacc = ar.b(128, 512)
        osb = [ar.b(128, 512) for _ in range(2)]
        bsb = [ar.f(128, 512) for _ in range(2)]
        srow = ar.f(1, 512)
        self._prevE = {}
        self._wexp_prev = {}
        wp = [self.w_poll_l0] if self.fused else []
        for lc in range(4):
            self.dma("sync", latkvF[lc], self.ds["Lkv_i"][lc], wp, sem="g2s")
        self.dma("sync", krotF, self.ds["Rt_i"], wp, sem="g2s")
        self.dma("gpsimd", rrow, self.ds["Rstd_i"], wp, sem="g2g")
        wgat = ("g2s", P.val("g2s"))
        wgat2 = ("g2g", P.val("g2g"))
        wBkv = []
        for kc in range(8):
            wb = self.mm(A[:, (kc % 2) * 512:(kc % 2) * 512 + 512],
                         self.sb["onesrow_b"][:], rrow[0:1, kc * 512:(kc + 1) * 512],
                         True, True,
                         [wgat, wgat2, self.w_ones] + ([wBkv[kc - 2]] if kc >= 2 else []),
                         ("MM", 1), skip=True)
            wc = P.emit("vector", lambda v, k=kc,
                        p=A[:, (kc % 2) * 512:(kc % 2) * 512 + 512]:
                        v.tensor_copy(Bkv[:, k * 512:(k + 1) * 512], p), [wb], ("DV", 1))
            wBkv.append(wc)
        wBkv_all = ("DV", P.val("DV"))
        wcp = [None, None]
        for tt in range(32):
            hb2 = tt % 2
            dst = Bp[:, hb2 * 512: hb2 * 512 + 1]
            wb = self.mm(dst, rrow[0:1, tt * 128:(tt + 1) * 128],
                         self.sb["onesrow_b"][0:1, 0:1], True, True,
                         [wgat, wgat2, self.w_ones] + ([wcp[hb2]] if wcp[hb2] else []),
                         ("MM", 1), skip=True)
            wcp[hb2] = P.emit("vector", lambda v, t=tt, p=dst:
                              v.tensor_copy(rstdc[:, t:t + 1], p), [wb], ("DV", 1))
        wrst = ("DV", P.val("DV"))
        wq_prev = [None, None]
        wkv_prev = None
        oev_half = [None, None]
        for h in range(MYH):
            hb = h % 2
            wqg = [self.w_qgather] if self.fused else []
            wqld1 = self.dma("sync", qpb[hb], self.ds["Qp_i"][h],
                             wqg + ([wq_prev[hb]] if wq_prev[hb] else []),
                             sem=f"q2l{hb}")
            wqld2 = self.dma("sync", qrb[hb], self.ds["Qr_i"][h],
                             [("MM", P.val("MM"))], sem=f"q2l{hb}")
            wqld1 = wqld2
            wwl = self.dma("sync", wkvb[hb].rearrange("p (c m) -> p c m", c=4),
                           self.ds["b_wkvb"][:, h * 256:(h + 1) * 256]
                           .rearrange("(c p) m -> p c m", p=128),
                           [self.wcast["wkvb"]] +
                           ([("MM", P.val("MM"))] if h >= 2 else []),
                           sem=f"p2w{hb}")
            wkp = []
            for kc in range(8):
                ps = A[:, (kc % 2) * 512:(kc % 2) * 512 + 512]
                w0 = ([wwl, wBkv_all, ("DV", P.val("DV")), ("AC", P.val("AC"))]
                      if kc < 2 else [wwl, wkp[kc - 2]])
                for lc in range(4):
                    self.mm(ps, wkvb[hb][:, lc * 256: lc * 256 + 128],
                            latkvF[lc][:, kc * 512:(kc + 1) * 512],
                            lc == 0, lc == 3,
                            (w0 + [wgat]) if lc == 0 else (),
                            ("MM", 1) if lc == 3 else None)
                wkp.append(P.emit("vector", lambda v, k=kc, p=ps:
                                  v.tensor_mul(kp[:, k * 512:(k + 1) * 512], p,
                                               Bkv[:, k * 512:(k + 1) * 512]),
                                  [("MM", P.val("MM"))], ("DV", 1)))
            wvt = []
            for g4 in range(8):
                ps = Bp[:, (g4 % 2) * 512:(g4 % 2) * 512 + 512]
                for s in range(4):
                    tt = g4 * 4 + s
                    if s == 0:
                        w0v = ([wrst, ("DV", P.val("DV"))] if g4 < 2
                               else [wvt[4 * (g4 - 2) + 3]])
                    for lc in range(4):
                        self.mm(ps[:, s * 128:(s + 1) * 128],
                                latkvF[lc][:, tt * 128:(tt + 1) * 128],
                                wkvb[hb][:, lc * 256 + 128: lc * 256 + 256],
                                lc == 0, lc == 3,
                                w0v if (s == 0 and lc == 0) else (),
                                ("MM", 1) if lc == 3 else None)
                wmm = ("MM", P.val("MM"))
                for s in range(4):
                    tt = g4 * 4 + s
                    wvt.append(P.emit("vector", lambda v, t=tt, p=ps, s_=s:
                                      v.tensor_scalar(vtm[:, t * 128:(t + 1) * 128],
                                                      p[:, s_ * 128:(s_ + 1) * 128],
                                                      rstdc[:, t:t + 1], None, ALU.mult),
                                      [wmm], ("DV", 1)))
            wkv_done = ("DV", P.val("DV"))
            wq1 = wqld1
            wq2 = wqld2
            for qc in range(8):
                qb = qc // 2
                pq = PERM[qb]
                klist = []
                for kb in range(4):
                    if PERM[kb] < pq:
                        klist += [(kb * 8 + i, None) for i in range(8)]
                for i in range((qc % 2) * 4 + 4):
                    b_ = (qc % 2) * 512 - i * 128
                    klist.append((qb * 8 + i, b_ if b_ < 512 else None))
                nkt = len(klist)
                o_ps = C[:, (qc % 2) * 512:(qc % 2) * 512 + 512]
                wA_last = None
                wexp_prev = getattr(self, "_wexp_prev", {})
                prevE = getattr(self, "_prevE", {})
                groups = list(range(0, nkt, 2))
                pend = []

                def emit_pv(ent, _oev=oev_half, _qc=qc, _ops=o_ps, _nkt=nkt):
                    ki0_, npair_, E_, wE_, eslot_ = ent
                    for sub in range(npair_):
                        kt_ = klist[ki0_ + sub][0]
                        ki_ = ki0_ + sub
                        self.mm(_ops, vtm[:, kt_ * 128:(kt_ + 1) * 128],
                                E_[:, sub * 512:(sub + 1) * 512],
                                ki_ == 0, ki_ == _nkt - 1,
                                [wE_] + ([_oev[_qc % 2]] if (ki_ == 0 and _oev[_qc % 2]) else []),
                                ("MM", 1) if ki_ == _nkt - 1 else None, skip=True)
                    prevE[eslot_] = tuple(prevE.get(eslot_, ())) + (("MM", P.val("MM")),)

                for gi, ki0 in enumerate(groups):
                    if len(pend) >= 2:
                        emit_pv(pend.pop(0))
                    sb_ix = gi % 2
                    eslot = gi % 4
                    S2 = (A if sb_ix == 0 else Bp)
                    wS = ([wq1, wq2, wkv_done] if ki0 == 0 and qc == 0 else []) + \
                         list(wexp_prev.get(sb_ix, ()))
                    npair = 2 if ki0 + 1 < nkt else 1
                    for sub in range(npair):
                        kt = klist[ki0 + sub][0]
                        sl = S2[:, sub * 512:(sub + 1) * 512]
                        self.mm(sl, kp[:, kt * 128:(kt + 1) * 128],
                                qpb[hb][:, qc * 512:(qc + 1) * 512],
                                True, False, wS if sub == 0 else (), None)
                        wS = []
                        self.mm(sl, krotF[:, kt * 128:(kt + 1) * 128],
                                qrb[hb][:, qc * 512:(qc + 1) * 512],
                                False, True, (), ("MM", 1))
                    wsc = ("MM", P.val("MM"))
                    E = Eb[eslot]
                    wexp = P.emit("scalar", lambda s, e=E, p=S2, n=npair:
                                  s.activation(e[:, 0:n * 512], p[:, 0:n * 512],
                                               AF.Exp, bias=self.sb["cst"][:, 2:3]),
                                  [wsc, self.w_ones] + list(prevE.get(eslot, ())),
                                  ("AC", 1))
                    prevE[eslot] = ()
                    wmask = []
                    for sub in range(npair):
                        mb = klist[ki0 + sub][1]
                        if mb is not None:
                            wmask.append(P.emit(
                                "gpsimd", lambda g, e=E, s_=sub, b_=mb:
                                g.affine_select(out=e[:, s_ * 512:(s_ + 1) * 512],
                                                in_=e[:, s_ * 512:(s_ + 1) * 512],
                                                pattern=[[1, 512]],
                                                compare_op=ALU.is_ge, fill=0.0,
                                                base=b_, channel_multiplier=-1),
                                [wexp], ("PO", 1)))
                    wE = wmask[-1] if wmask else wexp
                    for sub in range(npair):
                        if ki0 == 0 and sub == 0:
                            wA_last = P.emit("vector", lambda v, e=E:
                                             v.tensor_copy(Aacc[:], e[:, 0:512]),
                                             [wE, ("MM", P.val("MM"))], ("DV", 1))
                        else:
                            wA_last = P.emit("vector", lambda v, e=E, s_=sub:
                                             v.tensor_add(Aacc[:], Aacc[:],
                                                          e[:, s_ * 512:(s_ + 1) * 512]),
                                             [wE] + ([wA_last] if wA_last else []),
                                             ("DV", 1))
                    prevE[eslot] = tuple(prevE.get(eslot, ())) + \
                        (wA_last, ("PO", P.val("PO")))
                    wexp_prev[sb_ix] = (wexp,)
                    pend.append((ki0, npair, E, wE, eslot))
                while pend:
                    emit_pv(pend.pop(0))
                self._prevE = prevE
                self._wexp_prev = wexp_prev
                wpv = ("MM", P.val("MM"))
                hD = (qc % 2)
                ws = self.mm(Dp[0:1, hD * 512: hD * 512 + 512],
                             self.sb["onescol_b"][:], Aacc[:], True, True,
                             [wA_last, ("DV", P.val("DV"))], ("MM", 1), skip=True)
                w1 = P.emit("vector", lambda v, d=Dp[0:1, hD * 512: hD * 512 + 512]:
                            v.tensor_copy(srow[:], d), [ws], ("DV", 1))
                w2 = P.emit("vector", lambda v: v.reciprocal(srow[:], srow[:]),
                            [w1], ("DV", 1))
                wb = self.mm(Dp[:, (1 - hD) * 512:(1 - hD) * 512 + 512],
                             self.sb["onesrow_f"][:], srow[:], True, True,
                             [w2], ("MM", 1), skip=True)
                w3 = P.emit("vector", lambda v, d=Dp[:, (1 - hD) * 512:(1 - hD) * 512 + 512],
                            b=bsb[hD]: v.tensor_copy(b[:], d), [wb], ("DV", 1))
                ow_prev = getattr(self, "_ow_prev", [None, None])
                w4 = P.emit("vector", lambda v, o=osb[hD], b=bsb[hD], p=o_ps:
                            v.tensor_mul(o[:], p, b[:]),
                            [w3, wpv] + ([ow_prev[hD]] if ow_prev[hD] else []),
                            ("DV", 1))
                oev_half[qc % 2] = w4
                ow_prev[hD] = self.dma(
                    "gpsimd", self.ds["O_o"][h][:, qc * 512:(qc + 1) * 512],
                    osb[hD][:], [w4], sem=f"xow{hD}")
                self._ow_prev = ow_prev
            wq_prev[hb] = ("MM", P.val("MM"))

    def layernorm(self, get_r, put_out, tcs=4):
        P, ar = self.P, self.ar
        C, Dp = self.psC, self.psD
        sqt = [ar.b(128, 512) for _ in range(2)]
        mrow = ar.f(1, 512)
        vrow = ar.f(1, 512)
        trow = ar.f(1, 512)
        wap_prev = None
        sqg = getattr(self, "_ln_sqg", [None, None])
        for tc in range(tcs):
            for oc in range(FC):
                r_ap, wr = get_r(oc, tc)
                self.mm(C[0:1, 0:512], self.sb["onescol_b"][:], r_ap,
                        oc == 0, oc == FC - 1,
                        [wr, self.w_ones] + ([wap_prev] if (oc == 0 and wap_prev) else []),
                        ("MM", 1), skip=True)
                wa = P.emit("scalar", lambda s, q=sqt[oc % 2], r=r_ap:
                            s.activation(q[:], r, AF.Square),
                            [wr] + ([sqg[oc % 2]] if sqg[oc % 2] else []), ("AC", 1))
                sqg[oc % 2] = self.mm(C[0:1, 512:1024], self.sb["onescol_b"][:],
                                      sqt[oc % 2][:], oc == 0, oc == FC - 1, [wa],
                                      ("MM", 1), skip=True)
                self._ln_sqg = sqg
            wst = ("MM", P.val("MM"))
            w1 = P.emit("vector", lambda v: v.tensor_scalar(
                mrow[:], C[0:1, 0:512], 1.0 / D, None, ALU.mult), [wst], ("DV", 1))
            w2 = P.emit("vector", lambda v: v.tensor_scalar(
                vrow[:], C[0:1, 512:1024], 1.0 / D, None, ALU.mult), [wst], ("DV", 1))
            w3 = P.emit("vector", lambda v: v.tensor_mul(trow[:], mrow[:], mrow[:]),
                        [w1], ("DV", 1))
            w4 = P.emit("vector", lambda v: v.tensor_tensor(
                vrow[:], vrow[:], trow[:], ALU.subtract), [w2, w3], ("DV", 1))
            w5 = P.emit("scalar", lambda s: s.activation(
                vrow[:], vrow[:], AF.Sqrt,
                bias=self.sb["cst"][0:1, 1:2]), [w4], ("AC", 1))
            w6 = P.emit("vector", lambda v: v.reciprocal(vrow[:], vrow[:]),
                        [w5], ("DV", 1))
            w7 = P.emit("vector", lambda v: v.tensor_mul(trow[:], mrow[:], vrow[:]),
                        [w6], ("DV", 1))
            w8 = P.emit("vector", lambda v: v.tensor_scalar(
                trow[:], trow[:], -1.0, None, ALU.mult), [w7], ("DV", 1))
            wba = self.mm(Dp[:, 0:512], self.sb["onesrow_f"][:], vrow[:],
                          True, True, [w6], ("MM", 1), skip=True)
            wbb = self.mm(Dp[:, 512:1024], self.sb["onesrow_f"][:], trow[:],
                          True, True, [w8], ("MM", 1), skip=True)
            wlast = None
            for oc in range(FC):
                wlast = put_out(oc, tc, Dp[:, 0:512], Dp[:, 512:1024], [wba, wbb])
            wap_prev = wlast

    def p3(self, wo="wo0", obuf_src="O_i", ev=EV_ATT0, bo=None):
        P, ar = self.P, self.ar
        ar.reset()
        A, Bp = self.psA, self.psB
        xs = self.sb["xs"]
        obuf = [ar.b(128, TOK) for _ in range(NH)]
        wob = [ar.b(128, FC * 128) for _ in range(2)]
        ttile = [ar.b(128, 512) for _ in range(2)]
        wg = []
        if self.fused:
            wg = [self.w_ogather0 if ev == EV_ATT0 else self.w_ogather1]
        for gh in range(NH):
            self.dma("sync", obuf[gh], self.ds[obuf_src][gh], wg, sem="obr")
        wo_all = [("obr", P.val("obr"))]
        rdone = {}
        ev_half = {}
        for tc in range(4):
            for oc in range(FC):
                ps = (A if oc % 2 == 0 else Bp)[:, (tc % 2) * 512:(tc % 2) * 512 + 512]
                slot = (oc % 2, tc % 2)
                wld = self.dma("sync",
                               wob[oc % 2].rearrange("p (c m) -> p c m", c=FC),
                               self.ds["b_" + wo][:, oc * 128:(oc + 1) * 128]
                               .rearrange("(c p) m -> p c m", p=128),
                               [self.wcast[wo]] +
                               ([("MM", P.val("MM"))] if (tc > 0 or oc >= 2) else []),
                               sem=f"p3w{oc % 2}")
                w0 = [wld] + wo_all + ([ev_half[slot]] if ev_half.get(slot) else [])
                for hc in range(NH):
                    self.mm(ps, wob[oc % 2][:, hc * 128:(hc + 1) * 128],
                            obuf[hc][:, tc * 512:(tc + 1) * 512],
                            hc == 0, hc == NH - 1, w0 if hc == 0 else (),
                            ("MM", 1) if hc == NH - 1 else None)
                wmm = ("MM", P.val("MM"))
                xsl = xs[:, oc * TOK + tc * 512: oc * TOK + tc * 512 + 512]
                if bo is None:
                    wv = P.emit("vector", lambda v, p=ps, x=xsl:
                                v.tensor_add(x, p, x), [wmm], ("DV", 1))
                else:
                    wv = P.emit("vector", lambda v, p=ps, x=xsl, oc_=oc:
                                v.scalar_tensor_tensor(
                                    x, p, self.sb[bo][:, oc_:oc_ + 1], x,
                                    ALU.add, ALU.add), [wmm], ("DV", 1))
                ev_half[slot] = wv
                rdone[(oc, tc)] = wv

        def get_r(oc, tc):
            return (xs[:, oc * TOK + tc * 512: oc * TOK + tc * 512 + 512],
                    rdone[(oc, tc)])

        def put_out(oc, tc, Aap, Bap, waits):
            xsl = xs[:, oc * TOK + tc * 512: oc * TOK + tc * 512 + 512]
            w1 = P.emit("vector", lambda v, x=xsl, a=Aap, t=ttile[oc % 2]:
                        v.tensor_mul(t[:], x, a), waits, ("DV", 1))
            return P.emit("vector", lambda v, x=xsl, b=Bap, t=ttile[oc % 2]:
                          v.tensor_add(x, t[:], b), [w1], ("DV", 1))
        self.layernorm(get_r, put_out)


class Full3(Full2):
    def p4(self):
        P, ar = self.P, self.ar
        ar.reset()
        A, Bp = self.psA, self.psB
        xs = self.sb["xs"]
        x8 = [ar.e8(128, 512) for _ in range(FC)]
        a8 = [ar.e8(128, 512) for _ in range(64)]
        wg0b = [ar.e8(128, FC * 256) for _ in range(2)]
        wg1b = [ar.e8(128, FC * 256) for _ in range(2)]
        wlb = [ar.e8(128, 64 * 128) for _ in range(2)]
        gt = [ar.b(128, 512) for _ in range(2)]
        ft = [ar.b(128, 512) for _ in range(2)]
        st = [ar.b(128, 512) for _ in range(2)]
        g3t = [ar.b(128, 512) for _ in range(2)]
        wt = [ar.b(128, 512) for _ in range(2)]
        tt_ = [ar.b(128, 512) for _ in range(2)]
        mt_ = [ar.b(128, 512) for _ in range(2)]
        rdone = {}
        wg_prev = {}
        wsc = [self.w_scr]
        for tt in range(4):
            wx8 = []
            for ic in range(FC):
                wx8.append(P.emit("scalar", lambda s, i=ic, t=tt:
                                  s.activation(x8[i][:], xs[:, i * TOK + t * 512:
                                                            i * TOK + t * 512 + 512],
                                               AF.Copy),
                                  [rdone[(ic, tt - 1)]] if tt > 0 else (), ("AC", 1)))
            wx8a = ("AC", P.val("AC"))
            for hc in range(64):
                hb = hc % 2
                hp = (hc // 2) % 2
                if hc % 2 == 0:
                    self.dma("sync", wg0b[hp].rearrange("p (c m) -> p c m", c=FC),
                             self.ds["w8g0"][:, hc * 128:(hc + 2) * 128]
                             .rearrange("(c p) m -> p c m", p=128),
                             wsc + ([wg_prev[hp]] if hp in wg_prev else []),
                             sem=f"p4g{hp}")
                    wl1 = self.dma("sync", wg1b[hp].rearrange("p (c m) -> p c m", c=FC),
                                   self.ds["w8g1"][:, hc * 128:(hc + 2) * 128]
                                   .rearrange("(c p) m -> p c m", p=128), (),
                                   sem=f"p4g{hp}")
                    self._p4wl = wl1
                sub2 = hc % 2
                psg = A[:, hb * 512: hb * 512 + 512]
                psf = Bp[:, hb * 512: hb * 512 + 512]
                wevg = getattr(self, "_p4evg", {})
                w0 = [self._p4wl, wx8a] + list(wevg.get(hb, ()))
                for ic in range(FC):
                    self.mm(psg, wg0b[hp][:, ic * 256 + sub2 * 128:
                                          ic * 256 + sub2 * 128 + 128], x8[ic][:],
                            ic == 0, ic == FC - 1, w0 if ic == 0 else (),
                            ("MM", 1) if ic == FC - 1 else None, skip=True)
                for ic in range(FC):
                    self.mm(psf, wg1b[hp][:, ic * 256 + sub2 * 128:
                                          ic * 256 + sub2 * 128 + 128], x8[ic][:],
                            ic == 0, ic == FC - 1, (),
                            ("MM", 1) if ic == FC - 1 else None, skip=True)
                if hc % 2 == 1:
                    wg_prev[hp] = ("MM", P.val("MM"))
                wmm = ("MM", P.val("MM"))
                w1 = P.emit("scalar", lambda s, g=gt[hb], p=psg:
                            s.activation(g[:], p, AF.Copy), [wmm], ("AC", 1))
                w2 = P.emit("vector", lambda v, f=ft[hb], p=psf:
                            v.tensor_copy(f[:], p), [wmm], ("DV", 1))
                self._p4evg = {**getattr(self, "_p4evg", {}), hb: (w1, w2)}
                w3 = P.emit("scalar", lambda s, q=st[hb], g=gt[hb]:
                            s.activation(q[:], g[:], AF.Square), [w1], ("AC", 1))
                w4 = P.emit("gpsimd", lambda g_, a=g3t[hb], g=gt[hb], q=st[hb]:
                            g_.tensor_mul(a[:], g[:], q[:]), [w3], ("PO", 1))
                w5 = P.emit("vector", lambda v, w=wt[hb], a=g3t[hb], g=gt[hb]:
                            v.scalar_tensor_tensor(w[:], a[:], GC2, g[:],
                                                   ALU.mult, ALU.add), [w4], ("DV", 1))
                w6 = P.emit("scalar", lambda s, t_=tt_[hb], w=wt[hb]:
                            s.activation(t_[:], w[:], AF.Tanh, scale=GC1),
                            [w5], ("AC", 1))
                w7 = P.emit("gpsimd", lambda g_, m=mt_[hb], g=gt[hb], f=ft[hb]:
                            g_.tensor_mul(m[:], g[:], f[:]), [w2], ("PO", 1))
                w8 = P.emit("vector", lambda v, o=a8[hc], t_=tt_[hb], m=mt_[hb]:
                            v.scalar_tensor_tensor(o[:], t_[:], 1.0, m[:],
                                                   ALU.add, ALU.mult),
                            [w6, w7], ("DV", 1))
                rdone[("a8", hc)] = w8
            wa8 = ("DV", P.val("DV"))
            wdn_prev = {}
            for oc in range(FC):
                hb = oc % 2
                wl = self.dma("sync",
                              wlb[hb].rearrange("p (c m) -> p c m", c=64),
                              self.ds["w8lin"][:, oc * 128:(oc + 1) * 128]
                              .rearrange("(c p) m -> p c m", p=128),
                              wsc + ([wdn_prev[hb]] if hb in wdn_prev else []),
                              sem=f"p4l{hb}")
                ps = (A if hb == 0 else Bp)[:, 0:512]
                w0 = [wl, wa8] + ([rdone[("dn", oc - 2)]] if oc >= 2 else [])
                for hc in range(64):
                    self.mm(ps, wlb[hb][:, hc * 128:(hc + 1) * 128], a8[hc][:],
                            hc == 0, hc == 63, w0 if hc == 0 else (),
                            ("MM", 1) if hc == 63 else None, skip=True)
                wmm = ("MM", P.val("MM"))
                wdn_prev[hb] = wmm
                xsl = xs[:, oc * TOK + tt * 512: oc * TOK + tt * 512 + 512]
                wv = P.emit("vector", lambda v, p=ps, x=xsl:
                            v.scalar_tensor_tensor(x, p, 0.5, x, ALU.mult, ALU.add),
                            [wmm], ("DV", 1))
                rdone[("dn", oc)] = wv
                rdone[(oc, tt)] = wv

        def get_r(oc, tc):
            return (xs[:, oc * TOK + tc * 512: oc * TOK + tc * 512 + 512],
                    rdone[(oc, tc)])

        ttile = [ar.b(128, 512) for _ in range(2)]

        def put_out(oc, tc, Aap, Bap, waits):
            xsl = xs[:, oc * TOK + tc * 512: oc * TOK + tc * 512 + 512]
            w1 = P.emit("vector", lambda v, x=xsl, a=Aap, t=ttile[oc % 2]:
                        v.tensor_mul(t[:], x, a), waits, ("DV", 1))
            return P.emit("vector", lambda v, x=xsl, b=Bap, t=ttile[oc % 2]:
                          v.tensor_add(x, t[:], b), [w1], ("DV", 1))
        self.layernorm(get_r, put_out)

    def p5(self):
        P = self.P
        xs = self.sb["xs"]
        wd = ("DV", P.val("DV"))
        for fcc in range(FC):
            self.dma("gpsimd", self.ds["X2_o"][fcc],
                     xs[:, fcc * TOK:(fcc + 1) * TOK], [wd], sem="xxw")

    def p6(self):
        P, ar = self.P, self.ar
        ar.reset()
        self._prevE = {}
        self._wexp_prev = {}
        A, Bp, C, Dp = self.psA, self.psB, self.psC, self.psD
        wq1h = ar.b(128, FC * 512)
        wk1b = ar.b(128, FC * 256)
        wv1b = ar.b(128, FC * 256)
        xf = [[ar.b(128, 512) for _ in range(8)] for _ in range(2)]
        k1 = [ar.b(128, T) for _ in range(2)]
        v1 = ar.b(128, 32 * 256)
        qsb = [ar.b(128, 512) for _ in range(4)]
        qat = [ar.b(128, T) for _ in range(2)]
        Eb = [ar.b(128, 1024) for _ in range(4)]
        Aacc = ar.b(128, 512)
        osb = [ar.b(128, 512) for _ in range(2)]
        bsb = [ar.f(128, 512) for _ in range(2)]
        srow = ar.f(1, 512)
        wp = [self.w_poll_x2] if self.fused else []
        self.dma("sync", wk1b.rearrange("p (c m) -> p c m", c=FC),
                 self.ds["b_wk1"].rearrange("(c p) m -> p c m", p=128),
                 [self.wcast["wk1"]], sem="p6w")
        self.dma("sync", wv1b.rearrange("p (c m) -> p c m", c=FC),
                 self.ds["b_wv1"].rearrange("(c p) m -> p c m", p=128),
                 [self.wcast["wv1"]], sem="p6w")
        xsrc = self.ds["X2_i"]
        wxf_prev = [None, None]
        for pas in range(3):
            if pas < 2:
                wwq = self.dma(
                    "sync", wq1h.rearrange("p (c m) -> p c m", c=FC),
                    self.ds["b_wq1"][:, pas * 512:(pas + 1) * 512]
                    .rearrange("(c p) m -> p c m", p=128),
                    [self.wcast["wq1"]] +
                    ([("MM", P.val("MM"))] if pas > 0 else []), sem="p6q")
            ww = ("p6w", P.val("p6w"))
            for qc in range(8):
                xb = qc % 2
                pss = [A[:, 0:512], A[:, 512:1024], Bp[:, 0:512], Bp[:, 512:1024]]
                wevq = getattr(self, "_p6evq", None)
                wevk = getattr(self, "_p6evk", None)
                for hf in range(2):
                    wld = []
                    for f8_ in range(8):
                        fcc = hf * 8 + f8_
                        wld.append(self.dma(
                            "sync", xf[xb][f8_][:],
                            xsrc[fcc][:, qc * 512:(qc + 1) * 512],
                            wp + ([wxf_prev[xb]] if wxf_prev[xb] else []),
                            sem=f"p6x{xb}"))
                    wldl = (f"p6x{xb}", P.val(f"p6x{xb}"))
                    if pas < 2:
                        for f8_ in range(8):
                            fcc = hf * 8 + f8_
                            for g in range(4):
                                wxf_prev[xb] = self.mm(
                                    pss[g],
                                    wq1h[:, fcc * 512 + g * 128: fcc * 512 + g * 128 + 128],
                                    xf[xb][f8_][:], fcc == 0, fcc == FC - 1,
                                    ([wldl, ww, wwq] +
                                     (list(wevq) if (wevq and fcc == 0) else []))
                                    if f8_ == 0 else (),
                                    ("MM", 1) if (fcc == FC - 1 or f8_ == 7) else None,
                                    skip=True)
                    else:
                        for f8_ in range(8):
                            fcc = hf * 8 + f8_
                            for kvh in range(2):
                                self.mm(Dp[:, kvh * 512: kvh * 512 + 512],
                                        wk1b[:, fcc * 256 + kvh * 128: fcc * 256 + kvh * 128 + 128],
                                        xf[xb][f8_][:], fcc == 0, fcc == FC - 1,
                                        ([wldl, ww] + (list(wevk) if (wevk and fcc == 0) else []))
                                        if f8_ == 0 else (),
                                        ("MM", 1) if fcc == FC - 1 else None, skip=True)
                            for s in range(4):
                                wxf_prev[xb] = self.mm(
                                    C[:, s * 256:(s + 1) * 256],
                                    xf[xb][f8_][:, s * 128:(s + 1) * 128],
                                    wv1b[:, fcc * 256:(fcc + 1) * 256],
                                    fcc == 0, False, (),
                                    ("MM", 1) if (f8_ == 7 and s == 3) else None,
                                    skip=True)
                if pas < 2:
                    wmm = ("MM", P.val("MM"))
                    wevq = []
                    stq = getattr(self, "_p6stq", [None] * 4)
                    for g in range(4):
                        h = pas * 4 + g
                        wq_ = P.emit("vector", lambda v, g_=g, p=pss[g], h_=h:
                                     v.tensor_scalar(qsb[g_][:], p,
                                                     self.sb["bq1_sb"][:, h_:h_ + 1],
                                                     SC1, ALU.add, ALU.mult),
                                     [wmm] + ([stq[g]] if stq[g] else []), ("DV", 1))
                        wevq.append(wq_)
                        stq[g] = self.dma("sync",
                                          self.ds["Qs"][h][:, qc * 512:(qc + 1) * 512],
                                          qsb[g][:], [wq_], sem=f"qsw{g}")
                    self._p6stq = stq
                    self._p6evq = tuple(wevq)
                else:
                    for s in range(4):
                        self.mm(C[:, s * 256:(s + 1) * 256],
                                self.sb["onesrow_b"][:], self.sb["bv1_sb"][:],
                                False, True, [self.w_const[0]],
                                ("MM", 1) if s == 3 else None, skip=True)
                    wmm = ("MM", P.val("MM"))
                    wevk = []
                    for kvh in range(2):
                        wk_ = P.emit("vector", lambda v, k=kvh, p=Dp[:, kvh * 512: kvh * 512 + 512], q=qc:
                                     v.tensor_scalar(k1[k][:, q * 512:(q + 1) * 512], p,
                                                     self.sb["bk1_sb"][:, k:k + 1],
                                                     None, ALU.add),
                                     [wmm], ("DV", 1))
                        wevk.append(wk_)
                    for s in range(4):
                        tok = qc * 4 + s
                        wv_ = P.emit("vector", lambda v, t=tok, s_=s:
                                     v.tensor_copy(v1[:, t * 256:(t + 1) * 256],
                                                   C[:, s_ * 256:(s_ + 1) * 256]),
                                     [wmm], ("DV", 1))
                        wevk.append(wv_)
                    self._p6evk = tuple(wevk)
                wproj = ("DV", P.val("DV"))
        wq_prev = [None, None]
        oev_half = [None, None]
        for kvh in range(2):
            for g in range(4):
                h = kvh * 4 + g
                hb = h % 2
                wqld = self.dma("sync", qat[hb], self.ds["Qs"][h],
                                [(f"qsw{g_}", P.val(f"qsw{g_}")) for g_ in range(4)] +
                                ([wq_prev[hb]] if wq_prev[hb] else []),
                                sem=f"qal{hb}")
                for qc in range(8):
                    o_ps = C[:, (qc % 2) * 512:(qc % 2) * 512 + 512]
                    wA_last = None
                    prevE = self._prevE
                    wexp_prev = self._wexp_prev
                    pend = []

                    def emit_pv(ent, _oev=oev_half, _qc=qc, _ops=o_ps, _kvh=kvh):
                        kt0_, E_, wE_, eslot_ = ent
                        for sub in range(2):
                            kt_ = kt0_ + sub
                            self.mm(_ops, v1[:, kt_ * 256 + _kvh * 128: kt_ * 256 + _kvh * 128 + 128],
                                    E_[:, sub * 512:(sub + 1) * 512],
                                    kt_ == 0, kt_ == 31,
                                    [wE_] + ([_oev[_qc % 2]] if (kt_ == 0 and _oev[_qc % 2]) else []),
                                    ("MM", 1) if kt_ == 31 else None, skip=True)
                        prevE[eslot_] = tuple(prevE.get(eslot_, ())) + \
                            (("MM", P.val("MM")),)

                    for gi, kt2 in enumerate(range(0, 32, 2)):
                        if len(pend) >= 2:
                            emit_pv(pend.pop(0))
                        sb_ix = gi % 2
                        eslot = gi % 4
                        S2 = (A if sb_ix == 0 else Bp)
                        wS = ([wqld, wproj] if kt2 == 0 and qc == 0 and h == 0 else
                              ([wqld] if kt2 == 0 and qc == 0 else [])) + \
                             list(wexp_prev.get(sb_ix, ()))
                        for sub in range(2):
                            kt = kt2 + sub
                            self.mm(S2[:, sub * 512:(sub + 1) * 512],
                                    k1[kvh][:, kt * 128:(kt + 1) * 128],
                                    qat[hb][:, qc * 512:(qc + 1) * 512],
                                    True, True, wS if sub == 0 else (),
                                    ("MM", 1), skip=True)
                            wS = []
                        wsc = ("MM", P.val("MM"))
                        E = Eb[eslot]
                        wexp = P.emit("scalar", lambda s, e=E, p=S2:
                                      s.activation(e[:], p[:], AF.Exp,
                                                   bias=self.sb["cst"][:, 2:3]),
                                      [wsc, self.w_ones] + list(prevE.get(eslot, ())),
                                      ("AC", 1))
                        prevE[eslot] = ()
                        for sub in range(2):
                            if kt2 == 0 and sub == 0:
                                wA_last = P.emit("vector", lambda v, e=E:
                                                 v.tensor_copy(Aacc[:], e[:, 0:512]),
                                                 [wexp, ("MM", P.val("MM"))], ("DV", 1))
                            else:
                                wA_last = P.emit("vector", lambda v, e=E, s_=sub:
                                                 v.tensor_add(Aacc[:], Aacc[:],
                                                              e[:, s_ * 512:(s_ + 1) * 512]),
                                                 [wexp] + ([wA_last] if wA_last else []),
                                                 ("DV", 1))
                        prevE[eslot] = tuple(prevE.get(eslot, ())) + \
                            (wA_last, ("PO", P.val("PO")))
                        wexp_prev[sb_ix] = (wexp,)
                        pend.append((kt2, E, wexp, eslot))
                    while pend:
                        emit_pv(pend.pop(0))
                    wpv = ("MM", P.val("MM"))
                    hD = qc % 2
                    ws = self.mm(Dp[0:1, hD * 512: hD * 512 + 512],
                                 self.sb["onescol_b"][:], Aacc[:], True, True,
                                 [wA_last, ("DV", P.val("DV"))], ("MM", 1), skip=True)
                    w1 = P.emit("vector", lambda v, d=Dp[0:1, hD * 512: hD * 512 + 512]:
                                v.tensor_copy(srow[:], d), [ws], ("DV", 1))
                    w2 = P.emit("vector", lambda v: v.reciprocal(srow[:], srow[:]),
                                [w1], ("DV", 1))
                    wb = self.mm(Dp[:, (1 - hD) * 512:(1 - hD) * 512 + 512],
                                 self.sb["onesrow_f"][:], srow[:], True, True,
                                 [w2], ("MM", 1), skip=True)
                    w3 = P.emit("vector", lambda v, d=Dp[:, (1 - hD) * 512:(1 - hD) * 512 + 512],
                                b=bsb[hD]: v.tensor_copy(b[:], d), [wb], ("DV", 1))
                    ow_prev = getattr(self, "_ow1_prev", [None, None])
                    w4 = P.emit("vector", lambda v, o=osb[hD], b=bsb[hD], p=o_ps:
                                v.tensor_mul(o[:], p, b[:]),
                                [w3, wpv] + ([ow_prev[hD]] if ow_prev[hD] else []),
                                ("DV", 1))
                    oev_half[qc % 2] = w4
                    ow_prev[hD] = self.dma(
                        "gpsimd", self.ds["O1_o"][h][:, qc * 512:(qc + 1) * 512],
                        osb[hD][:], [w4], sem=f"xo1{hD}")
                    self._ow1_prev = ow_prev
                wq_prev[hb] = ("MM", P.val("MM"))

    def p7(self):
        self.p3(wo="wo1", obuf_src="O1_i", ev=EV_ATT1, bo="bo1_sb")

    def p8(self):
        P, ar = self.P, self.ar
        ar.reset()
        A, Bp = self.psA, self.psB
        xs = self.sb["xs"]
        w1b = [ar.b(128, FC * 128) for _ in range(2)]
        rbuf = [ar.b(128, 512) for _ in range(FC)]
        rdone = {}
        ev_half = {}
        for tc in range(4):
            for oc in range(FC):
                hb = oc % 2
                slot = (hb, tc % 2)
                wld = self.dma("sync",
                               w1b[hb].rearrange("p (c m) -> p c m", c=FC),
                               self.ds["b_w1t"][:, oc * 128:(oc + 1) * 128]
                               .rearrange("(c p) m -> p c m", p=128),
                               [self.wcast["w1t"]] +
                               ([("MM", P.val("MM"))] if (tc > 0 or oc >= 2) else []),
                               sem=f"p8w{hb}")
                ps = (A if hb == 0 else Bp)[:, (tc % 2) * 512:(tc % 2) * 512 + 512]
                w0 = [wld] + ([ev_half[slot]] if ev_half.get(slot) else [])
                for ic in range(FC):
                    self.mm(ps, w1b[hb][:, ic * 128:(ic + 1) * 128],
                            xs[:, ic * TOK + tc * 512: ic * TOK + tc * 512 + 512],
                            ic == 0, ic == FC - 1, w0 if ic == 0 else (),
                            ("MM", 1) if ic == FC - 1 else None)
                wmm = ("MM", P.val("MM"))
                wv = P.emit("vector", lambda v, p=ps, oc_=oc,
                            x=xs[:, oc * TOK + tc * 512: oc * TOK + tc * 512 + 512],
                            r=rbuf[oc]:
                            v.scalar_tensor_tensor(r[:], p,
                                                   self.sb["fb1_sb"][:, oc_:oc_ + 1],
                                                   x, ALU.add, ALU.add),
                            [wmm], ("DV", 1))
                ev_half[slot] = wv
                rdone[(oc, tc)] = wv

            def get_r(oc, tc2, _tc=tc):
                return (rbuf[oc][:], rdone[(oc, _tc)])

            obf = [ar.f(128, 512) for _ in range(2)] if tc == 0 else self._p8obf
            self._p8obf = obf

            def put_out(oc, tc2, Aap, Bap, waits, _tc=tc):
                t = obf[oc % 2]
                wprev = getattr(self, "_p8ost", {}).get(oc % 2)
                w1 = P.emit("vector", lambda v, r=rbuf[oc], a=Aap, t_=t:
                            v.tensor_mul(t_[:], r[:], a), list(waits) +
                            ([wprev] if wprev else []), ("DV", 1))
                w2 = P.emit("vector", lambda v, b=Bap, t_=t:
                            v.tensor_add(t_[:], t_[:], b), [w1], ("DV", 1))
                wst = self.dma("sync", self.ds["out"][oc][:, _tc * 512:(_tc + 1) * 512],
                               t[:], [w2], sem=f"out{oc % 2}")
                self._p8ost = {**getattr(self, "_p8ost", {}), oc % 2: wst}
                return w2
            self.layernorm(get_r, put_out, tcs=1)



ALLPH = ["P1", "P2", "P3", "P4", "P5", "P6", "P7", "P8"]
LAUNCHES = [["P1"], ["P2"], ["P3", "P4", "P5"], ["P6"], ["P7", "P8"]]
FUSED = False


def _run(nc, in_maps):
    return run_bass_kernel_spmd(nc, in_maps, core_ids=list(range(NCORE)))


def _pick(maps, names):
    return [{k: m[k] for k in names} for m in maps]


def _assemble_full(per_core, heads_dim, rows, posmap=None):
    out = []
    for b in range(4):
        out.append(np.concatenate([per_core[2 * b], per_core[2 * b + 1]],
                                  axis=-1))
    return out


def kernel(**inputs):
    maps = host_inputs(inputs)
    posmap = [(STRIPE[j][0] * 1024, STRIPE[j][1] * 1024) for j in range(2)]
    x = np.asarray(inputs["L0_attn_hidden_states"], np.float32)

    if FUSED:
        bld = Full3(ALLPH, True)
        nc = bld.build()
        res = _run(nc, [{k: m[k] for k in bld.innames} for m in maps])
        outs = [r["out"] for r in res.results]
    else:
        needed = None
        state = [dict(m) for m in maps]
        for li, ph in enumerate(LAUNCHES):
            bld = Full3(ph, False)
            nc = bld.build()
            inm = [{k: state[c][k] for k in bld.innames} for c in range(NCORE)]
            res = _run(nc, inm)
            for c in range(NCORE):
                for k, v in res.results[c].items():
                    state[c][k] = v
            if ph[-1] == "P1":
                qp = _assemble_full([state[c]["Qp_o"] for c in range(NCORE)],
                                    NH, 128, posmap)
                qr = _assemble_full([state[c]["Qr_o"] for c in range(NCORE)],
                                    NH, 64, posmap)
                lkv = _assemble_full([state[c]["Lkv_o"] for c in range(NCORE)],
                                     4, 128, posmap)
                rt = _assemble_full([state[c]["Rt_o"][None] for c in range(NCORE)],
                                    1, 64, posmap)
                rs = _assemble_full([state[c]["Rstd_o"][None] for c in range(NCORE)],
                                    1, 1, posmap)
                for c in range(NCORE):
                    b, j = divmod(c, 2)
                    hsl = slice(j * MYH, (j + 1) * MYH)
                    state[c]["Qp_i"] = np.ascontiguousarray(qp[b][hsl])
                    state[c]["Qr_i"] = np.ascontiguousarray(qr[b][hsl])
                    state[c]["Lkv_i"] = lkv[b]
                    state[c]["Rt_i"] = rt[b][0]
                    state[c]["Rstd_i"] = rs[b][0]
            if ph[-1] == "P2":
                for b in range(4):
                    ob = np.concatenate([state[2 * b]["O_o"],
                                         state[2 * b + 1]["O_o"]], axis=0)
                    for j in range(2):
                        state[2 * b + j]["O_i"] = np.ascontiguousarray(
                            ob[:, :, j * TOK:(j + 1) * TOK])
            if ph[-1] == "P5":
                x2 = _assemble_full([state[c]["X2_o"] for c in range(NCORE)],
                                    FC, 128, posmap)
                for c in range(NCORE):
                    state[c]["X2_i"] = x2[c // 2]
            if ph[-1] == "P6":
                for b in range(4):
                    ob = np.concatenate([state[2 * b]["O1_o"],
                                         state[2 * b + 1]["O1_o"]], axis=0)
                    for j in range(2):
                        state[2 * b + j]["O1_i"] = np.ascontiguousarray(
                            ob[:, :, j * TOK:(j + 1) * TOK])
            if "stream_out" in (res.results[0] or {}):
                for c in range(NCORE):
                    state[c]["stream_in"] = state[c]["stream_out"]
        outs = [state[c]["out"] for c in range(NCORE)]

    y = np.empty((Bz, T, D), np.float32)
    for c in range(NCORE):
        b, j = divmod(c, 2)
        pA, pB = posmap[j]
        fm = outs[c].reshape(D, TOK)
        y[b, pA:pA + 1024] = fm[:, 0:1024].T
        y[b, pB:pB + 1024] = fm[:, 1024:2048].T
    return y
